# version 21
# baseline (speedup 1.0000x reference)
import numpy as np
import ml_dtypes
import concourse.bass as bass
import concourse.mybir as mybir
from concourse.bass_utils import run_bass_kernel_spmd
from concourse.aot_env import allow_ici_all_to_all

F32 = mybir.dt.float32
BF16 = mybir.dt.bfloat16
AF = mybir.ActivationFunctionType
ALU = mybir.AluOpType
AX = mybir.AxisListType
BF = ml_dtypes.bfloat16
f32 = np.float32

NCORES = 8
B = 8
TLOC = 256
T = 2048
D = 1024
NT = 16
NDC = 8
HD = 64
NH = 16
NE0 = 16
NE1 = 8
EPS = 1e-5
EXPB = f32(-14.0)
SCALE = f32(0.125)

N_STAGES = 4


class Slot:
    __slots__ = ("w", "we", "r", "name")

    def __init__(self, name=""):
        self.w = {}
        self.we = {}
        self.r = {}
        self.name = name


def _merge(dst, src):
    for s, v in src.items():
        if v > dst.get(s, 0):
            dst[s] = v


class Prog:
    CENG = {"tensor": "tensor", "vector": "vector", "scalar": "scalar",
            "gpsimd": "gpsimd"}

    def __init__(self, nc):
        self.nc = nc
        self.ops = {e: [] for e in ("tensor", "vector", "scalar", "gpsimd", "sync")}
        self.sem = {}
        self.cnt = {}
        for s in ("tensor", "vector", "scalar", "gpsimd", "sync_q", "gp_q",
                  "cc", "w3c", "w1c", "ewc"):
            self.sem[s] = nc.alloc_semaphore("S_" + s)
            self.cnt[s] = 0
        self.hwm = {e: {} for e in self.ops}

    def emit(self, eng, fn, reads=(), writes=(), sem=None, inc=1, partial=False,
             extra=()):
        if sem is None:
            sem = self.CENG[eng]
        rdeps = {}
        for sl in reads:
            _merge(rdeps, sl.w)
        wdeps = {}
        ownsem = self.CENG.get(eng) if eng != "sync" else sem
        for sl in writes:
            _merge(wdeps, sl.r)
            if partial:
                _merge(wdeps, sl.we)
                if ownsem in sl.w and sl.w[ownsem] > wdeps.get(ownsem, 0):
                    wdeps[ownsem] = sl.w[ownsem]
            else:
                _merge(wdeps, sl.w)
        for s, v in extra:
            if v > wdeps.get(s, 0):
                wdeps[s] = v
        own = self.CENG.get(eng)
        deps = {}
        _merge(deps, wdeps)
        _merge(deps, rdeps)
        waits = []
        hw = self.hwm[eng]
        for s, v in sorted(deps.items()):
            if v <= 0:
                continue
            if s == own and eng == "tensor":
                continue
            if hw.get(s, 0) >= v:
                continue
            waits.append((self.sem[s], v))
            hw[s] = v
        self.cnt[sem] += inc
        val = self.cnt[sem]
        semh = self.sem[sem]

        def run(e, fn=fn, waits=tuple(waits), semh=semh, inc=inc):
            for h, v in waits:
                e.wait_ge(h, v)
            fn(e).then_inc(semh, inc)
        self.ops[eng].append(run)
        for sl in reads:
            sl.r[sem] = max(sl.r.get(sem, 0), val)
        for sl in writes:
            if partial:
                sl.w[sem] = max(sl.w.get(sem, 0), val)
            else:
                sl.w = {sem: val}
                sl.we = {sem: val}
                sl.r = {}
        return (sem, val)

    def barrier_wait(self, eng, deps):
        own = self.CENG.get(eng)
        waits = []
        hw = self.hwm[eng]
        for s, v in sorted(deps.items()):
            if v <= 0 or s == own or hw.get(s, 0) >= v:
                continue
            waits.append((self.sem[s], v))
            hw[s] = v
        if waits:
            def run(e, waits=tuple(waits)):
                for h, v in waits:
                    e.wait_ge(h, v)
            self.ops[eng].append(run)


class Region:

    def __init__(self, handle_ap):
        self.h = handle_ap
        self.live = []

    def view(self, off_bytes, p0, pn, nel, dtype, name=""):
        esz = 2 if dtype == BF16 else 4
        lo, hi = off_bytes, off_bytes + nel * esz
        assert hi <= 65536, (name, lo, hi)
        sl = Slot(name)
        for (a, b_, old) in self.live:
            if a < hi and lo < b_:
                _merge(sl.w, old.w)
                _merge(sl.we, old.we)
                _merge(sl.r, old.r)
                _merge(sl.w, old.r)
                _merge(sl.we, old.r)
        self.live.append((lo, hi, sl))
        ap = self.h[p0:p0 + pn, lo // 2: hi // 2]
        if dtype != BF16:
            ap = ap.bitcast(dtype)
        return ap, sl


def build_program(n_stages=N_STAGES):
    nc = bass.Bass("TRN2", target_bir_lowering=False)
    p = Prog(nc)

    def din(name, shape):
        return nc.dram_tensor(name, list(shape), F32, kind="ExternalInput").ap()

    X_in = din("x", (B, TLOC, D))
    W0qT = din("w0qT", (D, D))
    W0kkT = din("w0kkT", (D, 2 * HD))
    W0vT = din("w0vT", (D, HD))
    W0pT = din("w0pT", (D, D))
    B0p = din("b0p", (1, D))
    SIM = din("sim", (D, NE0))
    GATES = din("gates", (1, NE0))
    WQKV0 = din("wqkv0", (D, NE0 * 3 * HD))
    WO0 = din("wo0", (NE0 * HD, D))
    W1qT = din("w1qT", (D, D))
    W1kT = din("w1kT", (D, D))
    W1vT = din("w1vT", (D, D))
    B1q = din("b1q", (1, D))
    B1k = din("b1k", (1, D))
    B1v = din("b1v", (1, D))
    W1oT = din("w1oT", (D, D))
    B1o = din("b1o", (1, D))
    GW = din("gw", (D, NE1))
    GB = din("gb", (1, NE1))
    EW = din("ew", (NE1, D, D))
    EB = din("eb", (NE1, D))
    if n_stages >= 4:
        X_out = nc.dram_tensor("out", [D, TLOC * B], F32,
                               kind="ExternalOutput").ap()
    else:
        X_out = nc.dram_tensor("out", [B, TLOC, D], F32,
                               kind="ExternalOutput").ap()

    a2aI_t = [nc.dram_tensor(f"a2a{h}_in", [NCORES, 3 * 128 * HD], BF16)
              for h in range(2)]
    a2aO_t = [nc.dram_tensor(f"a2a{h}_out", [NCORES, 3 * 128 * HD], BF16)
              for h in range(2)]
    c2aI_t = [nc.dram_tensor(f"c2a{h}_in", [NCORES, 65 * 128], BF16)
              for h in range(2)]
    c2aO_t = [nc.dram_tensor(f"c2a{h}_out", [NCORES, 65 * 128], BF16)
              for h in range(2)]
    ewst_t = nc.dram_tensor("ewst", [NE1 * NDC, 128 * D], BF16)
    w1st_t = nc.dram_tensor("w1st", [4 * NDC, 128 * D], BF16)
    w3st_t = nc.dram_tensor("w3st", [NDC, 128 * 3 * D], BF16)
    sl_a2aih = [Slot(f"a2ai{h}") for h in range(2)]
    sl_a2aoh = [Slot(f"a2ao{h}") for h in range(2)]
    sl_c2aih = [Slot(f"c2ai{h}") for h in range(2)]
    sl_c2aoh = [Slot(f"c2ao{h}") for h in range(2)]
    sl_ewst = Slot("ewst")
    sl_w1st = Slot("w1st")
    sl_w3st = Slot("w3st")

    ident_bf = nc.inline_tensor(np.eye(128, dtype=BF), "ident_bf")
    ident_f = nc.inline_tensor(np.eye(128, dtype=np.float32), "ident_f")
    ones_row = nc.inline_tensor(np.ones((1, 512), dtype=BF), "ones_row")
    ones_col = nc.inline_tensor(np.ones((128, 1), dtype=BF), "ones_col")
    selp_np = np.zeros((NE0, 8 * 128), dtype=BF)
    for j in range(8):
        selp_np[2 * j, j * 128: j * 128 + 64] = 1.0
        selp_np[2 * j + 1, j * 128 + 64: j * 128 + 128] = 1.0
    SELPD = nc.inline_tensor(selp_np, "selp")
    sel8_np = np.zeros((NE1, 8 * 128), dtype=BF)
    for e in range(NE1):
        sel8_np[e, e * 128:(e + 1) * 128] = 1.0
    SEL8D = nc.inline_tensor(sel8_np, "sel8")
    bd = np.zeros((128, 128), dtype=BF)
    for g in range(16):
        bd[g * 8:(g + 1) * 8, g * 8:(g + 1) * 8] = 1.0
    BDMASKD = nc.inline_tensor(np.tile(bd, (1, 4)), "bdmask4")
    TRILD = nc.inline_tensor(np.triu(np.ones((128, 128), dtype=np.float32))
                             .astype(BF), "triu128")

    R1h = nc.alloc_sbuf_tensor("R1", [128, 32768], BF16)
    R2h = nc.alloc_sbuf_tensor("R2", [128, 32768], BF16)
    HTRh = nc.alloc_sbuf_tensor("HTR", [128, 16384], BF16)
    HFh = nc.alloc_sbuf_tensor("HFb", [128, 16384], BF16)
    STATh = nc.alloc_sbuf_tensor("STAT", [128, 96], F32)
    JNKVh = nc.alloc_sbuf_tensor("JNKV", [128, 1024], BF16)
    JNKSh = nc.alloc_sbuf_tensor("JNKS", [128, 1024], BF16)
    GSCRh = nc.alloc_sbuf_tensor("GSCR", [128, 160], F32)
    RSh = nc.alloc_sbuf_tensor("RSc", [128, 64], F32)
    RWh = nc.alloc_sbuf_tensor("RWg", [128, NT * NE0], F32)
    WTMh = nc.alloc_sbuf_tensor("WTM", [128, NT * NE1], F32)
    QKVAh = nc.alloc_sbuf_tensor("QKVA", [128, 3 * HD], F32)
    CSTh = nc.alloc_sbuf_tensor("CSTc", [128, 1824], BF16)
    CSTFh = nc.alloc_sbuf_tensor("CSTF", [128, 128], F32)
    BIAh = nc.alloc_sbuf_tensor("BIAc", [1, 1664], BF16)
    SIMBh = nc.alloc_sbuf_tensor("SIMB", [128, NDC * NE0], BF16)
    GWSh = nc.alloc_sbuf_tensor("GWS", [128, NDC * NE1], BF16)
    STT2h = nc.alloc_sbuf_tensor("STT2", [128, 64], F32)

    R1 = Region(R1h[:])
    R2 = Region(R2h[:])
    HTR = Region(HTRh[:])
    HF = HFh[:]
    sl_HF = Slot("HF")
    sl_HFg = [[Slot(f"HF{j}_{i}") for i in range(NT)] for j in range(NDC)]

    def hf_r(j, i0, i1):
        return sl_HFg[j][i0:i1]
    STAT = STATh[:]
    sl_STAT = Slot("STAT")
    JNKV = JNKVh[:]
    sl_JNKV = Slot("JNKV")
    JNKS = JNKSh[:]
    sl_JNKS = Slot("JNKS")
    GSCR = GSCRh[:]
    sl_GSCR = Slot("GSCR")
    RS = RSh[:]
    sl_RS = Slot("RS")
    RW = RWh[:]
    sl_RW = Slot("RW")
    WTM = WTMh[:]
    sl_WTM = Slot("WTM")
    QKVA = QKVAh[:]
    sl_QKVA = Slot("QKVA")
    CST = CSTh[:]
    sl_CST = Slot("CST")
    CSTF = CSTFh[:]
    sl_CSTF = Slot("CSTF")
    BIA = BIAh[:]
    sl_BIA = Slot("BIA")
    SIMB = SIMBh[:]
    sl_SIMB = Slot("SIMB")
    GWS = GWSh[:]
    sl_GWS = Slot("GWS")
    STT2 = STT2h[:]
    sl_STT2 = Slot("STT2")

    PSh = nc.alloc_psum_tensor("PS", [128, 4096], F32)
    PS = PSh[:]
    sl_PB = [Slot(f"pb{i}") for i in range(8)]

    def bank(i, off=0, w=512):
        return PS[:, i * 512 + off:i * 512 + off + w]

    def bank_bf(i, off_bf, w_bf):
        return PS[:, i * 512 + off_bf // 2:i * 512 + (off_bf + w_bf) // 2].bitcast(BF16)

    IDENT = CST[:, 0:128]
    SELP = CSTh[0:16, 128:1152]
    SEL8 = CSTh[0:8, 128:1152]
    BDMASK = CST[:, 1152:1664]
    ONE_COL = CST[:, 1664:1665]
    TRIL = CST[:, 1696:1824]
    ONE_ROW512 = BIAh[0:1, 1152:1664]
    ONE_ROW128 = BIAh[0:1, 1152:1280]

    def dma(queue, out_ap, in_ap, reads=(), writes=(), partial=False):
        eng, sem = {"sync": ("sync", "sync_q"), "gpsimd": ("gpsimd", "gp_q")}[queue]
        prev = p.cnt[sem]
        return p.emit(eng, (lambda e, o=out_ap, i=in_ap: e.dma_start(out=o, in_=i)),
                      reads=reads, writes=writes, sem=sem, inc=16, partial=partial,
                      extra=[(sem, prev)])

    def dma_cast(out_ap, in_ap, sem, reads=(), writes=(), partial=True):
        thr = max(0, p.cnt[sem] - 7 * 16)
        return p.emit("gpsimd",
                      (lambda e, o=out_ap, i=in_ap: e.dma_start(out=o, in_=i)),
                      reads=reads, writes=writes, sem=sem, inc=16,
                      partial=partial, extra=[(sem, thr)])

    def mm(out_ap, lhsT, rhs, start, stop, reads, wbank):
        return p.emit("tensor",
                      (lambda e, o=out_ap, l=lhsT, r=rhs, st=start, sp=stop:
                       e.matmul(o, l, r, start=st, stop=sp)),
                      reads=reads, writes=[wbank], partial=not start)

    def tr(out_ap, in_ap, ident_ap, reads, wbank, ident_slot=None):
        rd = list(reads) + ([ident_slot] if ident_slot else [sl_CST])
        return p.emit("tensor",
                      (lambda e, o=out_ap, i=in_ap, d=ident_ap: e.transpose(o, i, d)),
                      reads=rd, writes=[wbank])

    def vcopy(out_ap, in_ap, reads, writes, partial=True):
        return p.emit("vector", (lambda e, o=out_ap, i=in_ap: e.tensor_copy(o, i)),
                      reads=reads, writes=writes, partial=partial)

    def scopy(out_ap, in_ap, reads, writes, partial=True):
        return p.emit("scalar", (lambda e, o=out_ap, i=in_ap: e.copy(o, i)),
                      reads=reads, writes=writes, partial=partial)

    XAv, sl_XA = [], []
    for i in range(NT):
        a, s = R1.view(i * 4096, 0, 128, D, F32, f"XA{i}")
        XAv.append(a)
        sl_XA.append(s)

    p.emit("vector", (lambda e: e.memset(STAT[:, 0:4], 1.0)),
           reads=(), writes=[sl_STAT], partial=True)
    p.emit("vector", (lambda e: e.reciprocal(STAT[:, 4:8], STAT[:, 0:4])),
           reads=[sl_STAT], writes=[sl_STAT], partial=True)
    p.emit("vector", (lambda e: e.reciprocal(STAT[:, 4:8], STAT[:, 0:4])),
           reads=[sl_STAT], writes=[sl_STAT], partial=True)
    p.emit("vector", (lambda e: e.tensor_scalar(
        JNKV[:, 0:4], STAT[:, 0:4],
        1.0, 0.0, op0=ALU.mult, op1=ALU.add, accum_out=STAT[:, 8:9])),
        reads=[sl_STAT], writes=[sl_STAT, sl_JNKV], partial=True)
    p.emit("scalar", (lambda e: e.activation(
        JNKS[:, 0:4], STAT[:, 0:4], AF.Exp, bias=f32(-2.0), scale=f32(0.1),
        accum_out=STAT[:, 9:10])),
        reads=[sl_STAT], writes=[sl_STAT, sl_JNKS], partial=True)
    p.emit("scalar", (lambda e: e.activation(
        JNKS[:, 0:4], STAT[:, 0:4], AF.Ln, bias=f32(2.0), scale=f32(0.1))),
        reads=[sl_STAT], writes=[sl_JNKS], partial=True)

    def emit_ln_stats(Xv, sl_Xv, i):
        p.emit("vector", (lambda e, i=i: e.tensor_scalar(
            JNKV[:], Xv[i], 1.0, 0.0, op0=ALU.mult, op1=ALU.add,
            accum_out=STAT[:, i:i + 1])),
            reads=[sl_Xv[i]], writes=[sl_STAT, sl_JNKV], partial=True)
        p.emit("scalar", (lambda e, i=i: e.activation(
            JNKS[:], Xv[i], AF.Square, bias=f32(0.0), scale=f32(1.0),
            accum_out=STAT[:, 16 + i:17 + i])),
            reads=[sl_Xv[i]], writes=[sl_STAT, sl_JNKS], partial=True)

    def emit_ln(Xv, sl_Xv, tag, stats_emitted=False):
        HT, _slHT = HTR.view(0, 0, 128, NT * D, BF16, "HT_" + tag)
        sl_HTt = []
        for i in range(NT):
            _, s_ = HTR.view(i * 2048, 0, 128, D, BF16, f"HTt_{tag}_{i}")
            sl_HTt.append(s_)
        if not stats_emitted:
            for i in range(NT):
                emit_ln_stats(Xv, sl_Xv, i)
        for g4 in range(4):
            t0 = g4 * 4
            p.emit("vector", (lambda e, t0=t0: e.tensor_scalar(
                STAT[:, 32 + t0:36 + t0], STAT[:, t0:t0 + 4], 1.0 / D, None,
                op0=ALU.mult)),
                reads=[sl_STAT], writes=[sl_STAT], partial=True)
            p.emit("vector", (lambda e, t0=t0: e.tensor_tensor(
                STAT[:, 80 + t0:84 + t0], STAT[:, 32 + t0:36 + t0],
                STAT[:, 32 + t0:36 + t0], op=ALU.mult)),
                reads=[sl_STAT], writes=[sl_STAT], partial=True)
            p.emit("vector", (lambda e, t0=t0: e.scalar_tensor_tensor(
                STAT[:, 48 + t0:52 + t0], STAT[:, 16 + t0:20 + t0], 1.0 / D,
                STAT[:, 80 + t0:84 + t0], op0=ALU.mult, op1=ALU.subtract)),
                reads=[sl_STAT], writes=[sl_STAT], partial=True)
            p.emit("scalar", (lambda e, t0=t0: e.activation(
                STAT[:, 64 + t0:68 + t0], STAT[:, 48 + t0:52 + t0], AF.Ln,
                bias=f32(EPS), scale=f32(1.0))),
                reads=[sl_STAT], writes=[sl_STAT], partial=True)
            p.emit("scalar", (lambda e, t0=t0: e.activation(
                STAT[:, 64 + t0:68 + t0], STAT[:, 64 + t0:68 + t0], AF.Exp,
                bias=f32(0.0), scale=f32(-0.5))),
                reads=[sl_STAT], writes=[sl_STAT], partial=True)
        for i in range(NT):
            p.emit("vector", (lambda e, i=i, HT=HT: e.tensor_scalar(
                HT[:, i * D:(i + 1) * D], Xv[i],
                STAT[:, 32 + i:33 + i], STAT[:, 64 + i:65 + i],
                op0=ALU.subtract, op1=ALU.mult)),
                reads=[sl_Xv[i], sl_STAT], writes=[sl_HTt[i]], partial=False)
            for j in range(NDC):
                k = (i * NDC + j) % 16
                pslot = bank_bf(4 + k // 4, (k % 4) * 128, 128)
                bsl = sl_PB[4 + k // 4]
                tr(pslot, HT[:, i * D + j * 128:i * D + (j + 1) * 128], IDENT,
                   reads=[sl_HTt[i]], wbank=bsl)
                dst = HF[:, j * T + i * 128:j * T + (i + 1) * 128]
                if j < 3:
                    vcopy(dst, pslot, reads=[bsl], writes=[sl_HFg[j][i]],
                          partial=False)
                else:
                    scopy(dst, pslot, reads=[bsl], writes=[sl_HFg[j][i]],
                          partial=False)
        return HT, sl_HTt

    for i in range(NT):
        b_, half = i // 2, i % 2
        dma("sync", XAv[i], X_in[b_, half * 128:(half + 1) * 128, :],
            writes=[sl_XA[i]])
        emit_ln_stats(XAv, sl_XA, i)
    dma("sync", CST[:, 0:128], ident_bf.ap(), writes=[sl_CST], partial=True)
    dma("sync", CSTh[0:16, 128:1152], SELPD.ap(), writes=[sl_CST], partial=True)
    dma("sync", CST[:, 1152:1664], BDMASKD.ap(), writes=[sl_CST], partial=True)
    dma("sync", CST[:, 1664:1665], ones_col.ap(), writes=[sl_CST], partial=True)
    dma("sync", CST[:, 1696:1824], TRILD.ap(), writes=[sl_CST], partial=True)
    dma("sync", CSTF[:, 0:128], ident_f.ap(), writes=[sl_CSTF])
    dma("sync", BIAh[0:1, 1152:1664], ones_row.ap(), writes=[sl_BIA], partial=True)

    HT1, sl_HT1 = emit_ln(XAv, sl_XA, "l0a", stats_emitted=True)
    if n_stages == 0:
        for i in range(NT):
            p.emit("vector", (lambda e, i=i: e.tensor_copy(
                XAv[i], HT1[:, i * D:(i + 1) * D])),
                reads=[sl_HT1[i]], writes=[sl_XA[i]])
        p.emit("vector", (lambda e: e.tensor_copy(
            XAv[15][:, 900:996], STAT[:])),
            reads=[sl_STAT], writes=[sl_XA[15]], partial=True)
    if n_stages == -1:
        pass
    if n_stages <= 0:
        for i in range(NT):
            b_, half = i // 2, i % 2
            dma("sync", X_out[b_, half * 128:(half + 1) * 128, :], XAv[i],
                reads=[sl_XA[i]], writes=())
        p.barrier_wait("sync", {"sync_q": p.cnt["sync_q"]})
        with nc.Block() as block:
            @block.tensor
            def _(t):
                for f in p.ops["tensor"]:
                    f(t)

            @block.vector
            def _(v):
                for f in p.ops["vector"]:
                    f(v)

            @block.scalar
            def _(s):
                for f in p.ops["scalar"]:
                    f(s)

            @block.gpsimd
            def _(g):
                for f in p.ops["gpsimd"]:
                    f(g)

            @block.sync
            def _(sy):
                for f in p.ops["sync"]:
                    f(sy)
        return nc

    WQTc, sl_WQTc, WPTc, sl_WPTc = [], [], [], []
    for j in range(NDC):
        a_, s_ = HTR.view(j * 2048, 0, 128, 1024, BF16, f"WQT{j}")
        WQTc.append(a_)
        sl_WQTc.append(s_)
    for j in range(NDC):
        a_, s_ = HTR.view(16384 + j * 2048, 0, 128, 1024, BF16, f"WPT{j}")
        WPTc.append(a_)
        sl_WPTc.append(s_)
    QWv, sl_QW = [], []
    for s in range(2):
        a, sl = R2.view(s * 4096, 0, 128, T, BF16, f"QW{s}")
        QWv.append(a)
        sl_QW.append(sl)
    KKF, sl_KKF = R2.view(8192, 0, 128, T, BF16, "KKF")
    VF, sl_VF = R2.view(12288, 0, 64, T, BF16, "VF")
    VT0, sl_VT0 = R2.view(16384, 0, 128, NT * 65, BF16, "VT0")
    ETv, sl_ET = [], []
    for s in range(2):
        a, sl = R2.view(18464 + s * 2048, 0, 128, 1024, BF16, f"ET{s}")
        ETv.append(a)
        sl_ET.append(sl)
    CTMv, sl_CTM = [], []
    for s in range(2):
        a_, s_ = R2.view(22560 if s == 0 else 58432, 0, 128, 256, BF16,
                         f"CTM{s}")
        CTMv.append(a_)
        sl_CTM.append(s_)
    KKW, sl_KKW = R2.view(23072, 0, 128, 1024, BF16, "KKW")
    KVW2, sl_KVW2 = R2.view(25120, 0, 128, 512, BF16, "KVW2")
    CTXF, sl_CTXF = R2.view(25632, 0, 128, NDC * T, BF16, "CTXF")
    p.emit("vector", (lambda e: e.memset(
        VT0[:].rearrange("p (i n) -> p i n", n=65)[:, :, 64], 1.0)),
        reads=(), writes=[sl_VT0], partial=False)
    dma("gpsimd", KKW[:].rearrange("p (j n) -> p j n", j=8),
        W0kkT.rearrange("(j p) n -> p j n", j=8), writes=[sl_KKW])
    dma("gpsimd", KVW2[:].rearrange("p (j n) -> p j n", j=8),
        W0vT.rearrange("(j p) n -> p j n", j=8), writes=[sl_KVW2])
    dma("gpsimd", BIAh[0:1, 0:1024], B0p, writes=[sl_BIA], partial=True)
    B0P_S = BIAh[0:1, 0:1024]
    for j in range(NDC):
        dma("gpsimd", WQTc[j][:], W0qT[j * 128:(j + 1) * 128, :],
            writes=[sl_WQTc[j]])
    for j in range(NDC):
        dma("gpsimd", WPTc[j][:], W0pT[j * 128:(j + 1) * 128, :],
            writes=[sl_WPTc[j]])
    w3st = w3st_t.ap()
    w1stA = w1st_t.ap()
    if n_stages >= 2:
        for j in range(NDC):
            dma_cast(w3st[j, :], WQKV0[j * 128:(j + 1) * 128, :], "w3c",
                     writes=[sl_w3st], partial=(j > 0))

    for tt in range(4):
        bk = 4
        for j in range(NDC):
            mm(bank(bk), KKW[:, j * 128:(j + 1) * 128],
               HF[:, j * T + tt * 512:j * T + (tt + 1) * 512],
               start=(j == 0), stop=(j == NDC - 1),
               reads=[sl_KKW] + hf_r(j, 4 * tt, 4 * tt + 4), wbank=sl_PB[bk])
        vcopy(KKF[:, tt * 512:(tt + 1) * 512], bank(bk),
              reads=[sl_PB[bk]], writes=[sl_KKF])
    for tt in range(4):
        bk = 5
        for j in range(NDC):
            mm(bank(bk)[0:64, :], KVW2[:, j * 64:(j + 1) * 64],
               HF[:, j * T + tt * 512:j * T + (tt + 1) * 512],
               start=(j == 0), stop=(j == NDC - 1),
               reads=[sl_KVW2] + hf_r(j, 4 * tt, 4 * tt + 4), wbank=sl_PB[bk])
        vcopy(VF[:, tt * 512:(tt + 1) * 512], bank(bk)[0:64, :],
              reads=[sl_PB[bk]], writes=[sl_VF])
    for i in range(NT):
        k = i % 4
        pslot = bank_bf(6 + k // 2, (k % 2) * 128, 64)
        bsl = sl_PB[6 + k // 2]
        tr(pslot, VF[0:64, i * 128:(i + 1) * 128], IDENT[0:64, 0:64],
           reads=[sl_VF], wbank=bsl)
        vcopy(VT0[:, i * 65:i * 65 + 64], pslot,
              reads=[bsl], writes=[sl_VT0])

    for hp in range(NDC):
        qs = hp % 2
        qsl = QWv[qs]
        for tt in range(4):
            bk = tt % 2
            for j in range(NDC):
                mm(bank(bk), WQTc[j][:, hp * 128:(hp + 1) * 128],
                   HF[:, j * T + tt * 512:j * T + (tt + 1) * 512],
                   start=(j == 0), stop=(j == NDC - 1),
                   reads=[sl_WQTc[j]] + hf_r(j, 4 * tt, 4 * tt + 4),
                   wbank=sl_PB[bk])
            p.emit("scalar", (lambda e, qsl=qsl, tt=tt, bk=bk: e.mul(
                qsl[:, tt * 512:(tt + 1) * 512], bank(bk), SCALE)),
                reads=[sl_PB[bk]], writes=[sl_QW[qs]], partial=(tt > 0))
        def l0_scores(b_):
            es = b_ % 2
            sb0 = 2 + 3 * es
            for h2 in range(2):
                for tkc in range(2):
                    out = PS[:, (sb0 + h2) * 512 + tkc * 256:
                             (sb0 + h2) * 512 + tkc * 256 + 256]
                    mm(out,
                       KKF[h2 * 64:h2 * 64 + 64,
                           b_ * 256 + tkc * 128:b_ * 256 + (tkc + 1) * 128],
                       qsl[h2 * 64:h2 * 64 + 64, b_ * 256:(b_ + 1) * 256],
                       start=True, stop=True,
                       reads=[sl_KKF, sl_QW[qs]], wbank=sl_PB[sb0 + h2])
            p.emit("scalar", (lambda e, es=es, sb0=sb0: e.activation(
                ETv[es], PS[:, sb0 * 512:sb0 * 512 + 1024],
                AF.Exp, bias=EXPB, scale=f32(1.0))),
                reads=[sl_PB[sb0], sl_PB[sb0 + 1]], writes=[sl_ET[es]])

        def l0_av(b_):
            es = b_ % 2
            ab = 4 if es == 0 else 7
            for h2 in range(2):
                for tqc in range(2):
                    for tkc in range(2):
                        lhs = ETv[es][:, h2 * 512 + tkc * 256 + tqc * 128:
                                      h2 * 512 + tkc * 256 + (tqc + 1) * 128]
                        mm(bank(ab, (h2 * 2 + tqc) * 65, 65), lhs,
                           VT0[:, (b_ * 2 + tkc) * 65:(b_ * 2 + tkc) * 65 + 65],
                           start=(tkc == 0), stop=(tkc == 1),
                           reads=[sl_ET[es], sl_VT0], wbank=sl_PB[ab])
            p.emit("vector", (lambda e, ab=ab, es=es: e.reciprocal(
                RS[:, es * 4:es * 4 + 4],
                bank(ab, 0, 260).rearrange("p (i n) -> p i n", n=65)[:, :, 64])),
                reads=[sl_PB[ab]], writes=[sl_RS], partial=True)
            for h2 in range(2):
                for tqc in range(2):
                    p.emit("vector", (lambda e, h2=h2, tqc=tqc, es=es, ab=ab:
                                      e.tensor_scalar(
                        CTMv[es][:, tqc * 128 + h2 * 64:tqc * 128 + h2 * 64 + 64],
                        bank(ab, (h2 * 2 + tqc) * 65, 64),
                        RS[:, es * 4 + h2 * 2 + tqc:es * 4 + h2 * 2 + tqc + 1],
                        None, op0=ALU.mult)),
                        reads=[sl_PB[ab], sl_RS], writes=[sl_CTM[es]],
                        partial=True)

        def l0_tr(b_):
            es = b_ % 2
            for tqc in range(2):
                pslot = bank_bf(es, 768 + tqc * 128, 128)
                bsl = sl_PB[es]
                tr(pslot, CTMv[es][:, tqc * 128:(tqc + 1) * 128], IDENT,
                   reads=[sl_CTM[es]], wbank=bsl)
                dst = CTXF[:, hp * T + b_ * 256 + tqc * 128:
                           hp * T + b_ * 256 + (tqc + 1) * 128]
                scopy(dst, pslot, reads=[bsl], writes=[sl_CTXF])

        l0_scores(0)
        for b_ in range(B):
            if b_ + 1 < B:
                l0_scores(b_ + 1)
            l0_av(b_)
            if b_ >= 1:
                l0_tr(b_ - 1)
        l0_tr(B - 1)
    for i in range(NT):
        for dh in range(2):
            bk = (i * 2 + dh) % 2
            mm(bank(bk), ONE_ROW128, B0P_S[0:1, dh * 512:(dh + 1) * 512],
               start=True, stop=False, reads=[sl_BIA], wbank=sl_PB[bk])
            for j in range(NDC):
                mm(bank(bk), CTXF[:, j * T + i * 128:j * T + (i + 1) * 128],
                   WPTc[j][:, dh * 512:(dh + 1) * 512],
                   start=False, stop=(j == NDC - 1),
                   reads=[sl_CTXF, sl_WPTc[j]], wbank=sl_PB[bk])
            p.emit("vector", (lambda e, i=i, dh=dh, bk=bk: e.scalar_tensor_tensor(
                XAv[i][:, dh * 512:(dh + 1) * 512], bank(bk), 1.0,
                XAv[i][:, dh * 512:(dh + 1) * 512],
                op0=ALU.mult, op1=ALU.add)),
                reads=[sl_PB[bk], sl_XA[i]], writes=[sl_XA[i]], partial=True)
        if n_stages >= 2:
            emit_ln_stats(XAv, sl_XA, i)

    if n_stages >= 2:
        HT2, sl_HT2 = emit_ln(XAv, sl_XA, "l0f", stats_emitted=True)
        W3c, sl_W3c = [], []
        for j in range(NDC):
            a, s = R2.view(j * 6144, 0, 128, 3072, BF16, f"W3_{j}")
            W3c.append(a)
            sl_W3c.append(s)
        QT0t, sl_QT0t, KT0t, sl_KT0t, VTft, sl_VTft = [], [], [], [], [], []
        for i in range(NT):
            a, s = R2.view(49152 + i * 128, 0, 128, HD, BF16, f"QT0_{i}")
            QT0t.append(a)
            sl_QT0t.append(s)
            a, s = R2.view(51200 + i * 128, 0, 128, HD, BF16, f"KT0_{i}")
            KT0t.append(a)
            sl_KT0t.append(s)
            a, s = R2.view(53248 + i * 128, 0, 128, HD, BF16, f"VTf_{i}")
            VTft.append(a)
            sl_VTft.append(s)
        QKVsrc = R2h[:, 24576:27648]
        SQS, sl_SQS = R2.view(55296, 0, 128, 128, BF16, "SQS")
        for j in range(NDC):
            dma("sync" if j % 2 == 0 else "gpsimd", W3c[j],
                w3st[j, :].rearrange("(p n) -> p n", p=128),
                reads=[sl_w3st], writes=[sl_W3c[j]])
        dma("gpsimd", SIMB[:].rearrange("p (j n) -> p j n", j=8),
            SIM.rearrange("(j p) n -> p j n", j=8), writes=[sl_SIMB])
        if n_stages >= 3:
            for j in range(NDC):
                dma_cast(w1stA[2 * NDC + j, :],
                         W1kT[j * 128:(j + 1) * 128, :], "w1c",
                         writes=[sl_w1st], partial=True)
        dma("sync", STT2[0:1, 0:16], GATES, writes=[sl_STT2], partial=True)
        p.emit("vector", (lambda e: e.tensor_tensor(
            SQS[:], SIMB[:], SIMB[:], op=ALU.mult)),
            reads=[sl_SIMB], writes=[sl_SQS])
        for j in range(NDC):
            mm(bank(7, 0, 16)[0:1, :], ONE_COL, SQS[:, j * 16:(j + 1) * 16],
               start=(j == 0), stop=(j == NDC - 1),
               reads=[sl_SQS, sl_CST], wbank=sl_PB[7])
        p.emit("scalar", (lambda e: e.activation(
            STT2[0:1, 16:32], bank(7, 0, 16)[0:1, :], AF.Ln,
            bias=f32(1e-12), scale=f32(1.0))),
            reads=[sl_PB[7]], writes=[sl_STT2], partial=True)
        p.emit("scalar", (lambda e: e.activation(
            STT2[0:1, 16:32], STT2[0:1, 16:32], AF.Exp, bias=f32(0.0),
            scale=f32(-0.5))),
            reads=[sl_STT2], writes=[sl_STT2], partial=True)
        p.emit("scalar", (lambda e: e.mul(
            STT2[0:1, 16:32], STT2[0:1, 16:32], f32(1.0 / 32.0))),
            reads=[sl_STT2], writes=[sl_STT2], partial=True)
        p.emit("scalar", (lambda e: e.activation(
            STT2[0:1, 32:48], STT2[0:1, 0:16], AF.Exp, bias=f32(0.0),
            scale=f32(-1.0))),
            reads=[sl_STT2], writes=[sl_STT2], partial=True)
        p.emit("vector", (lambda e: e.tensor_scalar(
            STT2[0:1, 32:48], STT2[0:1, 32:48], 1.0, None, op0=ALU.add)),
            reads=[sl_STT2], writes=[sl_STT2], partial=True)
        p.emit("vector", (lambda e: e.reciprocal(
            STT2[0:1, 32:48], STT2[0:1, 32:48])),
            reads=[sl_STT2], writes=[sl_STT2], partial=True)
        p.emit("vector", (lambda e: e.tensor_copy(
            BIAh[0:1, 1024:1040], STT2[0:1, 16:32])),
            reads=[sl_STT2], writes=[sl_BIA], partial=True)
        p.emit("vector", (lambda e: e.tensor_copy(
            BIAh[0:1, 1040:1056], STT2[0:1, 32:48])),
            reads=[sl_STT2], writes=[sl_BIA], partial=True)
        mm(bank(7, 16, 16), ONE_ROW128, BIAh[0:1, 1024:1040],
           start=True, stop=True, reads=[sl_BIA], wbank=sl_PB[7])
        mm(bank(7, 32, 16), ONE_ROW128, BIAh[0:1, 1040:1056],
           start=True, stop=True, reads=[sl_BIA], wbank=sl_PB[7])
        p.emit("vector", (lambda e: e.tensor_copy(
            GSCR[:, 128:144], bank(7, 16, 16))),
            reads=[sl_PB[7]], writes=[sl_GSCR], partial=True)
        p.emit("vector", (lambda e: e.tensor_copy(
            GSCR[:, 144:160], bank(7, 32, 16))),
            reads=[sl_PB[7]], writes=[sl_GSCR], partial=True)
        RSNB = GSCR[:, 128:144]
        SGB = GSCR[:, 144:160]
        p.emit("vector", (lambda e: e.tensor_tensor(
            RS[:, 16:32], STAT[:, 64:80], STAT[:, 64:80], op=ALU.mult)),
            reads=[sl_STAT], writes=[sl_RS], partial=True)
        p.emit("vector", (lambda e: e.tensor_tensor(
            RS[:, 32:48], STAT[:, 48:64], RS[:, 16:32], op=ALU.mult)),
            reads=[sl_STAT, sl_RS], writes=[sl_RS], partial=True)
        p.emit("scalar", (lambda e: e.activation(
            RS[:, 32:48], RS[:, 32:48], AF.Ln, bias=f32(0.0), scale=f32(1.0))),
            reads=[sl_RS], writes=[sl_RS], partial=True)
        p.emit("scalar", (lambda e: e.activation(
            RS[:, 32:48], RS[:, 32:48], AF.Exp, bias=f32(0.0), scale=f32(-0.5))),
            reads=[sl_RS], writes=[sl_RS], partial=True)
        RHINV = RS[:, 32:48]

        LG = GSCR[:, 0:16]
        MK = GSCR[:, 16:32]
        TMP = GSCR[:, 32:48]
        GTD = GSCR[:, 48:64]
        M1 = GSCR[:, 64:65]
        M2 = GSCR[:, 65:66]
        SEL = GSCR[:, 66:67]
        NMX = GSCR[:, 67:68]
        SUM = GSCR[:, 68:69]
        RSM = GSCR[:, 69:70]
        FB = GSCR[:, 80:96]
        UU = GSCR[:, 96:112]
        AM = GSCR[:, 112:128]
        a2aI = [t.ap() for t in a2aI_t]
        a2aO = [t.ap() for t in a2aO_t]
        for ti in range(NT):
            tc = (ti * 2) % NT + (ti * 2) // NT
            gb = 6 + (ti % 2)
            for j in range(NDC):
                mm(bank(gb, 0, 16), HF[:, j * T + tc * 128:j * T + (tc + 1) * 128],
                   SIMB[:, j * 16:(j + 1) * 16],
                   start=(j == 0), stop=(j == NDC - 1),
                   reads=[sl_HFg[j][tc], sl_SIMB], wbank=sl_PB[gb])
            p.emit("vector", (lambda e, tc=tc, gb=gb: e.scalar_tensor_tensor(
                LG, bank(gb, 0, 16), RHINV[:, tc:tc + 1], RSNB,
                op0=ALU.mult, op1=ALU.mult)),
                reads=[sl_PB[gb], sl_RS, sl_GSCR], writes=[sl_GSCR], partial=True)
            p.emit("vector", (lambda e: e.tensor_tensor(
                LG, LG, SGB, op=ALU.subtract)),
                reads=[sl_GSCR], writes=[sl_GSCR], partial=True)
            p.emit("vector", (lambda e: e.tensor_scalar(
                TMP, LG, 1.0, -3e38, op0=ALU.mult, op1=ALU.max, accum_out=M1)),
                reads=[sl_GSCR], writes=[sl_GSCR], partial=True)
            p.emit("vector", (lambda e: e.tensor_scalar(
                MK, LG, 0.0, None, op0=ALU.is_gt)),
                reads=[sl_GSCR], writes=[sl_GSCR], partial=True)
            p.emit("vector", (lambda e: e.tensor_scalar(
                GTD, LG, 0.0, None, op0=ALU.max)),
                reads=[sl_GSCR], writes=[sl_GSCR], partial=True)
            p.emit("vector", (lambda e: e.tensor_scalar(
                AM, LG, M1, None, op0=ALU.is_ge)),
                reads=[sl_GSCR], writes=[sl_GSCR], partial=True)
            p.emit("vector", (lambda e: e.scalar_tensor_tensor(
                TMP, AM, -2e9, LG, op0=ALU.mult, op1=ALU.add)),
                reads=[sl_GSCR], writes=[sl_GSCR], partial=True)
            p.emit("vector", (lambda e: e.tensor_scalar(
                TMP, TMP, 1.0, -3e38, op0=ALU.mult, op1=ALU.max,
                accum_out=M2)),
                reads=[sl_GSCR], writes=[sl_GSCR], partial=True)
            p.emit("vector", (lambda e: e.tensor_scalar(
                FB, LG, M2, None, op0=ALU.is_ge)),
                reads=[sl_GSCR], writes=[sl_GSCR], partial=True)
            p.emit("vector", (lambda e: e.tensor_scalar(
                SEL, M1, 0.0, None, op0=ALU.is_le)),
                reads=[sl_GSCR], writes=[sl_GSCR], partial=True)
            p.emit("vector", (lambda e: e.tensor_scalar(
                FB, FB, SEL, None, op0=ALU.mult)),
                reads=[sl_GSCR], writes=[sl_GSCR], partial=True)
            p.emit("vector", (lambda e: e.tensor_tensor(
                MK, MK, FB, op=ALU.add)),
                reads=[sl_GSCR], writes=[sl_GSCR], partial=True)
            p.emit("vector", (lambda e: e.tensor_scalar(
                UU, MK, 1e9, -1e9, op0=ALU.mult, op1=ALU.add)),
                reads=[sl_GSCR], writes=[sl_GSCR], partial=True)
            p.emit("vector", (lambda e: e.tensor_tensor(
                UU, UU, GTD, op=ALU.add)),
                reads=[sl_GSCR], writes=[sl_GSCR], partial=True)
            p.emit("vector", (lambda e: e.tensor_scalar(
                TMP, UU, 1.0, -3e38, op0=ALU.mult, op1=ALU.max,
                accum_out=NMX)),
                reads=[sl_GSCR], writes=[sl_GSCR], partial=True)
            p.emit("vector", (lambda e: e.tensor_scalar(
                NMX, NMX, -1.0, None, op0=ALU.mult)),
                reads=[sl_GSCR], writes=[sl_GSCR], partial=True)
            p.emit("scalar", (lambda e: e.activation(
                UU, UU, AF.Exp, bias=NMX, scale=f32(1.0), accum_out=SUM)),
                reads=[sl_GSCR], writes=[sl_GSCR], partial=True)
            p.emit("vector", (lambda e: e.reciprocal(RSM, SUM)),
                   reads=[sl_GSCR], writes=[sl_GSCR], partial=True)
            p.emit("vector", (lambda e, tc=tc: e.tensor_scalar(
                RW[:, tc * 16:(tc + 1) * 16], UU, RSM, None, op0=ALU.mult)),
                reads=[sl_GSCR], writes=[sl_RW], partial=True)
            for hf2 in range(2):
                for j in range(NDC):
                    lhs = HF[:, j * T + tc * 128:j * T + (tc + 1) * 128]
                    for s3 in range(3):
                        s = hf2 * 3 + s3
                        mm(bank(s), lhs,
                           W3c[j][:, s * 512:(s + 1) * 512],
                           start=(j == 0), stop=(j == NDC - 1),
                           reads=[sl_HFg[j][tc], sl_W3c[j]], wbank=sl_PB[s])
                for e8 in range(8):
                    e_ = hf2 * 8 + e8
                    src = PS[:, e_ * 192:(e_ + 1) * 192]
                    rbs = [sl_PB[(e_ * 192) // 512],
                           sl_PB[((e_ + 1) * 192 - 1) // 512]]
                    if e_ == 0:
                        p.emit("vector", (lambda e, src=src, tc=tc: e.tensor_scalar(
                            QKVA[:], src, RW[:, tc * 16:tc * 16 + 1], None,
                            op0=ALU.mult)),
                            reads=rbs + [sl_RW], writes=[sl_QKVA])
                    else:
                        p.emit("vector", (lambda e, src=src, tc=tc, e_=e_:
                                          e.scalar_tensor_tensor(
                            QKVA[:], src, RW[:, tc * 16 + e_:tc * 16 + e_ + 1],
                            QKVA[:], op0=ALU.mult, op1=ALU.add)),
                            reads=rbs + [sl_RW, sl_QKVA], writes=[sl_QKVA],
                            partial=True)
            p.emit("scalar", (lambda e, tc=tc: e.mul(
                QT0t[tc][:], QKVA[:, 0:64], SCALE)),
                reads=[sl_QKVA], writes=[sl_QT0t[tc]], partial=False)
            scopy(KT0t[tc][:], QKVA[:, 64:128],
                  reads=[sl_QKVA], writes=[sl_KT0t[tc]], partial=False)
            scopy(VTft[tc][:], QKVA[:, 128:192],
                  reads=[sl_QKVA], writes=[sl_VTft[tc]], partial=False)
            h_, j_ = tc % 2, tc // 2
            dma("sync",
                a2aI[h_][j_, :].rearrange("(blk p n) -> blk p n",
                                          blk=3, p=128).transpose([1, 0, 2]),
                QKVsrc.rearrange("p (blk n) -> p blk n",
                                 blk=3)[:, :, tc * HD:(tc + 1) * HD],
                reads=[sl_QT0t[tc], sl_KT0t[tc], sl_VTft[tc]],
                writes=[sl_a2aih[h_]], partial=(j_ > 0))
            if ti == NT // 2 - 1 or ti == NT - 1:
                hh = ti // (NT // 2)

                def cc_a2a(e, hh=hh):
                    with allow_ici_all_to_all():
                        return e.collective_compute(
                            "AllToAll", ALU.bypass,
                            replica_groups=[list(range(NCORES))],
                            ins=[a2aI_t[hh].ap().opt()],
                            outs=[a2aO_t[hh].ap().opt()])
                p.emit("gpsimd", cc_a2a, reads=[sl_a2aih[hh]],
                       writes=[sl_a2aoh[hh]], sem="cc", inc=1)

        GKVv, sl_GKVh = [], []
        for h in range(2):
            a_, s_ = R2.view(0, 0, 128, 3088, BF16, f"GKV{h}")
            GKVv.append(a_)
            sl_GKVh.append(s_)
        GKV = GKVv[0]
        p.emit("vector", (lambda e: e.memset(GKV[:, 2048:3088].rearrange(
            "p (i n) -> p i n", n=65)[:, :, 64], 1.0)),
            reads=(), writes=[sl_GKVh[0], sl_GKVh[1]], partial=False)
        QFF, sl_QFF0 = R2.view(6176, 0, 64, T, BF16, "QFF")
        KFF, sl_KFF0 = R2.view(10272, 0, 64, T, BF16, "KFF")
        sl_QFFt = [Slot(f"QFF{i}") for i in range(NT)]
        sl_KFFt = [Slot(f"KFF{i}") for i in range(NT)]
        for i in range(NT):
            _merge(sl_QFFt[i].w, sl_QFF0.w)
            _merge(sl_QFFt[i].we, sl_QFF0.we)
            _merge(sl_QFFt[i].r, sl_QFF0.r)
            _merge(sl_KFFt[i].w, sl_KFF0.w)
            _merge(sl_KFFt[i].we, sl_KFF0.we)
            _merge(sl_KFFt[i].r, sl_KFF0.r)
        ET2v, sl_ET2v = [], []
        for s in range(3):
            a_, s_ = R2.view((16416 + s * 2048) if s < 2 else 28704,
                             0, 128, 1024, BF16, f"ET2_{s}")
            ET2v.append(a_)
            sl_ET2v.append(s_)
        for h in range(2):
            for blk in range(3):
                if blk < 2:
                    dst = GKVv[h][:, blk * 1024:(blk + 1) * 1024].rearrange(
                        "p (r two n) -> p r two n", two=2, n=64)[:, :, h, :]
                else:
                    dst = GKVv[h][:, 2048:3088].rearrange(
                        "p (r two n) -> p r two n", two=2, n=65)[:, :, h, 0:64]
                srcb = a2aO[h][:, blk * 8192:(blk + 1) * 8192].rearrange(
                    "r (pp n) -> r pp n", pp=128)
                dma("gpsimd", dst, srcb.transpose([1, 0, 2]),
                    reads=[sl_a2aoh[h]], writes=[sl_GKVh[h]],
                    partial=True)
        if n_stages >= 3:
            for j in range(NDC):
                dma_cast(w1stA[3 * NDC + j, :],
                         W1vT[j * 128:(j + 1) * 128, :], "w1c",
                         writes=[sl_w1st], partial=True)
            for j in range(NDC):
                dma_cast(w1stA[j, :], W1qT[j * 128:(j + 1) * 128, :], "w1c",
                         writes=[sl_w1st], partial=True)
            for j in range(NDC):
                dma_cast(w1stA[NDC + j, :], W1oT[j * 128:(j + 1) * 128, :],
                         "w1c", writes=[sl_w1st], partial=True)
        if n_stages >= 4:
            for e_ in range(NE1):
                for j in range(NDC):
                    dma_cast(ewst_t.ap()[e_ * NDC + j, :],
                             EW[e_, j * 128:(j + 1) * 128, :], "ewc",
                             writes=[sl_ewst], partial=True)
        RWF, sl_RWF = R2.view(61440, 0, 16, T, BF16, "RWF")
        for tc in range(NT):
            k = tc % 2
            pslot = bank(0, k * 128, 128)
            bsl = sl_PB[0]
            tr(pslot[0:16, :], RW[:, tc * 16:(tc + 1) * 16], CSTF[:, 0:128],
               reads=[sl_RW], wbank=bsl, ident_slot=sl_CSTF)
            vcopy(RWF[0:16, tc * 128:(tc + 1) * 128], pslot[0:16, :],
                  reads=[bsl], writes=[sl_RWF])
        WOS, sl_WOS = R2.view(45056, 0, 128, 8192, BF16, "WOS")
        for j in range(NDC):
            dma("gpsimd", WOS[:, j * 1024:(j + 1) * 1024],
                WO0[j * 128:(j + 1) * 128, :], writes=[sl_WOS], partial=(j > 0))
        for i in [x for x in range(NT) if x % 2 == 0] + \
                 [x for x in range(NT) if x % 2 == 1]:
            k = i % 2
            pslot = bank_bf(6, k * 128, 128)[0:64, :]
            bsl = sl_PB[6]
            tr(pslot, GKV[:, i * HD:(i + 1) * HD], IDENT,
               reads=[sl_GKVh[i % 2]], wbank=bsl)
            vcopy(QFF[0:64, i * 128:(i + 1) * 128], pslot,
                  reads=[bsl], writes=[sl_QFFt[i]], partial=False)
            pslot2 = bank_bf(7, k * 128, 128)[0:64, :]
            bsl2 = sl_PB[7]
            tr(pslot2, GKV[:, 1024 + i * HD:1024 + (i + 1) * HD], IDENT,
               reads=[sl_GKVh[i % 2]], wbank=bsl2)
            scopy(KFF[0:64, i * 128:(i + 1) * 128], pslot2,
                  reads=[bsl2], writes=[sl_KFFt[i]], partial=False)

        c2aI = [t.ap() for t in c2aI_t]
        c2aO = [t.ap() for t in c2aO_t]
        CTX65, sl_CTX65 = R2.view(20512, 0, 65, 4096, BF16, "CTX65")
        qgroups = [[0, 2, 4, 6], [8, 10, 12, 14], [1, 3, 5, 7], [9, 11, 13, 15]]
        for gi, grp in enumerate(qgroups):
            t0 = grp[0]
            cb = 4 + gi % 2
            rhsq = QFF[0:64, :].rearrange(
                "p (a two n) -> p a two n", two=2,
                n=128)[:, t0 // 2:t0 // 2 + 4, t0 % 2, :]
            ntk = grp[-1] + 1
            first_av = True
            wl = [list(range(w0, min(w0 + 2, ntk))) for w0 in range(0, ntk, 2)]
            for wvi, wtk in enumerate(wl):
                pp = wvi % 3
                et = ET2v[pp]
                set_ = sl_ET2v[pp]
                for wi, tkc in enumerate(wtk):
                    bkx = (0, 2, 6)[pp] + wi
                    mm(bank(bkx), KFF[0:64, tkc * 128:(tkc + 1) * 128], rhsq,
                       start=True, stop=True,
                       reads=[sl_KFFt[tkc]] + [sl_QFFt[t] for t in grp],
                       wbank=sl_PB[bkx])
                b0 = (0, 2, 6)[pp]
                p.emit("scalar", (lambda e, b0=b0, et=et, wn=len(wtk):
                                  e.activation(
                    et[:, 0:wn * 512], PS[:, b0 * 512:b0 * 512 + wn * 512],
                    AF.Exp, bias=EXPB, scale=f32(1.0))),
                    reads=[sl_PB[b0], sl_PB[b0 + 1]],
                    writes=[set_], partial=False)
                for wi, tkc in enumerate(wtk):
                    for s, tq in enumerate(grp):
                        if tq < tkc:
                            p.emit("vector", (lambda e, wi=wi, s=s, et=et: e.memset(
                                et[:, wi * 512 + s * 128:wi * 512 + (s + 1) * 128],
                                0.0)),
                                reads=[set_], writes=[set_], partial=True)
                        elif tq == tkc:
                            p.emit("vector", (lambda e, wi=wi, s=s, et=et:
                                              e.tensor_tensor(
                                et[:, wi * 512 + s * 128:wi * 512 + (s + 1) * 128],
                                et[:, wi * 512 + s * 128:wi * 512 + (s + 1) * 128],
                                TRIL, op=ALU.mult)),
                                reads=[set_, sl_CST], writes=[set_],
                                partial=True)
                for wi, tkc in enumerate(wtk):
                    mm(PS[0:65, cb * 512:(cb + 1) * 512],
                       GKV[:, 2048 + tkc * 65:2048 + (tkc + 1) * 65],
                       et[:, wi * 512:(wi + 1) * 512],
                       start=first_av, stop=(tkc == ntk - 1),
                       reads=[sl_GKVh[tkc % 2], set_], wbank=sl_PB[cb])
                    first_av = False
            p.emit("vector", (lambda e, t0=t0, cb=cb: e.tensor_copy(
                CTX65[0:65, :].rearrange(
                    "p (a two n) -> p a two n", two=2,
                    n=128)[:, t0 // 2:t0 // 2 + 4, t0 % 2, :],
                PS[0:65, cb * 512:(cb + 1) * 512].rearrange(
                    "p (s n) -> p s n", n=128))),
                reads=[sl_PB[cb]], writes=[sl_CTX65], partial=(gi > 0))
            for s, tq in enumerate(grp):
                h_, j_ = tq % 2, tq // 2
                dma("sync", c2aI[h_][j_, :].rearrange("(p n) -> p n", p=65),
                    CTX65[0:65, tq * 128:(tq + 1) * 128],
                    reads=[sl_CTX65], writes=[sl_c2aih[h_]], partial=(j_ > 0))
            if gi in (1, 3):
                hh = gi // 2 if gi == 1 else 1
                hh = 0 if gi == 1 else 1

                def cc_c2a(e, hh=hh):
                    with allow_ici_all_to_all():
                        return e.collective_compute(
                            "AllToAll", ALU.bypass,
                            replica_groups=[list(range(NCORES))],
                            ins=[c2aI_t[hh].ap().opt()],
                            outs=[c2aO_t[hh].ap().opt()])
                p.emit("gpsimd", cc_c2a, reads=[sl_c2aih[hh]],
                       writes=[sl_c2aoh[hh]], sem="cc", inc=1)

        C2F, sl_C2F = R2.view(4096, 0, 128, T, BF16, "C2F")
        SUMR, sl_SUMR = R2.view(24608, 0, 1, T, BF16, "SUMR")
        for h in range(2):
            srcb = c2aO[h][:, :].rearrange("r (p n) -> r p n", p=65)
            dma("gpsimd",
                C2F[0:64, :].rearrange("p (r two n) -> p r two n",
                                       two=2, n=128)[:, :, h, :],
                srcb[:, 0:64, :].transpose([1, 0, 2]),
                reads=[sl_c2aoh[h]], writes=[sl_C2F], partial=(h > 0))
            dma("gpsimd",
                SUMR[0:1, :].rearrange("p (r two n) -> p r two n",
                                       two=2, n=128)[:, :, h, :],
                srcb[:, 64:65, :].transpose([1, 0, 2]),
                reads=[sl_c2aoh[h]], writes=[sl_SUMR], partial=(h > 0))
        dma("sync", C2F[64:128, :], C2F[0:64, :],
            reads=[sl_C2F], writes=[sl_C2F], partial=True)
        for sfc in range(4):
            mm(PS[0:16, sfc * 512:(sfc + 1) * 512],
               BIAh[0:1, 1152:1168],
               SUMR[0:1, sfc * 512:(sfc + 1) * 512],
               start=True, stop=True,
               reads=[sl_BIA, sl_SUMR], wbank=sl_PB[sfc])
            p.emit("vector", (lambda e, sfc=sfc: e.reciprocal(
                PS[0:16, sfc * 512:(sfc + 1) * 512],
                PS[0:16, sfc * 512:(sfc + 1) * 512])),
                reads=[sl_PB[sfc]], writes=[sl_PB[sfc]], partial=True)
            p.emit("vector", (lambda e, sfc=sfc: e.tensor_tensor(
                RWF[0:16, sfc * 512:(sfc + 1) * 512],
                RWF[0:16, sfc * 512:(sfc + 1) * 512],
                PS[0:16, sfc * 512:(sfc + 1) * 512], op=ALU.mult)),
                reads=[sl_RWF, sl_PB[sfc]], writes=[sl_RWF], partial=True)

        for i in range(NT):
            _merge(sl_QFF0.r, sl_QFFt[i].r)
            _merge(sl_QFF0.w, sl_QFFt[i].w)
            _merge(sl_KFF0.r, sl_KFFt[i].r)
            _merge(sl_KFF0.w, sl_KFFt[i].w)
        MIXv, sl_MIX = [], []
        for pr in range(8):
            a, sl = R2.view(12288 + pr * 4096, 0, 128, T, BF16, f"MIX{pr}")
            MIXv.append(a)
            sl_MIX.append(sl)
        for pr in range(8):
            for s in range(4):
                bk = 4 + (pr * 4 + s) % 2
                mm(bank(bk), SELP[:, pr * 128:(pr + 1) * 128],
                   RWF[0:16, s * 512:(s + 1) * 512],
                   start=True, stop=True,
                   reads=[sl_RWF, sl_CST], wbank=sl_PB[bk])
                p.emit("vector", (lambda e, pr=pr, s=s, bk=bk: e.tensor_tensor(
                    MIXv[pr][:, s * 512:(s + 1) * 512],
                    C2F[:, s * 512:(s + 1) * 512], bank(bk), op=ALU.mult)),
                    reads=[sl_C2F, sl_PB[bk]], writes=[sl_MIX[pr]],
                    partial=(s > 0))
        XBv = [None] * NT
        sl_XB = [None] * NT
        for i in range(3):
            a, s = R2.view(i * 4096, 0, 128, D, F32, f"XB{i}")
            XBv[i] = a
            sl_XB[i] = s
        XAfull = R1h[:].bitcast(F32)

        def emit_permute(i2, q):
            p0 = 16 * (i2 % 8)
            hi = i2 // 8
            srcp = XAfull[p0:p0 + 16, :].rearrange(
                "p (b two c) -> p b two c", two=2, c=1024)[:, :, hi, :]
            dma(q, XBv[i2], srcp,
                reads=[sl_XA[2 * b_ + hi] for b_ in range(8)],
                writes=[sl_XB[i2]])
            if n_stages >= 3:
                emit_ln_stats(XBv, sl_XB, i2)

        for i in [x for x in range(NT) if x % 2 == 0] + \
                 [x for x in range(NT) if x % 2 == 1]:
            for dh in range(2):
                bk = (i * 2 + dh) % 2
                for pr in range(8):
                    mm(bank(bk), MIXv[pr][:, i * 128:(i + 1) * 128],
                       WOS[:, pr * 1024 + dh * 512:pr * 1024 + (dh + 1) * 512],
                       start=(pr == 0), stop=(pr == 7),
                       reads=[sl_MIX[pr], sl_WOS], wbank=sl_PB[bk])
                p.emit("vector", (lambda e, i=i, dh=dh, bk=bk:
                                  e.scalar_tensor_tensor(
                    XAv[i][:, dh * 512:(dh + 1) * 512], bank(bk), 1.0,
                    XAv[i][:, dh * 512:(dh + 1) * 512],
                    op0=ALU.mult, op1=ALU.add)),
                    reads=[sl_PB[bk], sl_XA[i]], writes=[sl_XA[i]], partial=True)
            if i == NT - 2:
                for i2 in range(3):
                    emit_permute(i2, "sync" if i2 % 2 == 0 else "gpsimd")
            if i == NT - 1:
                for i2 in range(3, NT):
                    a_, s_ = R2.view(i2 * 4096, 0, 128, D, F32, f"XB{i2}")
                    XBv[i2] = a_
                    sl_XB[i2] = s_
                    emit_permute(i2, "sync" if i2 % 2 == 0 else "gpsimd")

    if n_stages >= 3:
        HT3, sl_HT3 = emit_ln(XBv, sl_XB, "l1a", stats_emitted=True)
        VT1, sl_VT1 = R1.view(0, 0, 128, NT * 16 * 65, BF16, "VT1")
        WKc, sl_WKc = [], []
        for j in range(NDC):
            a_, s_ = R1.view(33280 + j * 2048, 0, 128, 1024, BF16, f"WK_{j}")
            WKc.append(a_)
            sl_WKc.append(s_)
        WSv, sl_WS = [], []
        for s in range(3):
            a_, sl = R1.view(51712 + s * 2048, 0, 128, 1024, BF16, f"WS1_{s}")
            WSv.append(a_)
            sl_WS.append(sl)
        EB1, sl_EB1 = R1.view(57856, 0, 128, 2048, BF16, "EB1")
        sl_EB1g = []
        for hg in range(4):
            _, s_ = R1.view(57856 + hg * 1024, 0, 128, 512, BF16, f"EB1g{hg}")
            sl_EB1g.append(s_)
        KA, sl_KA = HTR.view(0, 0, 128, 16384, BF16, "KA")
        sl_KAc = []
        for dc in range(NDC):
            _, s_ = HTR.view(dc * 4096, 0, 128, 2048, BF16, f"KAc{dc}")
            sl_KAc.append(s_)
        p.emit("vector", (lambda e: e.memset(VT1[:].rearrange(
            "p (t h n) -> p t h n", h=16, n=65)[:, :, :, 64], 1.0)),
            reads=(), writes=[sl_VT1], partial=False)
        w1st = w1st_t.ap()

        for j in range(NDC):
            dma("sync" if j % 2 == 0 else "gpsimd", WKc[j][:],
                w1st[2 * NDC + j, :].rearrange("(p n) -> p n", p=128),
                reads=[sl_w1st], writes=[sl_WKc[j]])
        dma("gpsimd", BIAh[0:1, 0:1024], B1k, writes=[sl_BIA], partial=True)
        for tt in range(4):
            tcol = tt * 512
            for dc in range(NDC):
                for j in range(NDC):
                    mm(bank(dc), WKc[j][:, dc * 128:(dc + 1) * 128],
                       HF[:, j * T + tcol:j * T + tcol + 512],
                       start=(j == 0), stop=False,
                       reads=[sl_WKc[j]] + hf_r(j, 4 * tt, 4 * tt + 4),
                       wbank=sl_PB[dc])
                mm(bank(dc), BIAh[0:1, dc * 128:(dc + 1) * 128],
                   ONE_ROW512,
                   start=False, stop=True, reads=[sl_BIA], wbank=sl_PB[dc])
                p.emit("scalar", (lambda e, dc=dc, tcol=tcol: e.copy(
                    KA[:, dc * 2048 + tcol:dc * 2048 + tcol + 512], bank(dc))),
                    reads=[sl_PB[dc]], writes=[sl_KAc[dc]], partial=(tt > 0))
        WVc, sl_WVc = [], []
        for j in range(NDC):
            a_, s_ = R1.view(33280 + j * 2048, 0, 128, 1024, BF16, f"WV_{j}")
            WVc.append(a_)
            sl_WVc.append(s_)
            dma("sync" if j % 2 == 0 else "gpsimd", WVc[j][:],
                w1st[3 * NDC + j, :].rearrange("(p n) -> p n", p=128),
                reads=[sl_w1st], writes=[sl_WVc[j]])
        dma("gpsimd", BIAh[0:1, 0:1024], B1v, writes=[sl_BIA], partial=True)

        for tt in range(4):
            tcol = tt * 512
            for i4 in range(4):
                for half in range(2):
                    bk = i4 * 2 + half
                    for j in range(NDC):
                        mm(bank(bk), HF[:, j * T + tcol + i4 * 128:
                                        j * T + tcol + (i4 + 1) * 128],
                           WVc[j][:, half * 512:(half + 1) * 512],
                           start=(j == 0), stop=False,
                           reads=[sl_WVc[j], sl_HFg[j][4 * tt + i4]],
                           wbank=sl_PB[bk])
                    mm(bank(bk), ONE_ROW128,
                       BIAh[0:1, half * 512:(half + 1) * 512],
                       start=False, stop=True, reads=[sl_BIA], wbank=sl_PB[bk])
                    p.emit("scalar", (lambda e, tt=tt, i4=i4, half=half, bk=bk:
                                      e.copy(
                        VT1[:, (tt * 4 + i4) * 1040 + half * 520:
                            (tt * 4 + i4) * 1040 + half * 520 + 520].rearrange(
                            "p (h n) -> p h n", n=65)[:, :, 0:64],
                        bank(bk).rearrange("p (h n) -> p h n", n=64))),
                        reads=[sl_PB[bk]], writes=[sl_VT1], partial=True)

        QF1, sl_QF1 = R1.view(33280, 0, 128, 4096, BF16, "QF1")
        sl_QF1c = []
        for dc in range(NDC):
            _, s_ = R1.view(33280 + dc * 1024, 0, 128, 512, BF16, f"QF1c{dc}")
            sl_QF1c.append(s_)
        CT1, sl_CT1 = R1.view(41472, 0, 128, 4096, BF16, "CT1")
        CTTv2, sl_CTT2 = [], []
        for s in range(2):
            a_, s_ = R1.view(49664 + s * 12288, 0, 128, 1024, BF16, f"CTT{s}")
            CTTv2.append(a_)
            sl_CTT2.append(s_)
        ws_ctr = [0]

        def stream_w(src_f32, stash_row, g):
            ws = ws_ctr[0] % 3
            q = "sync" if ws_ctr[0] % 2 == 0 else "gpsimd"
            ws_ctr[0] += 1
            dma(q, WSv[ws],
                w1st[stash_row, :].rearrange("(p n) -> p n", p=128),
                reads=[sl_w1st], writes=[sl_WS[ws]])
            return ws

        def ebcol(h):
            hg = h // 4
            return hg * 512 + (h % 2) * 256 + ((h % 4) // 2) * 128

        for g in range(4):
            gcol = g * 512
            for j in range(NDC):
                ws = stream_w(W1qT[j * 128:(j + 1) * 128, :], j, g)
                for dc in range(NDC):
                    mm(bank(dc), WSv[ws][:, dc * 128:(dc + 1) * 128],
                       HF[:, j * T + gcol:j * T + gcol + 512],
                       start=(j == 0), stop=False,
                       reads=[sl_WS[ws]] + hf_r(j, 4 * g, 4 * g + 4),
                       wbank=sl_PB[dc])
            dma("gpsimd", BIAh[0:1, 0:1024], B1q, writes=[sl_BIA], partial=True)
            for dc in range(NDC):
                mm(bank(dc), BIAh[0:1, dc * 128:(dc + 1) * 128], ONE_ROW512,
                   start=False, stop=True, reads=[sl_BIA], wbank=sl_PB[dc])
                p.emit("scalar", (lambda e, dc=dc: e.mul(
                    QF1[:, dc * 512:(dc + 1) * 512], bank(dc), SCALE)),
                    reads=[sl_PB[dc]], writes=[sl_QF1c[dc]], partial=False)
            def s3_scores(i4):
                icol = i4 * 128
                for hg in range(4):
                    be = (hg % 2) * 2
                    for hh in range(4):
                        h = hg * 4 + hh
                        hp, hlo = h // 2, (h % 2) * 64
                        bk = be + (h % 2)
                        mm(bank(bk, ((h % 4) // 2) * 128, 128),
                           KA[hlo:hlo + 64,
                              hp * 2048 + gcol + icol:hp * 2048 + gcol + icol + 128],
                           QF1[hlo:hlo + 64, hp * 512 + icol:hp * 512 + icol + 128],
                           start=True, stop=True,
                           reads=[sl_KAc[hp], sl_QF1c[hp]], wbank=sl_PB[bk])
                    for bh in range(2):
                        p.emit("scalar", (lambda e, hg=hg, be=be, bh=bh:
                                          e.activation(
                            EB1[:, hg * 512 + bh * 256:hg * 512 + bh * 256 + 256],
                            PS[:, (be + bh) * 512:(be + bh) * 512 + 256],
                            AF.Exp, bias=EXPB, scale=f32(1.0))),
                            reads=[sl_PB[be + bh]], writes=[sl_EB1g[hg]],
                            partial=(bh > 0))
                    p.emit("vector", (lambda e, hg=hg: e.tensor_tensor(
                        EB1[:, hg * 512:(hg + 1) * 512],
                        EB1[:, hg * 512:(hg + 1) * 512],
                        BDMASK, op=ALU.mult)),
                        reads=[sl_EB1g[hg], sl_CST], writes=[sl_EB1g[hg]],
                        partial=True)

            def avb(h):
                if h < 7:
                    return 4, h * 65
                if h < 14:
                    return 5, (h - 7) * 65
                return 6, (h - 14) * 65

            def s3_av(i4):
                for h in range(NH):
                    lhs = EB1[:, ebcol(h):ebcol(h) + 128]
                    bkx, off = avb(h)
                    mm(bank(bkx, off, 65), lhs,
                       VT1[:, (g * 4 + i4) * 1040 + h * 65:
                           (g * 4 + i4) * 1040 + (h + 1) * 65],
                       start=True, stop=True,
                       reads=[sl_EB1g[h // 4], sl_VT1], wbank=sl_PB[bkx])

            def s3_evac(i4):
                ctt = CTTv2[i4 % 2]
                sct = sl_CTT2[i4 % 2]
                p.emit("vector", (lambda e: e.reciprocal(
                    RS[:, 0:7], bank(4, 0, 455).rearrange(
                        "p (i n) -> p i n", n=65)[:, :, 64])),
                    reads=[sl_PB[4]], writes=[sl_RS], partial=True)
                p.emit("vector", (lambda e: e.reciprocal(
                    RS[:, 7:14], bank(5, 0, 455).rearrange(
                        "p (i n) -> p i n", n=65)[:, :, 64])),
                    reads=[sl_PB[5]], writes=[sl_RS], partial=True)
                p.emit("vector", (lambda e: e.reciprocal(
                    RS[:, 14:16], bank(6, 0, 130).rearrange(
                        "p (i n) -> p i n", n=65)[:, :, 64])),
                    reads=[sl_PB[6]], writes=[sl_RS], partial=True)
                for h in range(NH):
                    bkx, off = avb(h)
                    p.emit("vector", (lambda e, h=h, ctt=ctt, bkx=bkx, off=off:
                                      e.tensor_scalar(
                        ctt[:, h * 64:(h + 1) * 64],
                        bank(bkx, off, 64),
                        RS[:, h:h + 1], None, op0=ALU.mult)),
                        reads=[sl_PB[bkx], sl_RS], writes=[sct],
                        partial=(h > 0))

            def s3_tr(i4):
                icol = i4 * 128
                ctt = CTTv2[i4 % 2]
                sct = sl_CTT2[i4 % 2]
                for hp in range(NDC):
                    k = hp % 2
                    pslot = bank_bf(7, k * 128, 128)
                    tr(pslot, ctt[:, hp * 128:(hp + 1) * 128], IDENT,
                       reads=[sct], wbank=sl_PB[7])
                    if hp % 2 == 0:
                        scopy(CT1[:, hp * 512 + icol:hp * 512 + icol + 128],
                              pslot, reads=[sl_PB[7]], writes=[sl_CT1])
                    else:
                        vcopy(CT1[:, hp * 512 + icol:hp * 512 + icol + 128],
                              pslot, reads=[sl_PB[7]], writes=[sl_CT1])

            for i4 in range(4):
                s3_scores(i4)
                s3_av(i4)
                if i4 > 0:
                    s3_tr(i4 - 1)
                s3_evac(i4)
            s3_tr(3)
            for j in range(NDC):
                ws = stream_w(W1oT[j * 128:(j + 1) * 128, :], NDC + j, g)
                for i4 in range(4):
                    for half in range(2):
                        bk = i4 * 2 + half
                        mm(bank(bk), CT1[:, j * 512 + i4 * 128:
                                         j * 512 + (i4 + 1) * 128],
                           WSv[ws][:, half * 512:(half + 1) * 512],
                           start=(j == 0), stop=False,
                           reads=[sl_WS[ws], sl_CT1], wbank=sl_PB[bk])
            dma("gpsimd", BIAh[0:1, 0:1024], B1o, writes=[sl_BIA], partial=True)
            for i4 in range(4):
                ti = g * 4 + i4
                for half in range(2):
                    bk = i4 * 2 + half
                    mm(bank(bk), ONE_ROW128,
                       BIAh[0:1, half * 512:(half + 1) * 512],
                       start=False, stop=True, reads=[sl_BIA], wbank=sl_PB[bk])
                    p.emit("vector", (lambda e, ti=ti, half=half, bk=bk:
                                      e.scalar_tensor_tensor(
                        XBv[ti][:, half * 512:(half + 1) * 512], bank(bk),
                        1.0, XBv[ti][:, half * 512:(half + 1) * 512],
                        op0=ALU.mult, op1=ALU.add)),
                        reads=[sl_PB[bk], sl_XB[ti]], writes=[sl_XB[ti]],
                        partial=True)
                if n_stages >= 4:
                    emit_ln_stats(XBv, sl_XB, ti)

    if n_stages >= 4:
        HT4, sl_HT4 = emit_ln(XBv, sl_XB, "l1f", stats_emitted=True)
        WFM, sl_WFM = R1.view(0, 0, 8, T, BF16, "WFM")
        WBBv, sl_WBB = [], []
        for e_ in range(NE1):
            a, sl = R1.view(4096 + e_ * 4096, 0, 128, T, BF16, f"WBB{e_}")
            WBBv.append(a)
            sl_WBB.append(sl)
        XEv, sl_XE = [], []
        for s in range(2):
            a, sl = R1.view(36864 + s * 1024, 0, 128, 512, BF16, f"XE{s}")
            XEv.append(a)
            sl_XE.append(sl)
        OSTv, sl_OST = [], []
        for s in range(2):
            a, sl = R1.view(38912 + s * 2048, 0, 128, 512, F32, f"OST{s}")
            OSTv.append(a)
            sl_OST.append(sl)
        EXBs, sl_EXB = R1.view(55296, 0, 8, 1024, BF16, "EXB")
        WS4v, sl_WS4 = [], []
        for s in range(4):
            a, sl = R1.view(57344 + s * 2048, 0, 128, 1024, BF16, f"WS4_{s}")
            WS4v.append(a)
            sl_WS4.append(sl)
        dma("gpsimd", GWS[:].rearrange("p (j n) -> p j n", j=8),
            GW.rearrange("(j p) n -> p j n", j=8), writes=[sl_GWS])
        dma("gpsimd", BIAh[0:1, 1024:1032], GB, writes=[sl_BIA], partial=True)
        dma("gpsimd", EXBs[:], EB, writes=[sl_EXB])
        dma("sync", CSTh[0:8, 128:1152], SEL8D.ap(), writes=[sl_CST], partial=True)
        GBS = BIAh[0:1, 1024:1032]

        for tc in range(NT):
            gb4 = 6 + tc % 2
            for j in range(NDC):
                mm(bank(gb4, 0, 8), HF[:, j * T + tc * 128:j * T + (tc + 1) * 128],
                   GWS[:, j * 8:(j + 1) * 8],
                   start=(j == 0), stop=False,
                   reads=[sl_HFg[j][tc], sl_GWS], wbank=sl_PB[gb4])
            mm(bank(gb4, 0, 8), ONE_ROW128, GBS,
               start=False, stop=True, reads=[sl_BIA], wbank=sl_PB[gb4])
            G8 = GSCR[:, 0:8]
            NM = GSCR[:, 8:9]
            SM = GSCR[:, 9:10]
            RSM2 = GSCR[:, 10:11]
            MK8 = GSCR[:, 16:24]
            WSUM = GSCR[:, 24:25]
            p.emit("vector", (lambda e, gb4=gb4: e.tensor_scalar(
                GSCR[:, 32:40], bank(gb4, 0, 8), 1.0, -3e38, op0=ALU.mult,
                op1=ALU.max, accum_out=NM)),
                reads=[sl_PB[gb4]], writes=[sl_GSCR], partial=True)
            p.emit("vector", (lambda e: e.tensor_scalar(
                NM, NM, -1.0, None, op0=ALU.mult)),
                reads=[sl_GSCR], writes=[sl_GSCR], partial=True)
            p.emit("scalar", (lambda e, gb4=gb4: e.activation(
                G8, bank(gb4, 0, 8), AF.Exp, bias=NM, scale=f32(1.0),
                accum_out=SM)),
                reads=[sl_PB[gb4], sl_GSCR], writes=[sl_GSCR], partial=True)
            p.emit("vector", (lambda e: e.reciprocal(RSM2, SM)),
                   reads=[sl_GSCR], writes=[sl_GSCR], partial=True)
            p.emit("vector", (lambda e: e.tensor_scalar(
                G8, G8, RSM2, None, op0=ALU.mult)),
                reads=[sl_GSCR], writes=[sl_GSCR], partial=True)
            p.emit("vector", (lambda e: e.tensor_scalar(
                MK8, G8, 0.1, None, op0=ALU.is_ge)),
                reads=[sl_GSCR], writes=[sl_GSCR], partial=True)
            p.emit("vector", (lambda e: e.tensor_tensor(
                G8, G8, MK8, op=ALU.mult)),
                reads=[sl_GSCR], writes=[sl_GSCR], partial=True)
            p.emit("vector", (lambda e: e.tensor_scalar(
                GSCR[:, 32:40], G8, 1.0, 0.0, op0=ALU.mult,
                op1=ALU.add, accum_out=WSUM)),
                reads=[sl_GSCR], writes=[sl_GSCR], partial=True)
            p.emit("vector", (lambda e: e.reciprocal(RSM2, WSUM)),
                   reads=[sl_GSCR], writes=[sl_GSCR], partial=True)
            p.emit("vector", (lambda e, tc=tc: e.tensor_scalar(
                WTM[:, tc * 8:(tc + 1) * 8], G8, RSM2, None, op0=ALU.mult)),
                reads=[sl_GSCR], writes=[sl_WTM], partial=True)
        for tc in range(NT):
            k = tc % 2
            pslot = bank(6, k * 128, 128)
            bsl = sl_PB[6]
            tr(pslot[0:8, :], WTM[:, tc * 8:(tc + 1) * 8], CSTF[:, 0:128],
               reads=[sl_WTM], wbank=bsl, ident_slot=sl_CSTF)
            vcopy(WFM[0:8, tc * 128:(tc + 1) * 128], pslot[0:8, :],
                  reads=[bsl], writes=[sl_WFM])
        for e_ in range(NE1):
            for s in range(4):
                bk = 4 + (e_ * 4 + s) % 2
                mm(bank(bk), SEL8[:, e_ * 128:(e_ + 1) * 128],
                   WFM[0:8, s * 512:(s + 1) * 512],
                   start=True, stop=True,
                   reads=[sl_WFM, sl_CST], wbank=sl_PB[bk])
                p.emit("vector", (lambda e, e_=e_, s=s, bk=bk: e.tensor_copy(
                    WBBv[e_][:, s * 512:(s + 1) * 512], bank(bk))),
                    reads=[sl_PB[bk]], writes=[sl_WBB[e_]], partial=(s > 0))

        XOUTF = X_out
        for tt in range(4):
            tcol = tt * 512
            for dc in range(NDC):
                mm(bank(dc), EXBs[0:8, dc * 128:(dc + 1) * 128],
                   WFM[0:8, tcol:tcol + 512],
                   start=True, stop=False,
                   reads=[sl_EXB, sl_WFM], wbank=sl_PB[dc])
                for j2 in range(4):
                    ti = tt * 4 + j2
                    mm(bank(dc, j2 * 128, 128),
                       XBv[ti][:, dc * 128:(dc + 1) * 128], CSTF[:, 0:128],
                       start=False, stop=False,
                       reads=[sl_XB[ti], sl_CSTF], wbank=sl_PB[dc])
            for e_ in range(NE1):
                for j in range(NDC):
                    cix = e_ * NDC + j
                    ws = cix % 4
                    dma("sync" if cix % 2 == 0 else "gpsimd", WS4v[ws],
                        ewst_t.ap()[cix, :].rearrange("(p n) -> p n", p=128),
                        reads=[sl_ewst], writes=[sl_WS4[ws]])
                    xe = XEv[(e_ * NDC + j) % 2]
                    p.emit("vector", (lambda e, j=j, e_=e_, tcol=tcol, xe=xe:
                                      e.tensor_tensor(
                        xe, HF[:, j * T + tcol:j * T + tcol + 512],
                        WBBv[e_][:, tcol:tcol + 512], op=ALU.mult)),
                        reads=hf_r(j, 4 * tt, 4 * tt + 4) + [sl_WBB[e_]],
                        writes=[sl_XE[(e_ * NDC + j) % 2]])
                    last = (e_ == NE1 - 1 and j == NDC - 1)
                    for dc in range(NDC):
                        mm(bank(dc), WS4v[ws][:, dc * 128:(dc + 1) * 128], xe,
                           start=False, stop=last,
                           reads=[sl_WS4[ws], sl_XE[(e_ * NDC + j) % 2]],
                           wbank=sl_PB[dc])
            for dc in range(NDC):
                sslot = (tt * NDC + dc) % 2
                if dc % 2 == 0:
                    vcopy(OSTv[sslot][:], bank(dc),
                          reads=[sl_PB[dc]], writes=[sl_OST[sslot]],
                          partial=False)
                else:
                    scopy(OSTv[sslot][:], bank(dc),
                          reads=[sl_PB[dc]], writes=[sl_OST[sslot]],
                          partial=False)
                dma("sync",
                    XOUTF[dc * 128:(dc + 1) * 128, tt * 512:(tt + 1) * 512],
                    OSTv[sslot][:],
                    reads=[sl_OST[sslot]], writes=())

    if n_stages <= 2:
        for i in range(NT):
            b_, half = i // 2, i % 2
            dma("sync", X_out[b_, half * 128:(half + 1) * 128, :], XAv[i],
                reads=[sl_XA[i]], writes=())
    elif n_stages == 3:
        XOT = X_out.rearrange("b t c -> t b c")
        for i2 in range(NT):
            dma("sync", XOT[i2 * 16:(i2 + 1) * 16, :, :], XBv[i2],
                reads=[sl_XB[i2]], writes=())
    p.barrier_wait("sync", {"sync_q": p.cnt["sync_q"]})

    with nc.Block() as block:
        @block.tensor
        def _(t):
            for f in p.ops["tensor"]:
                f(t)

        @block.vector
        def _(v):
            for f in p.ops["vector"]:
                f(v)

        @block.scalar
        def _(s):
            for f in p.ops["scalar"]:
                f(s)

        @block.gpsimd
        def _(g):
            for f in p.ops["gpsimd"]:
                f(g)

        @block.sync
        def _(sy):
            for f in p.ops["sync"]:
                f(sy)

    return nc


def _stage_inputs(inputs, core):
    A = np.ascontiguousarray

    def idxs():
        return [h * 192 + k for h in range(NH) for k in range(HD)]

    x = inputs["L0_attn_x"]
    qi = [h * 192 + d_ for h in range(NH) for d_ in range(HD)]
    ki = [h * 192 + 64 + d_ for h in range(NH) for d_ in range(HD)]
    vi = [h * 192 + 128 + d_ for h in range(NH) for d_ in range(HD)]
    ipw = inputs["L1_attn_in_proj_weight"]
    ipb = inputs["L1_attn_in_proj_bias"]
    kvT = A(inputs["L0_attn_kv_w"].T)
    d = {
        "x": A(x[:, core * TLOC:(core + 1) * TLOC, :]),
        "w0qT": A(inputs["L0_attn_q_w"].T),
        "w0kkT": A(np.concatenate([kvT[:, 0:64], kvT[:, 0:64]], axis=1)),
        "w0vT": A(kvT[:, 64:128]),
        "w0pT": A(inputs["L0_attn_proj_w"].T),
        "b0p": A(inputs["L0_attn_proj_b"][None, :]),
        "sim": A(inputs["L0_ffn_sim_matrix"]),
        "gates": A(inputs["L0_ffn_gates"][None, :]),
        "wqkv0": A(np.concatenate(
            [np.concatenate([inputs["L0_ffn_q_proj"][e],
                             inputs["L0_ffn_k_proj"][e],
                             inputs["L0_ffn_v_proj"][e]], axis=1)
             for e in range(NE0)], axis=1)),
        "wo0": A(inputs["L0_ffn_o_proj"].reshape(NE0 * HD, D)),
        "w1qT": A(ipw[qi, :].T),
        "w1kT": A(ipw[ki, :].T),
        "w1vT": A(ipw[vi, :].T),
        "b1q": A(ipb[qi][None, :]),
        "b1k": A(ipb[ki][None, :]),
        "b1v": A(ipb[vi][None, :]),
        "w1oT": A(inputs["L1_attn_out_proj_weight"].T),
        "b1o": A(inputs["L1_attn_out_proj_bias"][None, :]),
        "gw": A(inputs["L1_ffn_gate_w"]),
        "gb": A(inputs["L1_ffn_gate_b"][None, :]),
        "ew": A(inputs["L1_ffn_expert_w"]),
        "eb": A(inputs["L1_ffn_expert_b"]),
    }
    return {k: np.asarray(v, dtype=np.float32) for k, v in d.items()}


_NC_CACHE = {}


def kernel(**inputs):
    key = N_STAGES
    if key not in _NC_CACHE:
        _NC_CACHE[key] = build_program(N_STAGES)
    nc = _NC_CACHE[key]
    in_maps = [_stage_inputs(inputs, c) for c in range(NCORES)]
    res = run_bass_kernel_spmd(nc, in_maps, core_ids=list(range(NCORES)))
    if N_STAGES >= 4:
        shards = [np.asarray(res.results[c]["out"])
                  .reshape(D, TLOC, B).transpose(2, 1, 0)
                  for c in range(NCORES)]
    else:
        shards = [np.asarray(res.results[c]["out"]) for c in range(NCORES)]
    out = np.concatenate(shards, axis=1)
    return out.astype(np.float32)


if __name__ == "__main__":
    nc = build_program(1)
    print("build ok")


# version 27
# speedup vs baseline: 1.0100x; 1.0100x over previous
import numpy as np
import ml_dtypes
import concourse.bass as bass
import concourse.mybir as mybir
from concourse.bass_utils import run_bass_kernel_spmd
from concourse.aot_env import allow_ici_all_to_all

F32 = mybir.dt.float32
BF16 = mybir.dt.bfloat16
AF = mybir.ActivationFunctionType
ALU = mybir.AluOpType
AX = mybir.AxisListType
BF = ml_dtypes.bfloat16
f32 = np.float32

NCORES = 8
B = 8
TLOC = 256
T = 2048
D = 1024
NT = 16
NDC = 8
HD = 64
NH = 16
NE0 = 16
NE1 = 8
EPS = 1e-5
EXPB = f32(-14.0)
SCALE = f32(0.125)

N_STAGES = 4


class Slot:
    __slots__ = ("w", "we", "r", "name")

    def __init__(self, name=""):
        self.w = {}
        self.we = {}
        self.r = {}
        self.name = name


def _merge(dst, src):
    for s, v in src.items():
        if v > dst.get(s, 0):
            dst[s] = v


class Prog:
    CENG = {"tensor": "tensor", "vector": "vector", "scalar": "scalar",
            "gpsimd": "gpsimd"}

    def __init__(self, nc):
        self.nc = nc
        self.ops = {e: [] for e in ("tensor", "vector", "scalar", "gpsimd", "sync")}
        self.sem = {}
        self.cnt = {}
        for s in ("tensor", "vector", "scalar", "gpsimd", "sync_q", "gp_q",
                  "cc", "w3c", "w1c", "ewc"):
            self.sem[s] = nc.alloc_semaphore("S_" + s)
            self.cnt[s] = 0
        self.hwm = {e: {} for e in self.ops}

    def emit(self, eng, fn, reads=(), writes=(), sem=None, inc=1, partial=False,
             extra=()):
        if sem is None:
            sem = self.CENG[eng]
        rdeps = {}
        for sl in reads:
            _merge(rdeps, sl.w)
        wdeps = {}
        ownsem = self.CENG.get(eng) if eng != "sync" else sem
        for sl in writes:
            _merge(wdeps, sl.r)
            if partial:
                _merge(wdeps, sl.we)
                if ownsem in sl.w and sl.w[ownsem] > wdeps.get(ownsem, 0):
                    wdeps[ownsem] = sl.w[ownsem]
            else:
                _merge(wdeps, sl.w)
        forced = []
        for s, v in extra:
            if s == "tensor" and eng == "tensor":
                if v > 0 and self.hwm[eng].get("tensor!", 0) < v:
                    forced.append((self.sem[s], v))
                    self.hwm[eng]["tensor!"] = v
            elif v > wdeps.get(s, 0):
                wdeps[s] = v
        own = self.CENG.get(eng)
        deps = {}
        _merge(deps, wdeps)
        _merge(deps, rdeps)
        waits = list(forced)
        hw = self.hwm[eng]
        for s, v in sorted(deps.items()):
            if v <= 0:
                continue
            if s == own and eng == "tensor":
                continue
            if hw.get(s, 0) >= v:
                continue
            waits.append((self.sem[s], v))
            hw[s] = v
        self.cnt[sem] += inc
        val = self.cnt[sem]
        semh = self.sem[sem]

        def run(e, fn=fn, waits=tuple(waits), semh=semh, inc=inc):
            for h, v in waits:
                e.wait_ge(h, v)
            fn(e).then_inc(semh, inc)
        self.ops[eng].append(run)
        for sl in reads:
            sl.r[sem] = max(sl.r.get(sem, 0), val)
        for sl in writes:
            if partial:
                sl.w[sem] = max(sl.w.get(sem, 0), val)
            else:
                sl.w = {sem: val}
                sl.we = {sem: val}
                sl.r = {}
        return (sem, val)

    def barrier_wait(self, eng, deps):
        own = self.CENG.get(eng)
        waits = []
        hw = self.hwm[eng]
        for s, v in sorted(deps.items()):
            if v <= 0 or s == own or hw.get(s, 0) >= v:
                continue
            waits.append((self.sem[s], v))
            hw[s] = v
        if waits:
            def run(e, waits=tuple(waits)):
                for h, v in waits:
                    e.wait_ge(h, v)
            self.ops[eng].append(run)


class Region:

    def __init__(self, handle_ap):
        self.h = handle_ap
        self.live = []

    def view(self, off_bytes, p0, pn, nel, dtype, name=""):
        esz = 2 if dtype == BF16 else 4
        lo, hi = off_bytes, off_bytes + nel * esz
        assert hi <= 65536, (name, lo, hi)
        sl = Slot(name)
        for (a, b_, old) in self.live:
            if a < hi and lo < b_:
                _merge(sl.w, old.w)
                _merge(sl.we, old.we)
                _merge(sl.r, old.r)
                _merge(sl.w, old.r)
                _merge(sl.we, old.r)
        self.live.append((lo, hi, sl))
        ap = self.h[p0:p0 + pn, lo // 2: hi // 2]
        if dtype != BF16:
            ap = ap.bitcast(dtype)
        return ap, sl


def build_program(n_stages=N_STAGES):
    nc = bass.Bass("TRN2", target_bir_lowering=False)
    p = Prog(nc)

    def din(name, shape):
        return nc.dram_tensor(name, list(shape), F32, kind="ExternalInput").ap()

    X_in = din("x", (B, TLOC, D))
    W0qT = din("w0qT", (D, D))
    W0kkT = din("w0kkT", (D, 2 * HD))
    W0vT = din("w0vT", (D, HD))
    W0pT = din("w0pT", (D, D))
    B0p = din("b0p", (1, D))
    SIM = din("sim", (D, NE0))
    GATES = din("gates", (1, NE0))
    WQKV0 = din("wqkv0", (D, NE0 * 3 * HD))
    WO0 = din("wo0", (NE0 * HD, D))
    W1qT = din("w1qT", (D, D))
    W1kT = din("w1kT", (D, D))
    W1vT = din("w1vT", (D, D))
    B1q = din("b1q", (1, D))
    B1k = din("b1k", (1, D))
    B1v = din("b1v", (1, D))
    W1oT = din("w1oT", (D, D))
    B1o = din("b1o", (1, D))
    GW = din("gw", (D, NE1))
    GB = din("gb", (1, NE1))
    EW = din("ew", (NE1, D, D))
    EB = din("eb", (NE1, D))
    if n_stages >= 4:
        X_out = nc.dram_tensor("out", [D, TLOC * B], F32,
                               kind="ExternalOutput").ap()
    else:
        X_out = nc.dram_tensor("out", [B, TLOC, D], F32,
                               kind="ExternalOutput").ap()

    a2aI_t = [nc.dram_tensor(f"a2a{h}_in", [NCORES, 3 * 128 * HD], BF16)
              for h in range(2)]
    a2aO_t = [nc.dram_tensor(f"a2a{h}_out", [NCORES, 3 * 128 * HD], BF16)
              for h in range(2)]
    c2aI_t = [nc.dram_tensor(f"c2a{h}_in", [NCORES, 65 * 128], BF16)
              for h in range(2)]
    c2aO_t = [nc.dram_tensor(f"c2a{h}_out", [NCORES, 65 * 128], BF16)
              for h in range(2)]
    ewst_t = nc.dram_tensor("ewst", [NE1 * NDC, 128 * D], BF16)
    w1st_t = nc.dram_tensor("w1st", [4 * NDC, 128 * D], BF16)
    w3st_t = nc.dram_tensor("w3st", [NDC, 128 * 3 * D], BF16)
    sl_a2aih = [Slot(f"a2ai{h}") for h in range(2)]
    sl_a2aoh = [Slot(f"a2ao{h}") for h in range(2)]
    sl_c2aih = [Slot(f"c2ai{h}") for h in range(2)]
    sl_c2aoh = [Slot(f"c2ao{h}") for h in range(2)]
    sl_ewst = Slot("ewst")
    sl_w1st = Slot("w1st")
    sl_w3st = Slot("w3st")

    ident_bf = nc.inline_tensor(np.eye(128, dtype=BF), "ident_bf")
    ident_f = nc.inline_tensor(np.eye(128, dtype=np.float32), "ident_f")
    ones_row = nc.inline_tensor(np.ones((1, 512), dtype=BF), "ones_row")
    ones_col = nc.inline_tensor(np.ones((128, 1), dtype=BF), "ones_col")
    selp_np = np.zeros((NE0, 8 * 128), dtype=BF)
    for j in range(8):
        selp_np[2 * j, j * 128: j * 128 + 64] = 1.0
        selp_np[2 * j + 1, j * 128 + 64: j * 128 + 128] = 1.0
    SELPD = nc.inline_tensor(selp_np, "selp")
    sel8_np = np.zeros((NE1, 8 * 128), dtype=BF)
    for e in range(NE1):
        sel8_np[e, e * 128:(e + 1) * 128] = 1.0
    SEL8D = nc.inline_tensor(sel8_np, "sel8")
    bd = np.zeros((128, 128), dtype=BF)
    for g in range(16):
        bd[g * 8:(g + 1) * 8, g * 8:(g + 1) * 8] = 1.0
    BDMASKD = nc.inline_tensor(np.tile(bd, (1, 4)), "bdmask4")
    TRILD = nc.inline_tensor(np.triu(np.ones((128, 128), dtype=np.float32))
                             .astype(BF), "triu128")

    R1h = nc.alloc_sbuf_tensor("R1", [128, 32768], BF16)
    R2h = nc.alloc_sbuf_tensor("R2", [128, 32768], BF16)
    HTRh = nc.alloc_sbuf_tensor("HTR", [128, 16384], BF16)
    HFh = nc.alloc_sbuf_tensor("HFb", [128, 16384], BF16)
    STATh = nc.alloc_sbuf_tensor("STAT", [128, 96], F32)
    JNKVh = nc.alloc_sbuf_tensor("JNKV", [128, 1024], BF16)
    JNKSh = nc.alloc_sbuf_tensor("JNKS", [128, 1024], BF16)
    GSCRh = nc.alloc_sbuf_tensor("GSCR", [128, 160], F32)
    RSh = nc.alloc_sbuf_tensor("RSc", [128, 64], F32)
    RWh = nc.alloc_sbuf_tensor("RWg", [128, NT * NE0], F32)
    WTMh = nc.alloc_sbuf_tensor("WTM", [128, NT * NE1], F32)
    QKVAh = nc.alloc_sbuf_tensor("QKVA", [128, 3 * HD], F32)
    CSTh = nc.alloc_sbuf_tensor("CSTc", [128, 1824], BF16)
    CSTFh = nc.alloc_sbuf_tensor("CSTF", [128, 128], F32)
    BIAh = nc.alloc_sbuf_tensor("BIAc", [1, 1664], BF16)
    SIMBh = nc.alloc_sbuf_tensor("SIMB", [128, NDC * NE0], BF16)
    GWSh = nc.alloc_sbuf_tensor("GWS", [128, NDC * NE1], BF16)
    STT2h = nc.alloc_sbuf_tensor("STT2", [128, 64], F32)

    R1 = Region(R1h[:])
    R2 = Region(R2h[:])
    HTR = Region(HTRh[:])
    HF = HFh[:]
    sl_HF = Slot("HF")
    sl_HFg = [[Slot(f"HF{j}_{i}") for i in range(NT)] for j in range(NDC)]

    def hf_r(j, i0, i1):
        return sl_HFg[j][i0:i1]
    STAT = STATh[:]
    sl_STAT = Slot("STAT")
    JNKV = JNKVh[:]
    sl_JNKV = Slot("JNKV")
    JNKS = JNKSh[:]
    sl_JNKS = Slot("JNKS")
    GSCR = GSCRh[:]
    sl_GSCR = Slot("GSCR")
    RS = RSh[:]
    sl_RS = Slot("RS")
    RW = RWh[:]
    sl_RW = Slot("RW")
    WTM = WTMh[:]
    sl_WTM = Slot("WTM")
    QKVA = QKVAh[:]
    sl_QKVA = Slot("QKVA")
    CST = CSTh[:]
    sl_CST = Slot("CST")
    CSTF = CSTFh[:]
    sl_CSTF = Slot("CSTF")
    BIA = BIAh[:]
    sl_BIA = Slot("BIA")
    SIMB = SIMBh[:]
    sl_SIMB = Slot("SIMB")
    GWS = GWSh[:]
    sl_GWS = Slot("GWS")
    STT2 = STT2h[:]
    sl_STT2 = Slot("STT2")

    PSh = nc.alloc_psum_tensor("PS", [128, 4096], F32)
    PS = PSh[:]
    sl_PB = [Slot(f"pb{i}") for i in range(8)]

    def bank(i, off=0, w=512):
        return PS[:, i * 512 + off:i * 512 + off + w]

    def bank_bf(i, off_bf, w_bf):
        return PS[:, i * 512 + off_bf // 2:i * 512 + (off_bf + w_bf) // 2].bitcast(BF16)

    IDENT = CST[:, 0:128]
    SELP = CSTh[0:16, 128:1152]
    SEL8 = CSTh[0:8, 128:1152]
    BDMASK = CST[:, 1152:1664]
    ONE_COL = CST[:, 1664:1665]
    TRIL = CST[:, 1696:1824]
    ONE_ROW512 = BIAh[0:1, 1152:1664]
    ONE_ROW128 = BIAh[0:1, 1152:1280]

    def dma(queue, out_ap, in_ap, reads=(), writes=(), partial=False):
        eng, sem = {"sync": ("sync", "sync_q"), "gpsimd": ("gpsimd", "gp_q")}[queue]
        prev = p.cnt[sem]
        return p.emit(eng, (lambda e, o=out_ap, i=in_ap: e.dma_start(out=o, in_=i)),
                      reads=reads, writes=writes, sem=sem, inc=16, partial=partial,
                      extra=[(sem, prev)])

    def dma_cast(out_ap, in_ap, sem, reads=(), writes=(), partial=True):
        thr = p.cnt[sem] if (p.cnt[sem] % 128 == 0) else 0
        return p.emit("gpsimd",
                      (lambda e, o=out_ap, i=in_ap: e.dma_start(out=o, in_=i)),
                      reads=reads, writes=writes, sem=sem, inc=16,
                      partial=partial, extra=[(sem, thr)])

    def mm(out_ap, lhsT, rhs, start, stop, reads, wbank):
        return p.emit("tensor",
                      (lambda e, o=out_ap, l=lhsT, r=rhs, st=start, sp=stop:
                       e.matmul(o, l, r, start=st, stop=sp)),
                      reads=reads, writes=[wbank], partial=not start)

    def tr(out_ap, in_ap, ident_ap, reads, wbank, ident_slot=None, extra=()):
        rd = list(reads) + ([ident_slot] if ident_slot else [sl_CST])
        return p.emit("tensor",
                      (lambda e, o=out_ap, i=in_ap, d=ident_ap: e.transpose(o, i, d)),
                      reads=rd, writes=[wbank], extra=extra)

    def vcopy(out_ap, in_ap, reads, writes, partial=True):
        return p.emit("vector", (lambda e, o=out_ap, i=in_ap: e.tensor_copy(o, i)),
                      reads=reads, writes=writes, partial=partial)

    def scopy(out_ap, in_ap, reads, writes, partial=True):
        return p.emit("scalar", (lambda e, o=out_ap, i=in_ap: e.copy(o, i)),
                      reads=reads, writes=writes, partial=partial)

    XAv, sl_XA = [], []
    for i in range(NT):
        a, s = R1.view(i * 4096, 0, 128, D, F32, f"XA{i}")
        XAv.append(a)
        sl_XA.append(s)

    p.emit("vector", (lambda e: e.memset(STAT[:, 0:4], 1.0)),
           reads=(), writes=[sl_STAT], partial=True)
    p.emit("vector", (lambda e: e.reciprocal(STAT[:, 4:8], STAT[:, 0:4])),
           reads=[sl_STAT], writes=[sl_STAT], partial=True)
    p.emit("vector", (lambda e: e.reciprocal(STAT[:, 4:8], STAT[:, 0:4])),
           reads=[sl_STAT], writes=[sl_STAT], partial=True)
    p.emit("vector", (lambda e: e.tensor_scalar(
        JNKV[:, 0:4], STAT[:, 0:4],
        1.0, 0.0, op0=ALU.mult, op1=ALU.add, accum_out=STAT[:, 8:9])),
        reads=[sl_STAT], writes=[sl_STAT, sl_JNKV], partial=True)
    p.emit("scalar", (lambda e: e.activation(
        JNKS[:, 0:4], STAT[:, 0:4], AF.Exp, bias=f32(-2.0), scale=f32(0.1),
        accum_out=STAT[:, 9:10])),
        reads=[sl_STAT], writes=[sl_STAT, sl_JNKS], partial=True)
    p.emit("scalar", (lambda e: e.activation(
        JNKS[:, 0:4], STAT[:, 0:4], AF.Ln, bias=f32(2.0), scale=f32(0.1))),
        reads=[sl_STAT], writes=[sl_JNKS], partial=True)

    def emit_ln_stats(Xv, sl_Xv, i):
        p.emit("vector", (lambda e, i=i: e.tensor_scalar(
            JNKV[:], Xv[i], 1.0, 0.0, op0=ALU.mult, op1=ALU.add,
            accum_out=STAT[:, i:i + 1])),
            reads=[sl_Xv[i]], writes=[sl_STAT, sl_JNKV], partial=True)
        p.emit("scalar", (lambda e, i=i: e.activation(
            JNKS[:], Xv[i], AF.Square, bias=f32(0.0), scale=f32(1.0),
            accum_out=STAT[:, 16 + i:17 + i])),
            reads=[sl_Xv[i]], writes=[sl_STAT, sl_JNKS], partial=True)

    def emit_ln(Xv, sl_Xv, tag, stats_emitted=False):
        HT, _slHT = HTR.view(0, 0, 128, NT * D, BF16, "HT_" + tag)
        sl_HTt = []
        for i in range(NT):
            _, s_ = HTR.view(i * 2048, 0, 128, D, BF16, f"HTt_{tag}_{i}")
            sl_HTt.append(s_)
        if not stats_emitted:
            for i in range(NT):
                emit_ln_stats(Xv, sl_Xv, i)
        for g4 in range(4):
            t0 = g4 * 4
            p.emit("vector", (lambda e, t0=t0: e.tensor_scalar(
                STAT[:, 32 + t0:36 + t0], STAT[:, t0:t0 + 4], 1.0 / D, None,
                op0=ALU.mult)),
                reads=[sl_STAT], writes=[sl_STAT], partial=True)
            p.emit("vector", (lambda e, t0=t0: e.tensor_tensor(
                STAT[:, 80 + t0:84 + t0], STAT[:, 32 + t0:36 + t0],
                STAT[:, 32 + t0:36 + t0], op=ALU.mult)),
                reads=[sl_STAT], writes=[sl_STAT], partial=True)
            p.emit("vector", (lambda e, t0=t0: e.scalar_tensor_tensor(
                STAT[:, 48 + t0:52 + t0], STAT[:, 16 + t0:20 + t0], 1.0 / D,
                STAT[:, 80 + t0:84 + t0], op0=ALU.mult, op1=ALU.subtract)),
                reads=[sl_STAT], writes=[sl_STAT], partial=True)
            p.emit("scalar", (lambda e, t0=t0: e.activation(
                STAT[:, 64 + t0:68 + t0], STAT[:, 48 + t0:52 + t0], AF.Ln,
                bias=f32(EPS), scale=f32(1.0))),
                reads=[sl_STAT], writes=[sl_STAT], partial=True)
            p.emit("scalar", (lambda e, t0=t0: e.activation(
                STAT[:, 64 + t0:68 + t0], STAT[:, 64 + t0:68 + t0], AF.Exp,
                bias=f32(0.0), scale=f32(-0.5))),
                reads=[sl_STAT], writes=[sl_STAT], partial=True)
        lnps = [Slot(f"lnps{k}") for k in range(16)]
        for k in range(16):
            b_sl = sl_PB[4 + k // 4]
            _merge(lnps[k].w, b_sl.w)
            _merge(lnps[k].we, b_sl.we)
            _merge(lnps[k].r, b_sl.r)
        drain = [("tensor", p.cnt["tensor"])]
        for i in range(NT):
            p.emit("vector", (lambda e, i=i, HT=HT: e.tensor_scalar(
                HT[:, i * D:(i + 1) * D], Xv[i],
                STAT[:, 32 + i:33 + i], STAT[:, 64 + i:65 + i],
                op0=ALU.subtract, op1=ALU.mult)),
                reads=[sl_Xv[i], sl_STAT], writes=[sl_HTt[i]], partial=False)
            for j in range(NDC):
                k = (i * NDC + j) % 16
                pslot = bank_bf(4 + k // 4, (k % 4) * 128, 128)
                bsl = sl_PB[4 + k // 4]
                tr(pslot, HT[:, i * D + j * 128:i * D + (j + 1) * 128], IDENT,
                   reads=[sl_HTt[i]], wbank=bsl,
                   extra=(drain if (i == 0 and j == 0) else ()))
                dst = HF[:, j * T + i * 128:j * T + (i + 1) * 128]
                if j < 3:
                    vcopy(dst, pslot, reads=[bsl], writes=[sl_HFg[j][i]],
                          partial=False)
                else:
                    scopy(dst, pslot, reads=[bsl], writes=[sl_HFg[j][i]],
                          partial=False)
        for k in range(16):
            b_sl = sl_PB[4 + k // 4]
            _merge(b_sl.w, lnps[k].w)
            _merge(b_sl.we, lnps[k].we)
            _merge(b_sl.r, lnps[k].r)
        return HT, sl_HTt

    for i in range(NT):
        b_, half = i // 2, i % 2
        dma("sync", XAv[i], X_in[b_, half * 128:(half + 1) * 128, :],
            writes=[sl_XA[i]])
        emit_ln_stats(XAv, sl_XA, i)
    dma("sync", CST[:, 0:128], ident_bf.ap(), writes=[sl_CST], partial=True)
    dma("sync", CSTh[0:16, 128:1152], SELPD.ap(), writes=[sl_CST], partial=True)
    dma("sync", CST[:, 1152:1664], BDMASKD.ap(), writes=[sl_CST], partial=True)
    dma("sync", CST[:, 1664:1665], ones_col.ap(), writes=[sl_CST], partial=True)
    dma("sync", CST[:, 1696:1824], TRILD.ap(), writes=[sl_CST], partial=True)
    dma("sync", CSTF[:, 0:128], ident_f.ap(), writes=[sl_CSTF])
    dma("sync", BIAh[0:1, 1152:1664], ones_row.ap(), writes=[sl_BIA], partial=True)

    HT1, sl_HT1 = emit_ln(XAv, sl_XA, "l0a", stats_emitted=True)
    if n_stages == 0:
        for i in range(NT):
            p.emit("vector", (lambda e, i=i: e.tensor_copy(
                XAv[i], HT1[:, i * D:(i + 1) * D])),
                reads=[sl_HT1[i]], writes=[sl_XA[i]])
        p.emit("vector", (lambda e: e.tensor_copy(
            XAv[15][:, 900:996], STAT[:])),
            reads=[sl_STAT], writes=[sl_XA[15]], partial=True)
    if n_stages == -1:
        pass
    if n_stages <= 0:
        for i in range(NT):
            b_, half = i // 2, i % 2
            dma("sync", X_out[b_, half * 128:(half + 1) * 128, :], XAv[i],
                reads=[sl_XA[i]], writes=())
        p.barrier_wait("sync", {"sync_q": p.cnt["sync_q"]})
        with nc.Block() as block:
            @block.tensor
            def _(t):
                for f in p.ops["tensor"]:
                    f(t)

            @block.vector
            def _(v):
                for f in p.ops["vector"]:
                    f(v)

            @block.scalar
            def _(s):
                for f in p.ops["scalar"]:
                    f(s)

            @block.gpsimd
            def _(g):
                for f in p.ops["gpsimd"]:
                    f(g)

            @block.sync
            def _(sy):
                for f in p.ops["sync"]:
                    f(sy)
        return nc

    WQTc, sl_WQTc, WPTc, sl_WPTc = [], [], [], []
    for j in range(NDC):
        a_, s_ = HTR.view(j * 2048, 0, 128, 1024, BF16, f"WQT{j}")
        WQTc.append(a_)
        sl_WQTc.append(s_)
    for j in range(NDC):
        a_, s_ = HTR.view(16384 + j * 2048, 0, 128, 1024, BF16, f"WPT{j}")
        WPTc.append(a_)
        sl_WPTc.append(s_)
    QWv, sl_QW = [], []
    for s in range(2):
        a, sl = R2.view(s * 4096, 0, 128, T, BF16, f"QW{s}")
        QWv.append(a)
        sl_QW.append(sl)
    KKF, sl_KKF = R2.view(8192, 0, 128, T, BF16, "KKF")
    VF, sl_VF = R2.view(12288, 0, 64, T, BF16, "VF")
    VT0, sl_VT0 = R2.view(16384, 0, 128, NT * 65, BF16, "VT0")
    ETv, sl_ET = [], []
    for s in range(2):
        a, sl = R2.view(18464 + s * 2048, 0, 128, 1024, BF16, f"ET{s}")
        ETv.append(a)
        sl_ET.append(sl)
    CTMv, sl_CTM = [], []
    for s in range(2):
        a_, s_ = R2.view(22560 if s == 0 else 58432, 0, 128, 256, BF16,
                         f"CTM{s}")
        CTMv.append(a_)
        sl_CTM.append(s_)
    KKW, sl_KKW = R2.view(23072, 0, 128, 1024, BF16, "KKW")
    KVW2, sl_KVW2 = R2.view(25120, 0, 128, 512, BF16, "KVW2")
    CTXF, sl_CTXF = R2.view(25632, 0, 128, NDC * T, BF16, "CTXF")
    p.emit("vector", (lambda e: e.memset(
        VT0[:].rearrange("p (i n) -> p i n", n=65)[:, :, 64], 1.0)),
        reads=(), writes=[sl_VT0], partial=False)
    dma("gpsimd", KKW[:].rearrange("p (j n) -> p j n", j=8),
        W0kkT.rearrange("(j p) n -> p j n", j=8), writes=[sl_KKW])
    dma("gpsimd", KVW2[:].rearrange("p (j n) -> p j n", j=8),
        W0vT.rearrange("(j p) n -> p j n", j=8), writes=[sl_KVW2])
    dma("gpsimd", BIAh[0:1, 0:1024], B0p, writes=[sl_BIA], partial=True)
    B0P_S = BIAh[0:1, 0:1024]
    for j in range(NDC):
        dma("gpsimd", WQTc[j][:], W0qT[j * 128:(j + 1) * 128, :],
            writes=[sl_WQTc[j]])
    for j in range(NDC):
        dma("gpsimd", WPTc[j][:], W0pT[j * 128:(j + 1) * 128, :],
            writes=[sl_WPTc[j]])
    w3st = w3st_t.ap()
    w1stA = w1st_t.ap()
    if n_stages >= 2:
        for j in range(NDC):
            dma_cast(w3st[j, :], WQKV0[j * 128:(j + 1) * 128, :], "w3c",
                     writes=[sl_w3st], partial=(j > 0))

    for tt in range(4):
        bk = 4
        for j in range(NDC):
            mm(bank(bk), KKW[:, j * 128:(j + 1) * 128],
               HF[:, j * T + tt * 512:j * T + (tt + 1) * 512],
               start=(j == 0), stop=(j == NDC - 1),
               reads=[sl_KKW] + hf_r(j, 4 * tt, 4 * tt + 4), wbank=sl_PB[bk])
        vcopy(KKF[:, tt * 512:(tt + 1) * 512], bank(bk),
              reads=[sl_PB[bk]], writes=[sl_KKF])
    for tt in range(4):
        bk = 5
        for j in range(NDC):
            mm(bank(bk)[0:64, :], KVW2[:, j * 64:(j + 1) * 64],
               HF[:, j * T + tt * 512:j * T + (tt + 1) * 512],
               start=(j == 0), stop=(j == NDC - 1),
               reads=[sl_KVW2] + hf_r(j, 4 * tt, 4 * tt + 4), wbank=sl_PB[bk])
        vcopy(VF[:, tt * 512:(tt + 1) * 512], bank(bk)[0:64, :],
              reads=[sl_PB[bk]], writes=[sl_VF])
    for i in range(NT):
        k = i % 4
        pslot = bank_bf(6 + k // 2, (k % 2) * 128, 64)
        bsl = sl_PB[6 + k // 2]
        tr(pslot, VF[0:64, i * 128:(i + 1) * 128], IDENT[0:64, 0:64],
           reads=[sl_VF], wbank=bsl)
        vcopy(VT0[:, i * 65:i * 65 + 64], pslot,
              reads=[bsl], writes=[sl_VT0])

    for hp in range(NDC):
        qs = hp % 2
        qsl = QWv[qs]
        for tt in range(4):
            bk = tt % 2
            for j in range(NDC):
                mm(bank(bk), WQTc[j][:, hp * 128:(hp + 1) * 128],
                   HF[:, j * T + tt * 512:j * T + (tt + 1) * 512],
                   start=(j == 0), stop=(j == NDC - 1),
                   reads=[sl_WQTc[j]] + hf_r(j, 4 * tt, 4 * tt + 4),
                   wbank=sl_PB[bk])
            p.emit("scalar", (lambda e, qsl=qsl, tt=tt, bk=bk: e.mul(
                qsl[:, tt * 512:(tt + 1) * 512], bank(bk), SCALE)),
                reads=[sl_PB[bk]], writes=[sl_QW[qs]], partial=(tt > 0))
        def l0_scores(b_):
            es = b_ % 2
            sb0 = 2 + 3 * es
            for h2 in range(2):
                for tkc in range(2):
                    out = PS[:, (sb0 + h2) * 512 + tkc * 256:
                             (sb0 + h2) * 512 + tkc * 256 + 256]
                    mm(out,
                       KKF[h2 * 64:h2 * 64 + 64,
                           b_ * 256 + tkc * 128:b_ * 256 + (tkc + 1) * 128],
                       qsl[h2 * 64:h2 * 64 + 64, b_ * 256:(b_ + 1) * 256],
                       start=True, stop=True,
                       reads=[sl_KKF, sl_QW[qs]], wbank=sl_PB[sb0 + h2])
            p.emit("scalar", (lambda e, es=es, sb0=sb0: e.activation(
                ETv[es], PS[:, sb0 * 512:sb0 * 512 + 1024],
                AF.Exp, bias=EXPB, scale=f32(1.0))),
                reads=[sl_PB[sb0], sl_PB[sb0 + 1]], writes=[sl_ET[es]])

        def l0_av(b_):
            es = b_ % 2
            ab = 4 if es == 0 else 7
            for h2 in range(2):
                for tqc in range(2):
                    for tkc in range(2):
                        lhs = ETv[es][:, h2 * 512 + tkc * 256 + tqc * 128:
                                      h2 * 512 + tkc * 256 + (tqc + 1) * 128]
                        mm(bank(ab, (h2 * 2 + tqc) * 65, 65), lhs,
                           VT0[:, (b_ * 2 + tkc) * 65:(b_ * 2 + tkc) * 65 + 65],
                           start=(tkc == 0), stop=(tkc == 1),
                           reads=[sl_ET[es], sl_VT0], wbank=sl_PB[ab])
            p.emit("vector", (lambda e, ab=ab, es=es: e.reciprocal(
                RS[:, es * 4:es * 4 + 4],
                bank(ab, 0, 260).rearrange("p (i n) -> p i n", n=65)[:, :, 64])),
                reads=[sl_PB[ab]], writes=[sl_RS], partial=True)
            for h2 in range(2):
                for tqc in range(2):
                    p.emit("vector", (lambda e, h2=h2, tqc=tqc, es=es, ab=ab:
                                      e.tensor_scalar(
                        CTMv[es][:, tqc * 128 + h2 * 64:tqc * 128 + h2 * 64 + 64],
                        bank(ab, (h2 * 2 + tqc) * 65, 64),
                        RS[:, es * 4 + h2 * 2 + tqc:es * 4 + h2 * 2 + tqc + 1],
                        None, op0=ALU.mult)),
                        reads=[sl_PB[ab], sl_RS], writes=[sl_CTM[es]],
                        partial=True)

        ctps = [Slot(f"ctps{hp}_{k}") for k in range(4)]
        for k in range(4):
            b_sl = sl_PB[k // 2]
            _merge(ctps[k].w, b_sl.w)
            _merge(ctps[k].we, b_sl.we)
            _merge(ctps[k].r, b_sl.r)

        def l0_tr(b_):
            es = b_ % 2
            for tqc in range(2):
                pslot = bank_bf(es, 768 + tqc * 128, 128)
                bsl = sl_PB[es]
                tr(pslot, CTMv[es][:, tqc * 128:(tqc + 1) * 128], IDENT,
                   reads=[sl_CTM[es]], wbank=bsl)
                dst = CTXF[:, hp * T + b_ * 256 + tqc * 128:
                           hp * T + b_ * 256 + (tqc + 1) * 128]
                scopy(dst, pslot, reads=[bsl], writes=[sl_CTXF])

        l0_scores(0)
        for b_ in range(B):
            if b_ + 1 < B:
                l0_scores(b_ + 1)
            l0_av(b_)
            if b_ >= 1:
                l0_tr(b_ - 1)
        l0_tr(B - 1)
        for k in range(4):
            b_sl = sl_PB[k // 2]
            _merge(b_sl.w, ctps[k].w)
            _merge(b_sl.we, ctps[k].we)
            _merge(b_sl.r, ctps[k].r)
    for i in range(NT):
        for dh in range(2):
            bk = (i * 2 + dh) % 2
            mm(bank(bk), ONE_ROW128, B0P_S[0:1, dh * 512:(dh + 1) * 512],
               start=True, stop=False, reads=[sl_BIA], wbank=sl_PB[bk])
            for j in range(NDC):
                mm(bank(bk), CTXF[:, j * T + i * 128:j * T + (i + 1) * 128],
                   WPTc[j][:, dh * 512:(dh + 1) * 512],
                   start=False, stop=(j == NDC - 1),
                   reads=[sl_CTXF, sl_WPTc[j]], wbank=sl_PB[bk])
            p.emit("vector", (lambda e, i=i, dh=dh, bk=bk: e.scalar_tensor_tensor(
                XAv[i][:, dh * 512:(dh + 1) * 512], bank(bk), 1.0,
                XAv[i][:, dh * 512:(dh + 1) * 512],
                op0=ALU.mult, op1=ALU.add)),
                reads=[sl_PB[bk], sl_XA[i]], writes=[sl_XA[i]], partial=True)
        if n_stages >= 2:
            emit_ln_stats(XAv, sl_XA, i)

    if n_stages >= 2:
        HT2, sl_HT2 = emit_ln(XAv, sl_XA, "l0f", stats_emitted=True)
        W3c, sl_W3c = [], []
        for j in range(NDC):
            a, s = R2.view(j * 6144, 0, 128, 3072, BF16, f"W3_{j}")
            W3c.append(a)
            sl_W3c.append(s)
        QT0t, sl_QT0t, KT0t, sl_KT0t, VTft, sl_VTft = [], [], [], [], [], []
        for i in range(NT):
            a, s = R2.view(49152 + i * 128, 0, 128, HD, BF16, f"QT0_{i}")
            QT0t.append(a)
            sl_QT0t.append(s)
            a, s = R2.view(51200 + i * 128, 0, 128, HD, BF16, f"KT0_{i}")
            KT0t.append(a)
            sl_KT0t.append(s)
            a, s = R2.view(53248 + i * 128, 0, 128, HD, BF16, f"VTf_{i}")
            VTft.append(a)
            sl_VTft.append(s)
        QKVsrc = R2h[:, 24576:27648]
        SQS, sl_SQS = R2.view(55296, 0, 128, 128, BF16, "SQS")
        for j in range(NDC):
            dma("sync" if j % 2 == 0 else "gpsimd", W3c[j],
                w3st[j, :].rearrange("(p n) -> p n", p=128),
                reads=[sl_w3st], writes=[sl_W3c[j]])
        dma("gpsimd", SIMB[:].rearrange("p (j n) -> p j n", j=8),
            SIM.rearrange("(j p) n -> p j n", j=8), writes=[sl_SIMB])
        if n_stages >= 3:
            for j in range(NDC):
                dma_cast(w1stA[2 * NDC + j, :],
                         W1kT[j * 128:(j + 1) * 128, :], "w1c",
                         writes=[sl_w1st], partial=True)
        dma("sync", STT2[0:1, 0:16], GATES, writes=[sl_STT2], partial=True)
        p.emit("vector", (lambda e: e.tensor_tensor(
            SQS[:], SIMB[:], SIMB[:], op=ALU.mult)),
            reads=[sl_SIMB], writes=[sl_SQS])
        for j in range(NDC):
            mm(bank(7, 0, 16)[0:1, :], ONE_COL, SQS[:, j * 16:(j + 1) * 16],
               start=(j == 0), stop=(j == NDC - 1),
               reads=[sl_SQS, sl_CST], wbank=sl_PB[7])
        p.emit("scalar", (lambda e: e.activation(
            STT2[0:1, 16:32], bank(7, 0, 16)[0:1, :], AF.Ln,
            bias=f32(1e-12), scale=f32(1.0))),
            reads=[sl_PB[7]], writes=[sl_STT2], partial=True)
        p.emit("scalar", (lambda e: e.activation(
            STT2[0:1, 16:32], STT2[0:1, 16:32], AF.Exp, bias=f32(0.0),
            scale=f32(-0.5))),
            reads=[sl_STT2], writes=[sl_STT2], partial=True)
        p.emit("scalar", (lambda e: e.mul(
            STT2[0:1, 16:32], STT2[0:1, 16:32], f32(1.0 / 32.0))),
            reads=[sl_STT2], writes=[sl_STT2], partial=True)
        p.emit("scalar", (lambda e: e.activation(
            STT2[0:1, 32:48], STT2[0:1, 0:16], AF.Exp, bias=f32(0.0),
            scale=f32(-1.0))),
            reads=[sl_STT2], writes=[sl_STT2], partial=True)
        p.emit("vector", (lambda e: e.tensor_scalar(
            STT2[0:1, 32:48], STT2[0:1, 32:48], 1.0, None, op0=ALU.add)),
            reads=[sl_STT2], writes=[sl_STT2], partial=True)
        p.emit("vector", (lambda e: e.reciprocal(
            STT2[0:1, 32:48], STT2[0:1, 32:48])),
            reads=[sl_STT2], writes=[sl_STT2], partial=True)
        p.emit("vector", (lambda e: e.tensor_copy(
            BIAh[0:1, 1024:1040], STT2[0:1, 16:32])),
            reads=[sl_STT2], writes=[sl_BIA], partial=True)
        p.emit("vector", (lambda e: e.tensor_copy(
            BIAh[0:1, 1040:1056], STT2[0:1, 32:48])),
            reads=[sl_STT2], writes=[sl_BIA], partial=True)
        mm(bank(7, 16, 16), ONE_ROW128, BIAh[0:1, 1024:1040],
           start=True, stop=True, reads=[sl_BIA], wbank=sl_PB[7])
        mm(bank(7, 32, 16), ONE_ROW128, BIAh[0:1, 1040:1056],
           start=True, stop=True, reads=[sl_BIA], wbank=sl_PB[7])
        p.emit("vector", (lambda e: e.tensor_copy(
            GSCR[:, 128:144], bank(7, 16, 16))),
            reads=[sl_PB[7]], writes=[sl_GSCR], partial=True)
        p.emit("vector", (lambda e: e.tensor_copy(
            GSCR[:, 144:160], bank(7, 32, 16))),
            reads=[sl_PB[7]], writes=[sl_GSCR], partial=True)
        RSNB = GSCR[:, 128:144]
        SGB = GSCR[:, 144:160]
        p.emit("vector", (lambda e: e.tensor_tensor(
            RS[:, 16:32], STAT[:, 64:80], STAT[:, 64:80], op=ALU.mult)),
            reads=[sl_STAT], writes=[sl_RS], partial=True)
        p.emit("vector", (lambda e: e.tensor_tensor(
            RS[:, 32:48], STAT[:, 48:64], RS[:, 16:32], op=ALU.mult)),
            reads=[sl_STAT, sl_RS], writes=[sl_RS], partial=True)
        p.emit("scalar", (lambda e: e.activation(
            RS[:, 32:48], RS[:, 32:48], AF.Ln, bias=f32(0.0), scale=f32(1.0))),
            reads=[sl_RS], writes=[sl_RS], partial=True)
        p.emit("scalar", (lambda e: e.activation(
            RS[:, 32:48], RS[:, 32:48], AF.Exp, bias=f32(0.0), scale=f32(-0.5))),
            reads=[sl_RS], writes=[sl_RS], partial=True)
        RHINV = RS[:, 32:48]

        LG = GSCR[:, 0:16]
        MK = GSCR[:, 16:32]
        TMP = GSCR[:, 32:48]
        GTD = GSCR[:, 48:64]
        M1 = GSCR[:, 64:65]
        M2 = GSCR[:, 65:66]
        SEL = GSCR[:, 66:67]
        NMX = GSCR[:, 67:68]
        SUM = GSCR[:, 68:69]
        RSM = GSCR[:, 69:70]
        FB = GSCR[:, 80:96]
        UU = GSCR[:, 96:112]
        AM = GSCR[:, 112:128]
        a2aI = [t.ap() for t in a2aI_t]
        a2aO = [t.ap() for t in a2aO_t]
        for ti in range(NT):
            tc = (ti * 2) % NT + (ti * 2) // NT
            gb = 6 + (ti % 2)
            for j in range(NDC):
                mm(bank(gb, 0, 16), HF[:, j * T + tc * 128:j * T + (tc + 1) * 128],
                   SIMB[:, j * 16:(j + 1) * 16],
                   start=(j == 0), stop=(j == NDC - 1),
                   reads=[sl_HFg[j][tc], sl_SIMB], wbank=sl_PB[gb])
            p.emit("vector", (lambda e, tc=tc, gb=gb: e.scalar_tensor_tensor(
                LG, bank(gb, 0, 16), RHINV[:, tc:tc + 1], RSNB,
                op0=ALU.mult, op1=ALU.mult)),
                reads=[sl_PB[gb], sl_RS, sl_GSCR], writes=[sl_GSCR], partial=True)
            p.emit("vector", (lambda e: e.tensor_tensor(
                LG, LG, SGB, op=ALU.subtract)),
                reads=[sl_GSCR], writes=[sl_GSCR], partial=True)
            p.emit("vector", (lambda e: e.tensor_scalar(
                TMP, LG, 1.0, -3e38, op0=ALU.mult, op1=ALU.max, accum_out=M1)),
                reads=[sl_GSCR], writes=[sl_GSCR], partial=True)
            p.emit("vector", (lambda e: e.tensor_scalar(
                MK, LG, 0.0, None, op0=ALU.is_gt)),
                reads=[sl_GSCR], writes=[sl_GSCR], partial=True)
            p.emit("vector", (lambda e: e.tensor_scalar(
                GTD, LG, 0.0, None, op0=ALU.max)),
                reads=[sl_GSCR], writes=[sl_GSCR], partial=True)
            p.emit("vector", (lambda e: e.tensor_scalar(
                AM, LG, M1, None, op0=ALU.is_ge)),
                reads=[sl_GSCR], writes=[sl_GSCR], partial=True)
            p.emit("vector", (lambda e: e.scalar_tensor_tensor(
                TMP, AM, -2e9, LG, op0=ALU.mult, op1=ALU.add)),
                reads=[sl_GSCR], writes=[sl_GSCR], partial=True)
            p.emit("vector", (lambda e: e.tensor_scalar(
                TMP, TMP, 1.0, -3e38, op0=ALU.mult, op1=ALU.max,
                accum_out=M2)),
                reads=[sl_GSCR], writes=[sl_GSCR], partial=True)
            p.emit("vector", (lambda e: e.tensor_scalar(
                FB, LG, M2, None, op0=ALU.is_ge)),
                reads=[sl_GSCR], writes=[sl_GSCR], partial=True)
            p.emit("vector", (lambda e: e.tensor_scalar(
                SEL, M1, 0.0, None, op0=ALU.is_le)),
                reads=[sl_GSCR], writes=[sl_GSCR], partial=True)
            p.emit("vector", (lambda e: e.tensor_scalar(
                FB, FB, SEL, None, op0=ALU.mult)),
                reads=[sl_GSCR], writes=[sl_GSCR], partial=True)
            p.emit("vector", (lambda e: e.tensor_tensor(
                MK, MK, FB, op=ALU.add)),
                reads=[sl_GSCR], writes=[sl_GSCR], partial=True)
            p.emit("vector", (lambda e: e.tensor_scalar(
                UU, MK, 1e9, -1e9, op0=ALU.mult, op1=ALU.add)),
                reads=[sl_GSCR], writes=[sl_GSCR], partial=True)
            p.emit("vector", (lambda e: e.tensor_tensor(
                UU, UU, GTD, op=ALU.add)),
                reads=[sl_GSCR], writes=[sl_GSCR], partial=True)
            p.emit("vector", (lambda e: e.tensor_scalar(
                TMP, UU, 1.0, -3e38, op0=ALU.mult, op1=ALU.max,
                accum_out=NMX)),
                reads=[sl_GSCR], writes=[sl_GSCR], partial=True)
            p.emit("vector", (lambda e: e.tensor_scalar(
                NMX, NMX, -1.0, None, op0=ALU.mult)),
                reads=[sl_GSCR], writes=[sl_GSCR], partial=True)
            p.emit("scalar", (lambda e: e.activation(
                UU, UU, AF.Exp, bias=NMX, scale=f32(1.0), accum_out=SUM)),
                reads=[sl_GSCR], writes=[sl_GSCR], partial=True)
            p.emit("vector", (lambda e: e.reciprocal(RSM, SUM)),
                   reads=[sl_GSCR], writes=[sl_GSCR], partial=True)
            p.emit("vector", (lambda e, tc=tc: e.tensor_scalar(
                RW[:, tc * 16:(tc + 1) * 16], UU, RSM, None, op0=ALU.mult)),
                reads=[sl_GSCR], writes=[sl_RW], partial=True)
            for hf2 in range(2):
                for j in range(NDC):
                    lhs = HF[:, j * T + tc * 128:j * T + (tc + 1) * 128]
                    for s3 in range(3):
                        s = hf2 * 3 + s3
                        mm(bank(s), lhs,
                           W3c[j][:, s * 512:(s + 1) * 512],
                           start=(j == 0), stop=(j == NDC - 1),
                           reads=[sl_HFg[j][tc], sl_W3c[j]], wbank=sl_PB[s])
                for e8 in range(8):
                    e_ = hf2 * 8 + e8
                    src = PS[:, e_ * 192:(e_ + 1) * 192]
                    rbs = [sl_PB[(e_ * 192) // 512],
                           sl_PB[((e_ + 1) * 192 - 1) // 512]]
                    if e_ == 0:
                        p.emit("vector", (lambda e, src=src, tc=tc: e.tensor_scalar(
                            QKVA[:], src, RW[:, tc * 16:tc * 16 + 1], None,
                            op0=ALU.mult)),
                            reads=rbs + [sl_RW], writes=[sl_QKVA])
                    else:
                        p.emit("vector", (lambda e, src=src, tc=tc, e_=e_:
                                          e.scalar_tensor_tensor(
                            QKVA[:], src, RW[:, tc * 16 + e_:tc * 16 + e_ + 1],
                            QKVA[:], op0=ALU.mult, op1=ALU.add)),
                            reads=rbs + [sl_RW, sl_QKVA], writes=[sl_QKVA],
                            partial=True)
            p.emit("scalar", (lambda e, tc=tc: e.mul(
                QT0t[tc][:], QKVA[:, 0:64], SCALE)),
                reads=[sl_QKVA], writes=[sl_QT0t[tc]], partial=False)
            scopy(KT0t[tc][:], QKVA[:, 64:128],
                  reads=[sl_QKVA], writes=[sl_KT0t[tc]], partial=False)
            scopy(VTft[tc][:], QKVA[:, 128:192],
                  reads=[sl_QKVA], writes=[sl_VTft[tc]], partial=False)
            h_, j_ = tc % 2, tc // 2
            dma("sync",
                a2aI[h_][j_, :].rearrange("(blk p n) -> blk p n",
                                          blk=3, p=128).transpose([1, 0, 2]),
                QKVsrc.rearrange("p (blk n) -> p blk n",
                                 blk=3)[:, :, tc * HD:(tc + 1) * HD],
                reads=[sl_QT0t[tc], sl_KT0t[tc], sl_VTft[tc]],
                writes=[sl_a2aih[h_]], partial=(j_ > 0))
            if ti == NT // 2 - 1 or ti == NT - 1:
                hh = ti // (NT // 2)

                def cc_a2a(e, hh=hh):
                    with allow_ici_all_to_all():
                        return e.collective_compute(
                            "AllToAll", ALU.bypass,
                            replica_groups=[list(range(NCORES))],
                            ins=[a2aI_t[hh].ap().opt()],
                            outs=[a2aO_t[hh].ap().opt()])
                p.emit("gpsimd", cc_a2a, reads=[sl_a2aih[hh]],
                       writes=[sl_a2aoh[hh]], sem="cc", inc=1)

        GKVv, sl_GKVh = [], []
        for h in range(2):
            a_, s_ = R2.view(0, 0, 128, 3088, BF16, f"GKV{h}")
            GKVv.append(a_)
            sl_GKVh.append(s_)
        GKV = GKVv[0]
        p.emit("vector", (lambda e: e.memset(GKV[:, 2048:3088].rearrange(
            "p (i n) -> p i n", n=65)[:, :, 64], 1.0)),
            reads=(), writes=[sl_GKVh[0], sl_GKVh[1]], partial=False)
        QFF, sl_QFF0 = R2.view(6176, 0, 64, T, BF16, "QFF")
        KFF, sl_KFF0 = R2.view(10272, 0, 64, T, BF16, "KFF")
        sl_QFFt = [Slot(f"QFF{i}") for i in range(NT)]
        sl_KFFt = [Slot(f"KFF{i}") for i in range(NT)]
        for i in range(NT):
            _merge(sl_QFFt[i].w, sl_QFF0.w)
            _merge(sl_QFFt[i].we, sl_QFF0.we)
            _merge(sl_QFFt[i].r, sl_QFF0.r)
            _merge(sl_KFFt[i].w, sl_KFF0.w)
            _merge(sl_KFFt[i].we, sl_KFF0.we)
            _merge(sl_KFFt[i].r, sl_KFF0.r)
        ET2v, sl_ET2v = [], []
        for s in range(3):
            a_, s_ = R2.view((16416 + s * 2048) if s < 2 else 28704,
                             0, 128, 1024, BF16, f"ET2_{s}")
            ET2v.append(a_)
            sl_ET2v.append(s_)
        for h in range(2):
            for blk in range(3):
                if blk < 2:
                    dst = GKVv[h][:, blk * 1024:(blk + 1) * 1024].rearrange(
                        "p (r two n) -> p r two n", two=2, n=64)[:, :, h, :]
                else:
                    dst = GKVv[h][:, 2048:3088].rearrange(
                        "p (r two n) -> p r two n", two=2, n=65)[:, :, h, 0:64]
                srcb = a2aO[h][:, blk * 8192:(blk + 1) * 8192].rearrange(
                    "r (pp n) -> r pp n", pp=128)
                dma("gpsimd", dst, srcb.transpose([1, 0, 2]),
                    reads=[sl_a2aoh[h]], writes=[sl_GKVh[h]],
                    partial=True)
        if n_stages >= 3:
            for j in range(NDC):
                dma_cast(w1stA[3 * NDC + j, :],
                         W1vT[j * 128:(j + 1) * 128, :], "w1c",
                         writes=[sl_w1st], partial=True)
            for j in range(NDC):
                dma_cast(w1stA[j, :], W1qT[j * 128:(j + 1) * 128, :], "w1c",
                         writes=[sl_w1st], partial=True)
            for j in range(NDC):
                dma_cast(w1stA[NDC + j, :], W1oT[j * 128:(j + 1) * 128, :],
                         "w1c", writes=[sl_w1st], partial=True)
        if n_stages >= 4:
            for e_ in range(NE1):
                for j in range(NDC):
                    dma_cast(ewst_t.ap()[e_ * NDC + j, :],
                             EW[e_, j * 128:(j + 1) * 128, :], "ewc",
                             writes=[sl_ewst], partial=True)
        RWF, sl_RWF = R2.view(61440, 0, 16, T, BF16, "RWF")
        for tc in range(NT):
            k = tc % 2
            pslot = bank(0, k * 128, 128)
            bsl = sl_PB[0]
            tr(pslot[0:16, :], RW[:, tc * 16:(tc + 1) * 16], CSTF[:, 0:128],
               reads=[sl_RW], wbank=bsl, ident_slot=sl_CSTF)
            vcopy(RWF[0:16, tc * 128:(tc + 1) * 128], pslot[0:16, :],
                  reads=[bsl], writes=[sl_RWF])
        WOS, sl_WOS = R2.view(45056, 0, 128, 8192, BF16, "WOS")
        for j in range(NDC):
            dma("gpsimd", WOS[:, j * 1024:(j + 1) * 1024],
                WO0[j * 128:(j + 1) * 128, :], writes=[sl_WOS], partial=(j > 0))
        for i in [x for x in range(NT) if x % 2 == 0] + \
                 [x for x in range(NT) if x % 2 == 1]:
            k = i % 2
            pslot = bank_bf(6, k * 128, 128)[0:64, :]
            bsl = sl_PB[6]
            tr(pslot, GKV[:, i * HD:(i + 1) * HD], IDENT,
               reads=[sl_GKVh[i % 2]], wbank=bsl)
            vcopy(QFF[0:64, i * 128:(i + 1) * 128], pslot,
                  reads=[bsl], writes=[sl_QFFt[i]], partial=False)
            pslot2 = bank_bf(7, k * 128, 128)[0:64, :]
            bsl2 = sl_PB[7]
            tr(pslot2, GKV[:, 1024 + i * HD:1024 + (i + 1) * HD], IDENT,
               reads=[sl_GKVh[i % 2]], wbank=bsl2)
            scopy(KFF[0:64, i * 128:(i + 1) * 128], pslot2,
                  reads=[bsl2], writes=[sl_KFFt[i]], partial=False)

        c2aI = [t.ap() for t in c2aI_t]
        c2aO = [t.ap() for t in c2aO_t]
        CTX65, sl_CTX65 = R2.view(20512, 0, 65, 4096, BF16, "CTX65")
        qgroups = [[0, 2, 4, 6], [8, 10, 12, 14], [1, 3, 5, 7], [9, 11, 13, 15]]
        for gi, grp in enumerate(qgroups):
            t0 = grp[0]
            cb = 4 + gi % 2
            rhsq = QFF[0:64, :].rearrange(
                "p (a two n) -> p a two n", two=2,
                n=128)[:, t0 // 2:t0 // 2 + 4, t0 % 2, :]
            ntk = grp[-1] + 1
            first_av = True
            wl = [list(range(w0, min(w0 + 2, ntk))) for w0 in range(0, ntk, 2)]
            for wvi, wtk in enumerate(wl):
                pp = wvi % 3
                et = ET2v[pp]
                set_ = sl_ET2v[pp]
                for wi, tkc in enumerate(wtk):
                    bkx = (0, 2, 6)[pp] + wi
                    mm(bank(bkx), KFF[0:64, tkc * 128:(tkc + 1) * 128], rhsq,
                       start=True, stop=True,
                       reads=[sl_KFFt[tkc]] + [sl_QFFt[t] for t in grp],
                       wbank=sl_PB[bkx])
                b0 = (0, 2, 6)[pp]
                p.emit("scalar", (lambda e, b0=b0, et=et, wn=len(wtk):
                                  e.activation(
                    et[:, 0:wn * 512], PS[:, b0 * 512:b0 * 512 + wn * 512],
                    AF.Exp, bias=EXPB, scale=f32(1.0))),
                    reads=[sl_PB[b0], sl_PB[b0 + 1]],
                    writes=[set_], partial=False)
                for wi, tkc in enumerate(wtk):
                    for s, tq in enumerate(grp):
                        if tq < tkc:
                            p.emit("vector", (lambda e, wi=wi, s=s, et=et: e.memset(
                                et[:, wi * 512 + s * 128:wi * 512 + (s + 1) * 128],
                                0.0)),
                                reads=[set_], writes=[set_], partial=True)
                        elif tq == tkc:
                            p.emit("vector", (lambda e, wi=wi, s=s, et=et:
                                              e.tensor_tensor(
                                et[:, wi * 512 + s * 128:wi * 512 + (s + 1) * 128],
                                et[:, wi * 512 + s * 128:wi * 512 + (s + 1) * 128],
                                TRIL, op=ALU.mult)),
                                reads=[set_, sl_CST], writes=[set_],
                                partial=True)
                for wi, tkc in enumerate(wtk):
                    mm(PS[0:65, cb * 512:(cb + 1) * 512],
                       GKV[:, 2048 + tkc * 65:2048 + (tkc + 1) * 65],
                       et[:, wi * 512:(wi + 1) * 512],
                       start=first_av, stop=(tkc == ntk - 1),
                       reads=[sl_GKVh[tkc % 2], set_], wbank=sl_PB[cb])
                    first_av = False
            p.emit("vector", (lambda e, t0=t0, cb=cb: e.tensor_copy(
                CTX65[0:65, :].rearrange(
                    "p (a two n) -> p a two n", two=2,
                    n=128)[:, t0 // 2:t0 // 2 + 4, t0 % 2, :],
                PS[0:65, cb * 512:(cb + 1) * 512].rearrange(
                    "p (s n) -> p s n", n=128))),
                reads=[sl_PB[cb]], writes=[sl_CTX65], partial=(gi > 0))
            for s, tq in enumerate(grp):
                h_, j_ = tq % 2, tq // 2
                dma("sync", c2aI[h_][j_, :].rearrange("(p n) -> p n", p=65),
                    CTX65[0:65, tq * 128:(tq + 1) * 128],
                    reads=[sl_CTX65], writes=[sl_c2aih[h_]], partial=(j_ > 0))
            if gi in (1, 3):
                hh = gi // 2 if gi == 1 else 1
                hh = 0 if gi == 1 else 1

                def cc_c2a(e, hh=hh):
                    with allow_ici_all_to_all():
                        return e.collective_compute(
                            "AllToAll", ALU.bypass,
                            replica_groups=[list(range(NCORES))],
                            ins=[c2aI_t[hh].ap().opt()],
                            outs=[c2aO_t[hh].ap().opt()])
                p.emit("gpsimd", cc_c2a, reads=[sl_c2aih[hh]],
                       writes=[sl_c2aoh[hh]], sem="cc", inc=1)

        C2F, sl_C2F = R2.view(4096, 0, 128, T, BF16, "C2F")
        SUMR, sl_SUMR = R2.view(24608, 0, 1, T, BF16, "SUMR")
        for h in range(2):
            srcb = c2aO[h][:, :].rearrange("r (p n) -> r p n", p=65)
            dma("gpsimd",
                C2F[0:64, :].rearrange("p (r two n) -> p r two n",
                                       two=2, n=128)[:, :, h, :],
                srcb[:, 0:64, :].transpose([1, 0, 2]),
                reads=[sl_c2aoh[h]], writes=[sl_C2F], partial=(h > 0))
            dma("gpsimd",
                SUMR[0:1, :].rearrange("p (r two n) -> p r two n",
                                       two=2, n=128)[:, :, h, :],
                srcb[:, 64:65, :].transpose([1, 0, 2]),
                reads=[sl_c2aoh[h]], writes=[sl_SUMR], partial=(h > 0))
        dma("sync", C2F[64:128, :], C2F[0:64, :],
            reads=[sl_C2F], writes=[sl_C2F], partial=True)
        for sfc in range(4):
            mm(PS[0:16, sfc * 512:(sfc + 1) * 512],
               BIAh[0:1, 1152:1168],
               SUMR[0:1, sfc * 512:(sfc + 1) * 512],
               start=True, stop=True,
               reads=[sl_BIA, sl_SUMR], wbank=sl_PB[sfc])
            p.emit("vector", (lambda e, sfc=sfc: e.reciprocal(
                PS[0:16, sfc * 512:(sfc + 1) * 512],
                PS[0:16, sfc * 512:(sfc + 1) * 512])),
                reads=[sl_PB[sfc]], writes=[sl_PB[sfc]], partial=True)
            p.emit("vector", (lambda e, sfc=sfc: e.tensor_tensor(
                RWF[0:16, sfc * 512:(sfc + 1) * 512],
                RWF[0:16, sfc * 512:(sfc + 1) * 512],
                PS[0:16, sfc * 512:(sfc + 1) * 512], op=ALU.mult)),
                reads=[sl_RWF, sl_PB[sfc]], writes=[sl_RWF], partial=True)

        for i in range(NT):
            _merge(sl_QFF0.r, sl_QFFt[i].r)
            _merge(sl_QFF0.w, sl_QFFt[i].w)
            _merge(sl_KFF0.r, sl_KFFt[i].r)
            _merge(sl_KFF0.w, sl_KFFt[i].w)
        MIXv, sl_MIX = [], []
        for pr in range(8):
            a, sl = R2.view(12288 + pr * 4096, 0, 128, T, BF16, f"MIX{pr}")
            MIXv.append(a)
            sl_MIX.append(sl)
        for pr in range(8):
            for s in range(4):
                bk = 4 + (pr * 4 + s) % 2
                mm(bank(bk), SELP[:, pr * 128:(pr + 1) * 128],
                   RWF[0:16, s * 512:(s + 1) * 512],
                   start=True, stop=True,
                   reads=[sl_RWF, sl_CST], wbank=sl_PB[bk])
                p.emit("vector", (lambda e, pr=pr, s=s, bk=bk: e.tensor_tensor(
                    MIXv[pr][:, s * 512:(s + 1) * 512],
                    C2F[:, s * 512:(s + 1) * 512], bank(bk), op=ALU.mult)),
                    reads=[sl_C2F, sl_PB[bk]], writes=[sl_MIX[pr]],
                    partial=(s > 0))
        XBv = [None] * NT
        sl_XB = [None] * NT
        for i in range(3):
            a, s = R2.view(i * 4096, 0, 128, D, F32, f"XB{i}")
            XBv[i] = a
            sl_XB[i] = s
        XAfull = R1h[:].bitcast(F32)

        def emit_permute(i2, q):
            p0 = 16 * (i2 % 8)
            hi = i2 // 8
            srcp = XAfull[p0:p0 + 16, :].rearrange(
                "p (b two c) -> p b two c", two=2, c=1024)[:, :, hi, :]
            dma(q, XBv[i2], srcp,
                reads=[sl_XA[2 * b_ + hi] for b_ in range(8)],
                writes=[sl_XB[i2]])
            if n_stages >= 3:
                emit_ln_stats(XBv, sl_XB, i2)

        for i in [x for x in range(NT) if x % 2 == 0] + \
                 [x for x in range(NT) if x % 2 == 1]:
            for dh in range(2):
                bk = (i * 2 + dh) % 2
                for pr in range(8):
                    mm(bank(bk), MIXv[pr][:, i * 128:(i + 1) * 128],
                       WOS[:, pr * 1024 + dh * 512:pr * 1024 + (dh + 1) * 512],
                       start=(pr == 0), stop=(pr == 7),
                       reads=[sl_MIX[pr], sl_WOS], wbank=sl_PB[bk])
                p.emit("vector", (lambda e, i=i, dh=dh, bk=bk:
                                  e.scalar_tensor_tensor(
                    XAv[i][:, dh * 512:(dh + 1) * 512], bank(bk), 1.0,
                    XAv[i][:, dh * 512:(dh + 1) * 512],
                    op0=ALU.mult, op1=ALU.add)),
                    reads=[sl_PB[bk], sl_XA[i]], writes=[sl_XA[i]], partial=True)
            if i == NT - 2:
                for i2 in range(3):
                    emit_permute(i2, "sync" if i2 % 2 == 0 else "gpsimd")
            if i == NT - 1:
                for i2 in range(3, NT):
                    a_, s_ = R2.view(i2 * 4096, 0, 128, D, F32, f"XB{i2}")
                    XBv[i2] = a_
                    sl_XB[i2] = s_
                    emit_permute(i2, "sync" if i2 % 2 == 0 else "gpsimd")

    if n_stages >= 3:
        HT3, sl_HT3 = emit_ln(XBv, sl_XB, "l1a", stats_emitted=True)
        VT1, sl_VT1 = R1.view(0, 0, 128, NT * 16 * 65, BF16, "VT1")
        WKc, sl_WKc = [], []
        for j in range(NDC):
            a_, s_ = R1.view(33280 + j * 2048, 0, 128, 1024, BF16, f"WK_{j}")
            WKc.append(a_)
            sl_WKc.append(s_)
        WSv, sl_WS = [], []
        for s in range(3):
            a_, sl = R1.view(51712 + s * 2048, 0, 128, 1024, BF16, f"WS1_{s}")
            WSv.append(a_)
            sl_WS.append(sl)
        EB1, sl_EB1 = R1.view(57856, 0, 128, 2048, BF16, "EB1")
        sl_EB1g = []
        for hg in range(4):
            _, s_ = R1.view(57856 + hg * 1024, 0, 128, 512, BF16, f"EB1g{hg}")
            sl_EB1g.append(s_)
        KA, sl_KA = HTR.view(0, 0, 128, 16384, BF16, "KA")
        sl_KAc = []
        for dc in range(NDC):
            _, s_ = HTR.view(dc * 4096, 0, 128, 2048, BF16, f"KAc{dc}")
            sl_KAc.append(s_)
        p.emit("vector", (lambda e: e.memset(VT1[:].rearrange(
            "p (t h n) -> p t h n", h=16, n=65)[:, :, :, 64], 1.0)),
            reads=(), writes=[sl_VT1], partial=False)
        w1st = w1st_t.ap()

        for j in range(NDC):
            dma("sync" if j % 2 == 0 else "gpsimd", WKc[j][:],
                w1st[2 * NDC + j, :].rearrange("(p n) -> p n", p=128),
                reads=[sl_w1st], writes=[sl_WKc[j]])
        dma("gpsimd", BIAh[0:1, 0:1024], B1k, writes=[sl_BIA], partial=True)
        for tt in range(4):
            tcol = tt * 512
            for dc in range(NDC):
                for j in range(NDC):
                    mm(bank(dc), WKc[j][:, dc * 128:(dc + 1) * 128],
                       HF[:, j * T + tcol:j * T + tcol + 512],
                       start=(j == 0), stop=False,
                       reads=[sl_WKc[j]] + hf_r(j, 4 * tt, 4 * tt + 4),
                       wbank=sl_PB[dc])
                mm(bank(dc), BIAh[0:1, dc * 128:(dc + 1) * 128],
                   ONE_ROW512,
                   start=False, stop=True, reads=[sl_BIA], wbank=sl_PB[dc])
                p.emit("scalar", (lambda e, dc=dc, tcol=tcol: e.copy(
                    KA[:, dc * 2048 + tcol:dc * 2048 + tcol + 512], bank(dc))),
                    reads=[sl_PB[dc]], writes=[sl_KAc[dc]], partial=(tt > 0))
        WVc, sl_WVc = [], []
        for j in range(NDC):
            a_, s_ = R1.view(33280 + j * 2048, 0, 128, 1024, BF16, f"WV_{j}")
            WVc.append(a_)
            sl_WVc.append(s_)
            dma("sync" if j % 2 == 0 else "gpsimd", WVc[j][:],
                w1st[3 * NDC + j, :].rearrange("(p n) -> p n", p=128),
                reads=[sl_w1st], writes=[sl_WVc[j]])
        dma("gpsimd", BIAh[0:1, 0:1024], B1v, writes=[sl_BIA], partial=True)

        for tt in range(4):
            tcol = tt * 512
            for i4 in range(4):
                for half in range(2):
                    bk = i4 * 2 + half
                    for j in range(NDC):
                        mm(bank(bk), HF[:, j * T + tcol + i4 * 128:
                                        j * T + tcol + (i4 + 1) * 128],
                           WVc[j][:, half * 512:(half + 1) * 512],
                           start=(j == 0), stop=False,
                           reads=[sl_WVc[j], sl_HFg[j][4 * tt + i4]],
                           wbank=sl_PB[bk])
                    mm(bank(bk), ONE_ROW128,
                       BIAh[0:1, half * 512:(half + 1) * 512],
                       start=False, stop=True, reads=[sl_BIA], wbank=sl_PB[bk])
                    p.emit("scalar", (lambda e, tt=tt, i4=i4, half=half, bk=bk:
                                      e.copy(
                        VT1[:, (tt * 4 + i4) * 1040 + half * 520:
                            (tt * 4 + i4) * 1040 + half * 520 + 520].rearrange(
                            "p (h n) -> p h n", n=65)[:, :, 0:64],
                        bank(bk).rearrange("p (h n) -> p h n", n=64))),
                        reads=[sl_PB[bk]], writes=[sl_VT1], partial=True)

        QF1, sl_QF1 = R1.view(33280, 0, 128, 4096, BF16, "QF1")
        sl_QF1c = []
        for dc in range(NDC):
            _, s_ = R1.view(33280 + dc * 1024, 0, 128, 512, BF16, f"QF1c{dc}")
            sl_QF1c.append(s_)
        CT1, sl_CT1 = R1.view(41472, 0, 128, 4096, BF16, "CT1")
        CTTv2, sl_CTT2 = [], []
        for s in range(2):
            a_, s_ = R1.view(49664 + s * 12288, 0, 128, 1024, BF16, f"CTT{s}")
            CTTv2.append(a_)
            sl_CTT2.append(s_)
        ws_ctr = [0]

        def stream_w(src_f32, stash_row, g):
            ws = ws_ctr[0] % 3
            q = "sync" if ws_ctr[0] % 2 == 0 else "gpsimd"
            ws_ctr[0] += 1
            dma(q, WSv[ws],
                w1st[stash_row, :].rearrange("(p n) -> p n", p=128),
                reads=[sl_w1st], writes=[sl_WS[ws]])
            return ws

        def ebcol(h):
            hg = h // 4
            return hg * 512 + (h % 2) * 256 + ((h % 4) // 2) * 128

        for g in range(4):
            gcol = g * 512
            for j in range(NDC):
                ws = stream_w(W1qT[j * 128:(j + 1) * 128, :], j, g)
                for dc in range(NDC):
                    mm(bank(dc), WSv[ws][:, dc * 128:(dc + 1) * 128],
                       HF[:, j * T + gcol:j * T + gcol + 512],
                       start=(j == 0), stop=False,
                       reads=[sl_WS[ws]] + hf_r(j, 4 * g, 4 * g + 4),
                       wbank=sl_PB[dc])
            dma("gpsimd", BIAh[0:1, 0:1024], B1q, writes=[sl_BIA], partial=True)
            for dc in range(NDC):
                mm(bank(dc), BIAh[0:1, dc * 128:(dc + 1) * 128], ONE_ROW512,
                   start=False, stop=True, reads=[sl_BIA], wbank=sl_PB[dc])
                p.emit("scalar", (lambda e, dc=dc: e.mul(
                    QF1[:, dc * 512:(dc + 1) * 512], bank(dc), SCALE)),
                    reads=[sl_PB[dc]], writes=[sl_QF1c[dc]], partial=False)
            def s3_scores(i4):
                icol = i4 * 128
                for hg in range(4):
                    be = (hg % 2) * 2
                    for hh in range(4):
                        h = hg * 4 + hh
                        hp, hlo = h // 2, (h % 2) * 64
                        bk = be + (h % 2)
                        mm(bank(bk, ((h % 4) // 2) * 128, 128),
                           KA[hlo:hlo + 64,
                              hp * 2048 + gcol + icol:hp * 2048 + gcol + icol + 128],
                           QF1[hlo:hlo + 64, hp * 512 + icol:hp * 512 + icol + 128],
                           start=True, stop=True,
                           reads=[sl_KAc[hp], sl_QF1c[hp]], wbank=sl_PB[bk])
                    for bh in range(2):
                        p.emit("scalar", (lambda e, hg=hg, be=be, bh=bh:
                                          e.activation(
                            EB1[:, hg * 512 + bh * 256:hg * 512 + bh * 256 + 256],
                            PS[:, (be + bh) * 512:(be + bh) * 512 + 256],
                            AF.Exp, bias=EXPB, scale=f32(1.0))),
                            reads=[sl_PB[be + bh]], writes=[sl_EB1g[hg]],
                            partial=(bh > 0))
                    p.emit("vector", (lambda e, hg=hg: e.tensor_tensor(
                        EB1[:, hg * 512:(hg + 1) * 512],
                        EB1[:, hg * 512:(hg + 1) * 512],
                        BDMASK, op=ALU.mult)),
                        reads=[sl_EB1g[hg], sl_CST], writes=[sl_EB1g[hg]],
                        partial=True)

            def avb(h):
                if h < 7:
                    return 4, h * 65
                if h < 14:
                    return 5, (h - 7) * 65
                return 6, (h - 14) * 65

            def s3_av(i4):
                for h in range(NH):
                    lhs = EB1[:, ebcol(h):ebcol(h) + 128]
                    bkx, off = avb(h)
                    mm(bank(bkx, off, 65), lhs,
                       VT1[:, (g * 4 + i4) * 1040 + h * 65:
                           (g * 4 + i4) * 1040 + (h + 1) * 65],
                       start=True, stop=True,
                       reads=[sl_EB1g[h // 4], sl_VT1], wbank=sl_PB[bkx])

            def s3_evac(i4):
                ctt = CTTv2[i4 % 2]
                sct = sl_CTT2[i4 % 2]
                p.emit("vector", (lambda e: e.reciprocal(
                    RS[:, 0:7], bank(4, 0, 455).rearrange(
                        "p (i n) -> p i n", n=65)[:, :, 64])),
                    reads=[sl_PB[4]], writes=[sl_RS], partial=True)
                p.emit("vector", (lambda e: e.reciprocal(
                    RS[:, 7:14], bank(5, 0, 455).rearrange(
                        "p (i n) -> p i n", n=65)[:, :, 64])),
                    reads=[sl_PB[5]], writes=[sl_RS], partial=True)
                p.emit("vector", (lambda e: e.reciprocal(
                    RS[:, 14:16], bank(6, 0, 130).rearrange(
                        "p (i n) -> p i n", n=65)[:, :, 64])),
                    reads=[sl_PB[6]], writes=[sl_RS], partial=True)
                for h in range(NH):
                    bkx, off = avb(h)
                    p.emit("vector", (lambda e, h=h, ctt=ctt, bkx=bkx, off=off:
                                      e.tensor_scalar(
                        ctt[:, h * 64:(h + 1) * 64],
                        bank(bkx, off, 64),
                        RS[:, h:h + 1], None, op0=ALU.mult)),
                        reads=[sl_PB[bkx], sl_RS], writes=[sct],
                        partial=(h > 0))

            b7ps = [Slot(f"b7ps{g}_{k}") for k in range(4)]
            for k in range(4):
                _merge(b7ps[k].w, sl_PB[7].w)
                _merge(b7ps[k].we, sl_PB[7].we)
                _merge(b7ps[k].r, sl_PB[7].r)

            def s3_tr(i4):
                icol = i4 * 128
                ctt = CTTv2[i4 % 2]
                sct = sl_CTT2[i4 % 2]
                for hp in range(NDC):
                    k = hp % 4
                    pslot = bank_bf(7, k * 128, 128)
                    bsl = sl_PB[7]
                    tr(pslot, ctt[:, hp * 128:(hp + 1) * 128], IDENT,
                       reads=[sct], wbank=bsl)
                    if hp % 2 == 0:
                        scopy(CT1[:, hp * 512 + icol:hp * 512 + icol + 128],
                              pslot, reads=[bsl], writes=[sl_CT1])
                    else:
                        vcopy(CT1[:, hp * 512 + icol:hp * 512 + icol + 128],
                              pslot, reads=[bsl], writes=[sl_CT1])

            for i4 in range(4):
                s3_scores(i4)
                s3_av(i4)
                if i4 > 0:
                    s3_tr(i4 - 1)
                s3_evac(i4)
            s3_tr(3)
            for k in range(4):
                _merge(sl_PB[7].w, b7ps[k].w)
                _merge(sl_PB[7].we, b7ps[k].we)
                _merge(sl_PB[7].r, b7ps[k].r)
            for j in range(NDC):
                ws = stream_w(W1oT[j * 128:(j + 1) * 128, :], NDC + j, g)
                for i4 in range(4):
                    for half in range(2):
                        bk = i4 * 2 + half
                        mm(bank(bk), CT1[:, j * 512 + i4 * 128:
                                         j * 512 + (i4 + 1) * 128],
                           WSv[ws][:, half * 512:(half + 1) * 512],
                           start=(j == 0), stop=False,
                           reads=[sl_WS[ws], sl_CT1], wbank=sl_PB[bk])
            dma("gpsimd", BIAh[0:1, 0:1024], B1o, writes=[sl_BIA], partial=True)
            for i4 in range(4):
                ti = g * 4 + i4
                for half in range(2):
                    bk = i4 * 2 + half
                    mm(bank(bk), ONE_ROW128,
                       BIAh[0:1, half * 512:(half + 1) * 512],
                       start=False, stop=True, reads=[sl_BIA], wbank=sl_PB[bk])
                    p.emit("vector", (lambda e, ti=ti, half=half, bk=bk:
                                      e.scalar_tensor_tensor(
                        XBv[ti][:, half * 512:(half + 1) * 512], bank(bk),
                        1.0, XBv[ti][:, half * 512:(half + 1) * 512],
                        op0=ALU.mult, op1=ALU.add)),
                        reads=[sl_PB[bk], sl_XB[ti]], writes=[sl_XB[ti]],
                        partial=True)
                if n_stages >= 4:
                    emit_ln_stats(XBv, sl_XB, ti)

    if n_stages >= 4:
        HT4, sl_HT4 = emit_ln(XBv, sl_XB, "l1f", stats_emitted=True)
        WFM, sl_WFM = R1.view(0, 0, 8, T, BF16, "WFM")
        WBBv, sl_WBB = [], []
        for e_ in range(NE1):
            a, sl = R1.view(4096 + e_ * 4096, 0, 128, T, BF16, f"WBB{e_}")
            WBBv.append(a)
            sl_WBB.append(sl)
        XEv, sl_XE = [], []
        for s in range(2):
            a, sl = R1.view(36864 + s * 1024, 0, 128, 512, BF16, f"XE{s}")
            XEv.append(a)
            sl_XE.append(sl)
        OSTv, sl_OST = [], []
        for s in range(2):
            a, sl = R1.view(38912 + s * 2048, 0, 128, 512, F32, f"OST{s}")
            OSTv.append(a)
            sl_OST.append(sl)
        EXBs, sl_EXB = R1.view(55296, 0, 8, 1024, BF16, "EXB")
        WS4v, sl_WS4 = [], []
        for s in range(4):
            a, sl = R1.view(57344 + s * 2048, 0, 128, 1024, BF16, f"WS4_{s}")
            WS4v.append(a)
            sl_WS4.append(sl)
        dma("gpsimd", GWS[:].rearrange("p (j n) -> p j n", j=8),
            GW.rearrange("(j p) n -> p j n", j=8), writes=[sl_GWS])
        dma("gpsimd", BIAh[0:1, 1024:1032], GB, writes=[sl_BIA], partial=True)
        dma("gpsimd", EXBs[:], EB, writes=[sl_EXB])
        dma("sync", CSTh[0:8, 128:1152], SEL8D.ap(), writes=[sl_CST], partial=True)
        GBS = BIAh[0:1, 1024:1032]

        for tc in range(NT):
            gb4 = 6 + tc % 2
            for j in range(NDC):
                mm(bank(gb4, 0, 8), HF[:, j * T + tc * 128:j * T + (tc + 1) * 128],
                   GWS[:, j * 8:(j + 1) * 8],
                   start=(j == 0), stop=False,
                   reads=[sl_HFg[j][tc], sl_GWS], wbank=sl_PB[gb4])
            mm(bank(gb4, 0, 8), ONE_ROW128, GBS,
               start=False, stop=True, reads=[sl_BIA], wbank=sl_PB[gb4])
            G8 = GSCR[:, 0:8]
            NM = GSCR[:, 8:9]
            SM = GSCR[:, 9:10]
            RSM2 = GSCR[:, 10:11]
            MK8 = GSCR[:, 16:24]
            WSUM = GSCR[:, 24:25]
            p.emit("vector", (lambda e, gb4=gb4: e.tensor_scalar(
                GSCR[:, 32:40], bank(gb4, 0, 8), 1.0, -3e38, op0=ALU.mult,
                op1=ALU.max, accum_out=NM)),
                reads=[sl_PB[gb4]], writes=[sl_GSCR], partial=True)
            p.emit("vector", (lambda e: e.tensor_scalar(
                NM, NM, -1.0, None, op0=ALU.mult)),
                reads=[sl_GSCR], writes=[sl_GSCR], partial=True)
            p.emit("scalar", (lambda e, gb4=gb4: e.activation(
                G8, bank(gb4, 0, 8), AF.Exp, bias=NM, scale=f32(1.0),
                accum_out=SM)),
                reads=[sl_PB[gb4], sl_GSCR], writes=[sl_GSCR], partial=True)
            p.emit("vector", (lambda e: e.reciprocal(RSM2, SM)),
                   reads=[sl_GSCR], writes=[sl_GSCR], partial=True)
            p.emit("vector", (lambda e: e.tensor_scalar(
                G8, G8, RSM2, None, op0=ALU.mult)),
                reads=[sl_GSCR], writes=[sl_GSCR], partial=True)
            p.emit("vector", (lambda e: e.tensor_scalar(
                MK8, G8, 0.1, None, op0=ALU.is_ge)),
                reads=[sl_GSCR], writes=[sl_GSCR], partial=True)
            p.emit("vector", (lambda e: e.tensor_tensor(
                G8, G8, MK8, op=ALU.mult)),
                reads=[sl_GSCR], writes=[sl_GSCR], partial=True)
            p.emit("vector", (lambda e: e.tensor_scalar(
                GSCR[:, 32:40], G8, 1.0, 0.0, op0=ALU.mult,
                op1=ALU.add, accum_out=WSUM)),
                reads=[sl_GSCR], writes=[sl_GSCR], partial=True)
            p.emit("vector", (lambda e: e.reciprocal(RSM2, WSUM)),
                   reads=[sl_GSCR], writes=[sl_GSCR], partial=True)
            p.emit("vector", (lambda e, tc=tc: e.tensor_scalar(
                WTM[:, tc * 8:(tc + 1) * 8], G8, RSM2, None, op0=ALU.mult)),
                reads=[sl_GSCR], writes=[sl_WTM], partial=True)
        for tc in range(NT):
            k = tc % 2
            pslot = bank(6, k * 128, 128)
            bsl = sl_PB[6]
            tr(pslot[0:8, :], WTM[:, tc * 8:(tc + 1) * 8], CSTF[:, 0:128],
               reads=[sl_WTM], wbank=bsl, ident_slot=sl_CSTF)
            vcopy(WFM[0:8, tc * 128:(tc + 1) * 128], pslot[0:8, :],
                  reads=[bsl], writes=[sl_WFM])
        for e_ in range(NE1):
            for s in range(4):
                bk = 4 + (e_ * 4 + s) % 2
                mm(bank(bk), SEL8[:, e_ * 128:(e_ + 1) * 128],
                   WFM[0:8, s * 512:(s + 1) * 512],
                   start=True, stop=True,
                   reads=[sl_WFM, sl_CST], wbank=sl_PB[bk])
                p.emit("vector", (lambda e, e_=e_, s=s, bk=bk: e.tensor_copy(
                    WBBv[e_][:, s * 512:(s + 1) * 512], bank(bk))),
                    reads=[sl_PB[bk]], writes=[sl_WBB[e_]], partial=(s > 0))

        XOUTF = X_out
        for tt in range(4):
            tcol = tt * 512
            for dc in range(NDC):
                mm(bank(dc), EXBs[0:8, dc * 128:(dc + 1) * 128],
                   WFM[0:8, tcol:tcol + 512],
                   start=True, stop=False,
                   reads=[sl_EXB, sl_WFM], wbank=sl_PB[dc])
                for j2 in range(4):
                    ti = tt * 4 + j2
                    mm(bank(dc, j2 * 128, 128),
                       XBv[ti][:, dc * 128:(dc + 1) * 128], CSTF[:, 0:128],
                       start=False, stop=False,
                       reads=[sl_XB[ti], sl_CSTF], wbank=sl_PB[dc])
            for e_ in range(NE1):
                for j in range(NDC):
                    cix = e_ * NDC + j
                    ws = cix % 4
                    dma("sync" if cix % 2 == 0 else "gpsimd", WS4v[ws],
                        ewst_t.ap()[cix, :].rearrange("(p n) -> p n", p=128),
                        reads=[sl_ewst], writes=[sl_WS4[ws]])
                    xe = XEv[(e_ * NDC + j) % 2]
                    p.emit("vector", (lambda e, j=j, e_=e_, tcol=tcol, xe=xe:
                                      e.tensor_tensor(
                        xe, HF[:, j * T + tcol:j * T + tcol + 512],
                        WBBv[e_][:, tcol:tcol + 512], op=ALU.mult)),
                        reads=hf_r(j, 4 * tt, 4 * tt + 4) + [sl_WBB[e_]],
                        writes=[sl_XE[(e_ * NDC + j) % 2]])
                    last = (e_ == NE1 - 1 and j == NDC - 1)
                    for dc in range(NDC):
                        mm(bank(dc), WS4v[ws][:, dc * 128:(dc + 1) * 128], xe,
                           start=False, stop=last,
                           reads=[sl_WS4[ws], sl_XE[(e_ * NDC + j) % 2]],
                           wbank=sl_PB[dc])
            for dc in range(NDC):
                sslot = (tt * NDC + dc) % 2
                if dc % 2 == 0:
                    vcopy(OSTv[sslot][:], bank(dc),
                          reads=[sl_PB[dc]], writes=[sl_OST[sslot]],
                          partial=False)
                else:
                    scopy(OSTv[sslot][:], bank(dc),
                          reads=[sl_PB[dc]], writes=[sl_OST[sslot]],
                          partial=False)
                dma("sync",
                    XOUTF[dc * 128:(dc + 1) * 128, tt * 512:(tt + 1) * 512],
                    OSTv[sslot][:],
                    reads=[sl_OST[sslot]], writes=())

    if n_stages <= 2:
        for i in range(NT):
            b_, half = i // 2, i % 2
            dma("sync", X_out[b_, half * 128:(half + 1) * 128, :], XAv[i],
                reads=[sl_XA[i]], writes=())
    elif n_stages == 3:
        XOT = X_out.rearrange("b t c -> t b c")
        for i2 in range(NT):
            dma("sync", XOT[i2 * 16:(i2 + 1) * 16, :, :], XBv[i2],
                reads=[sl_XB[i2]], writes=())
    p.barrier_wait("sync", {"sync_q": p.cnt["sync_q"]})

    with nc.Block() as block:
        @block.tensor
        def _(t):
            for f in p.ops["tensor"]:
                f(t)

        @block.vector
        def _(v):
            for f in p.ops["vector"]:
                f(v)

        @block.scalar
        def _(s):
            for f in p.ops["scalar"]:
                f(s)

        @block.gpsimd
        def _(g):
            for f in p.ops["gpsimd"]:
                f(g)

        @block.sync
        def _(sy):
            for f in p.ops["sync"]:
                f(sy)

    return nc


def _stage_inputs(inputs, core):
    A = np.ascontiguousarray

    def idxs():
        return [h * 192 + k for h in range(NH) for k in range(HD)]

    x = inputs["L0_attn_x"]
    qi = [h * 192 + d_ for h in range(NH) for d_ in range(HD)]
    ki = [h * 192 + 64 + d_ for h in range(NH) for d_ in range(HD)]
    vi = [h * 192 + 128 + d_ for h in range(NH) for d_ in range(HD)]
    ipw = inputs["L1_attn_in_proj_weight"]
    ipb = inputs["L1_attn_in_proj_bias"]
    kvT = A(inputs["L0_attn_kv_w"].T)
    d = {
        "x": A(x[:, core * TLOC:(core + 1) * TLOC, :]),
        "w0qT": A(inputs["L0_attn_q_w"].T),
        "w0kkT": A(np.concatenate([kvT[:, 0:64], kvT[:, 0:64]], axis=1)),
        "w0vT": A(kvT[:, 64:128]),
        "w0pT": A(inputs["L0_attn_proj_w"].T),
        "b0p": A(inputs["L0_attn_proj_b"][None, :]),
        "sim": A(inputs["L0_ffn_sim_matrix"]),
        "gates": A(inputs["L0_ffn_gates"][None, :]),
        "wqkv0": A(np.concatenate(
            [np.concatenate([inputs["L0_ffn_q_proj"][e],
                             inputs["L0_ffn_k_proj"][e],
                             inputs["L0_ffn_v_proj"][e]], axis=1)
             for e in range(NE0)], axis=1)),
        "wo0": A(inputs["L0_ffn_o_proj"].reshape(NE0 * HD, D)),
        "w1qT": A(ipw[qi, :].T),
        "w1kT": A(ipw[ki, :].T),
        "w1vT": A(ipw[vi, :].T),
        "b1q": A(ipb[qi][None, :]),
        "b1k": A(ipb[ki][None, :]),
        "b1v": A(ipb[vi][None, :]),
        "w1oT": A(inputs["L1_attn_out_proj_weight"].T),
        "b1o": A(inputs["L1_attn_out_proj_bias"][None, :]),
        "gw": A(inputs["L1_ffn_gate_w"]),
        "gb": A(inputs["L1_ffn_gate_b"][None, :]),
        "ew": A(inputs["L1_ffn_expert_w"]),
        "eb": A(inputs["L1_ffn_expert_b"]),
    }
    return {k: np.asarray(v, dtype=np.float32) for k, v in d.items()}


_NC_CACHE = {}


def kernel(**inputs):
    key = N_STAGES
    if key not in _NC_CACHE:
        _NC_CACHE[key] = build_program(N_STAGES)
    nc = _NC_CACHE[key]
    in_maps = [_stage_inputs(inputs, c) for c in range(NCORES)]
    res = run_bass_kernel_spmd(nc, in_maps, core_ids=list(range(NCORES)))
    if N_STAGES >= 4:
        shards = [np.asarray(res.results[c]["out"])
                  .reshape(D, TLOC, B).transpose(2, 1, 0)
                  for c in range(NCORES)]
    else:
        shards = [np.asarray(res.results[c]["out"]) for c in range(NCORES)]
    out = np.concatenate(shards, axis=1)
    return out.astype(np.float32)


if __name__ == "__main__":
    nc = build_program(1)
    print("build ok")


# version 28 (final)
# speedup vs baseline: 1.0894x; 1.0786x over previous
import numpy as np
import ml_dtypes
import concourse.bass as bass
import concourse.mybir as mybir
from concourse.bass_utils import run_bass_kernel_spmd
from concourse.aot_env import allow_ici_all_to_all

F32 = mybir.dt.float32
BF16 = mybir.dt.bfloat16
AF = mybir.ActivationFunctionType
ALU = mybir.AluOpType
AX = mybir.AxisListType
BF = ml_dtypes.bfloat16
f32 = np.float32

NCORES = 8
B = 8
TLOC = 256
T = 2048
D = 1024
NT = 16
NDC = 8
HD = 64
NH = 16
NE0 = 16
NE1 = 8
EPS = 1e-5
EXPB = f32(-14.0)
SCALE = f32(0.125)

N_STAGES = 4


class Slot:
    __slots__ = ("w", "we", "r", "name")

    def __init__(self, name=""):
        self.w = {}
        self.we = {}
        self.r = {}
        self.name = name


def _merge(dst, src):
    for s, v in src.items():
        if v > dst.get(s, 0):
            dst[s] = v


class Prog:
    CENG = {"tensor": "tensor", "vector": "vector", "scalar": "scalar",
            "gpsimd": "gpsimd"}

    def __init__(self, nc):
        self.nc = nc
        self.ops = {e: [] for e in ("tensor", "vector", "scalar", "gpsimd", "sync")}
        self.sem = {}
        self.cnt = {}
        for s in ("tensor", "vector", "scalar", "gpsimd", "sync_q", "gp_q",
                  "cc", "w3c", "w1c", "ewc"):
            self.sem[s] = nc.alloc_semaphore("S_" + s)
            self.cnt[s] = 0
        self.hwm = {e: {} for e in self.ops}

    def emit(self, eng, fn, reads=(), writes=(), sem=None, inc=1, partial=False,
             extra=()):
        if sem is None:
            sem = self.CENG[eng]
        rdeps = {}
        for sl in reads:
            _merge(rdeps, sl.w)
        wdeps = {}
        ownsem = self.CENG.get(eng) if eng != "sync" else sem
        for sl in writes:
            _merge(wdeps, sl.r)
            if partial:
                _merge(wdeps, sl.we)
                if ownsem in sl.w and sl.w[ownsem] > wdeps.get(ownsem, 0):
                    wdeps[ownsem] = sl.w[ownsem]
            else:
                _merge(wdeps, sl.w)
        forced = []
        for s, v in extra:
            if s == "tensor" and eng == "tensor":
                if v > 0 and self.hwm[eng].get("tensor!", 0) < v:
                    forced.append((self.sem[s], v))
                    self.hwm[eng]["tensor!"] = v
            elif v > wdeps.get(s, 0):
                wdeps[s] = v
        own = self.CENG.get(eng)
        deps = {}
        _merge(deps, wdeps)
        _merge(deps, rdeps)
        waits = list(forced)
        hw = self.hwm[eng]
        for s, v in sorted(deps.items()):
            if v <= 0:
                continue
            if s == own and eng == "tensor":
                continue
            if hw.get(s, 0) >= v:
                continue
            waits.append((self.sem[s], v))
            hw[s] = v
        self.cnt[sem] += inc
        val = self.cnt[sem]
        semh = self.sem[sem]

        def run(e, fn=fn, waits=tuple(waits), semh=semh, inc=inc):
            for h, v in waits:
                e.wait_ge(h, v)
            fn(e).then_inc(semh, inc)
        self.ops[eng].append(run)
        for sl in reads:
            sl.r[sem] = max(sl.r.get(sem, 0), val)
        for sl in writes:
            if partial:
                sl.w[sem] = max(sl.w.get(sem, 0), val)
            else:
                sl.w = {sem: val}
                sl.we = {sem: val}
                sl.r = {}
        return (sem, val)

    def barrier_wait(self, eng, deps):
        own = self.CENG.get(eng)
        waits = []
        hw = self.hwm[eng]
        for s, v in sorted(deps.items()):
            if v <= 0 or s == own or hw.get(s, 0) >= v:
                continue
            waits.append((self.sem[s], v))
            hw[s] = v
        if waits:
            def run(e, waits=tuple(waits)):
                for h, v in waits:
                    e.wait_ge(h, v)
            self.ops[eng].append(run)


class Region:

    def __init__(self, handle_ap):
        self.h = handle_ap
        self.live = []

    def view(self, off_bytes, p0, pn, nel, dtype, name=""):
        esz = 2 if dtype == BF16 else 4
        lo, hi = off_bytes, off_bytes + nel * esz
        assert hi <= 65536, (name, lo, hi)
        sl = Slot(name)
        for (a, b_, old) in self.live:
            if a < hi and lo < b_:
                _merge(sl.w, old.w)
                _merge(sl.we, old.we)
                _merge(sl.r, old.r)
                _merge(sl.w, old.r)
                _merge(sl.we, old.r)
        self.live.append((lo, hi, sl))
        ap = self.h[p0:p0 + pn, lo // 2: hi // 2]
        if dtype != BF16:
            ap = ap.bitcast(dtype)
        return ap, sl


def build_program(n_stages=N_STAGES):
    nc = bass.Bass("TRN2", target_bir_lowering=False)
    p = Prog(nc)

    def din(name, shape):
        return nc.dram_tensor(name, list(shape), F32, kind="ExternalInput").ap()

    X_in = din("x", (B, TLOC, D))
    W0qT = din("w0qT", (D, D))
    W0kkT = din("w0kkT", (D, 2 * HD))
    W0vT = din("w0vT", (D, HD))
    W0pT = din("w0pT", (D, D))
    B0p = din("b0p", (1, D))
    SIM = din("sim", (D, NE0))
    GATES = din("gates", (1, NE0))
    WQKV0 = din("wqkv0", (D, NE0 * 3 * HD))
    WO0 = din("wo0", (NE0 * HD, D))
    W1qT = din("w1qT", (D, D))
    W1kT = din("w1kT", (D, D))
    W1vT = din("w1vT", (D, D))
    B1q = din("b1q", (1, D))
    B1k = din("b1k", (1, D))
    B1v = din("b1v", (1, D))
    W1oT = din("w1oT", (D, D))
    B1o = din("b1o", (1, D))
    GW = din("gw", (D, NE1))
    GB = din("gb", (1, NE1))
    EW = din("ew", (NE1, D, D))
    EB = din("eb", (NE1, D))
    if n_stages >= 4:
        X_out = nc.dram_tensor("out", [D, TLOC * B], F32,
                               kind="ExternalOutput").ap()
    else:
        X_out = nc.dram_tensor("out", [B, TLOC, D], F32,
                               kind="ExternalOutput").ap()

    a2aI_t = [nc.dram_tensor(f"a2a{h}_in", [NCORES, 3 * 128 * HD], BF16)
              for h in range(2)]
    a2aO_t = [nc.dram_tensor(f"a2a{h}_out", [NCORES, 3 * 128 * HD], BF16)
              for h in range(2)]
    c2aI_t = [nc.dram_tensor(f"c2a{h}_in", [NCORES, 65 * 128], BF16)
              for h in range(2)]
    c2aO_t = [nc.dram_tensor(f"c2a{h}_out", [NCORES, 65 * 128], BF16)
              for h in range(2)]
    ewst_t = nc.dram_tensor("ewst", [NE1 * NDC, 128 * D], BF16)
    w1st_t = nc.dram_tensor("w1st", [4 * NDC, 128 * D], BF16)
    w3st_t = nc.dram_tensor("w3st", [NDC, 128 * 3 * D], BF16)
    sl_a2aih = [Slot(f"a2ai{h}") for h in range(2)]
    sl_a2aoh = [Slot(f"a2ao{h}") for h in range(2)]
    sl_c2aih = [Slot(f"c2ai{h}") for h in range(2)]
    sl_c2aoh = [Slot(f"c2ao{h}") for h in range(2)]
    sl_ewst = Slot("ewst")
    sl_w1st = Slot("w1st")
    sl_w3st = Slot("w3st")

    ident_bf = nc.inline_tensor(np.eye(128, dtype=BF), "ident_bf")
    ident_f = nc.inline_tensor(np.eye(128, dtype=np.float32), "ident_f")
    ones_row = nc.inline_tensor(np.ones((1, 512), dtype=BF), "ones_row")
    ones_col = nc.inline_tensor(np.ones((128, 1), dtype=BF), "ones_col")
    selp_np = np.zeros((NE0, 8 * 128), dtype=BF)
    for j in range(8):
        selp_np[2 * j, j * 128: j * 128 + 64] = 1.0
        selp_np[2 * j + 1, j * 128 + 64: j * 128 + 128] = 1.0
    SELPD = nc.inline_tensor(selp_np, "selp")
    sel8_np = np.zeros((NE1, 8 * 128), dtype=BF)
    for e in range(NE1):
        sel8_np[e, e * 128:(e + 1) * 128] = 1.0
    SEL8D = nc.inline_tensor(sel8_np, "sel8")
    bd = np.zeros((128, 128), dtype=BF)
    for g in range(16):
        bd[g * 8:(g + 1) * 8, g * 8:(g + 1) * 8] = 1.0
    BDMASKD = nc.inline_tensor(np.tile(bd, (1, 4)), "bdmask4")
    TRILD = nc.inline_tensor(np.triu(np.ones((128, 128), dtype=np.float32))
                             .astype(BF), "triu128")

    R1h = nc.alloc_sbuf_tensor("R1", [128, 32768], BF16)
    R2h = nc.alloc_sbuf_tensor("R2", [128, 32768], BF16)
    HTRh = nc.alloc_sbuf_tensor("HTR", [128, 16384], BF16)
    HFh = nc.alloc_sbuf_tensor("HFb", [128, 16384], BF16)
    STATh = nc.alloc_sbuf_tensor("STAT", [128, 96], F32)
    JNKVh = nc.alloc_sbuf_tensor("JNKV", [128, 1024], BF16)
    JNKSh = nc.alloc_sbuf_tensor("JNKS", [128, 1024], BF16)
    GSCRh = nc.alloc_sbuf_tensor("GSCR", [128, 160], F32)
    RSh = nc.alloc_sbuf_tensor("RSc", [128, 64], F32)
    RWh = nc.alloc_sbuf_tensor("RWg", [128, NT * NE0], F32)
    WTMh = nc.alloc_sbuf_tensor("WTM", [128, NT * NE1], F32)
    QKVAh = nc.alloc_sbuf_tensor("QKVA", [128, 3 * HD], F32)
    CSTh = nc.alloc_sbuf_tensor("CSTc", [128, 1824], BF16)
    CSTFh = nc.alloc_sbuf_tensor("CSTF", [128, 128], F32)
    BIAh = nc.alloc_sbuf_tensor("BIAc", [1, 1664], BF16)
    SIMBh = nc.alloc_sbuf_tensor("SIMB", [128, NDC * NE0], BF16)
    GWSh = nc.alloc_sbuf_tensor("GWS", [128, NDC * NE1], BF16)
    STT2h = nc.alloc_sbuf_tensor("STT2", [128, 64], F32)

    R1 = Region(R1h[:])
    R2 = Region(R2h[:])
    HTR = Region(HTRh[:])
    HF = HFh[:]
    sl_HF = Slot("HF")
    sl_HFg = [[Slot(f"HF{j}_{i}") for i in range(NT)] for j in range(NDC)]

    def hf_r(j, i0, i1):
        return sl_HFg[j][i0:i1]
    STAT = STATh[:]
    sl_STAT = Slot("STAT")
    JNKV = JNKVh[:]
    sl_JNKV = Slot("JNKV")
    JNKS = JNKSh[:]
    sl_JNKS = Slot("JNKS")
    GSCR = GSCRh[:]
    sl_GSCR = Slot("GSCR")
    RS = RSh[:]
    sl_RS = Slot("RS")
    RW = RWh[:]
    sl_RW = Slot("RW")
    WTM = WTMh[:]
    sl_WTM = Slot("WTM")
    QKVA = QKVAh[:]
    sl_QKVA = Slot("QKVA")
    CST = CSTh[:]
    sl_CST = Slot("CST")
    CSTF = CSTFh[:]
    sl_CSTF = Slot("CSTF")
    BIA = BIAh[:]
    sl_BIA = Slot("BIA")
    SIMB = SIMBh[:]
    sl_SIMB = Slot("SIMB")
    GWS = GWSh[:]
    sl_GWS = Slot("GWS")
    STT2 = STT2h[:]
    sl_STT2 = Slot("STT2")

    PSh = nc.alloc_psum_tensor("PS", [128, 4096], F32)
    PS = PSh[:]
    sl_PB = [Slot(f"pb{i}") for i in range(8)]

    def bank(i, off=0, w=512):
        return PS[:, i * 512 + off:i * 512 + off + w]

    def bank_bf(i, off_bf, w_bf):
        return PS[:, i * 512 + off_bf // 2:i * 512 + (off_bf + w_bf) // 2].bitcast(BF16)

    IDENT = CST[:, 0:128]
    SELP = CSTh[0:16, 128:1152]
    SEL8 = CSTh[0:8, 128:1152]
    BDMASK = CST[:, 1152:1664]
    ONE_COL = CST[:, 1664:1665]
    TRIL = CST[:, 1696:1824]
    ONE_ROW512 = BIAh[0:1, 1152:1664]
    ONE_ROW128 = BIAh[0:1, 1152:1280]

    def dma(queue, out_ap, in_ap, reads=(), writes=(), partial=False):
        eng, sem = {"sync": ("sync", "sync_q"), "gpsimd": ("gpsimd", "gp_q")}[queue]
        prev = p.cnt[sem]
        return p.emit(eng, (lambda e, o=out_ap, i=in_ap: e.dma_start(out=o, in_=i)),
                      reads=reads, writes=writes, sem=sem, inc=16, partial=partial,
                      extra=[(sem, prev)])

    def dma_cast(out_ap, in_ap, sem, reads=(), writes=(), partial=True):
        thr = p.cnt[sem] if (p.cnt[sem] % 128 == 0) else 0
        return p.emit("gpsimd",
                      (lambda e, o=out_ap, i=in_ap: e.dma_start(out=o, in_=i)),
                      reads=reads, writes=writes, sem=sem, inc=16,
                      partial=partial, extra=[(sem, thr)])

    def mm(out_ap, lhsT, rhs, start, stop, reads, wbank):
        return p.emit("tensor",
                      (lambda e, o=out_ap, l=lhsT, r=rhs, st=start, sp=stop:
                       e.matmul(o, l, r, start=st, stop=sp)),
                      reads=reads, writes=[wbank], partial=not start)

    def tr(out_ap, in_ap, ident_ap, reads, wbank, ident_slot=None, extra=()):
        rd = list(reads) + ([ident_slot] if ident_slot else [sl_CST])
        return p.emit("tensor",
                      (lambda e, o=out_ap, i=in_ap, d=ident_ap: e.transpose(o, i, d)),
                      reads=rd, writes=[wbank], extra=extra)

    def vcopy(out_ap, in_ap, reads, writes, partial=True):
        return p.emit("vector", (lambda e, o=out_ap, i=in_ap: e.tensor_copy(o, i)),
                      reads=reads, writes=writes, partial=partial)

    def scopy(out_ap, in_ap, reads, writes, partial=True):
        return p.emit("scalar", (lambda e, o=out_ap, i=in_ap: e.copy(o, i)),
                      reads=reads, writes=writes, partial=partial)

    XAv, sl_XA = [], []
    for i in range(NT):
        a, s = R1.view(i * 4096, 0, 128, D, F32, f"XA{i}")
        XAv.append(a)
        sl_XA.append(s)

    p.emit("vector", (lambda e: e.memset(STAT[:, 0:4], 1.0)),
           reads=(), writes=[sl_STAT], partial=True)
    p.emit("vector", (lambda e: e.reciprocal(STAT[:, 4:8], STAT[:, 0:4])),
           reads=[sl_STAT], writes=[sl_STAT], partial=True)
    p.emit("vector", (lambda e: e.reciprocal(STAT[:, 4:8], STAT[:, 0:4])),
           reads=[sl_STAT], writes=[sl_STAT], partial=True)
    p.emit("vector", (lambda e: e.tensor_scalar(
        JNKV[:, 0:4], STAT[:, 0:4],
        1.0, 0.0, op0=ALU.mult, op1=ALU.add, accum_out=STAT[:, 8:9])),
        reads=[sl_STAT], writes=[sl_STAT, sl_JNKV], partial=True)
    p.emit("scalar", (lambda e: e.activation(
        JNKS[:, 0:4], STAT[:, 0:4], AF.Exp, bias=f32(-2.0), scale=f32(0.1),
        accum_out=STAT[:, 9:10])),
        reads=[sl_STAT], writes=[sl_STAT, sl_JNKS], partial=True)
    p.emit("scalar", (lambda e: e.activation(
        JNKS[:, 0:4], STAT[:, 0:4], AF.Ln, bias=f32(2.0), scale=f32(0.1))),
        reads=[sl_STAT], writes=[sl_JNKS], partial=True)

    def emit_ln_stats(Xv, sl_Xv, i):
        p.emit("vector", (lambda e, i=i: e.tensor_scalar(
            JNKV[:], Xv[i], 1.0, 0.0, op0=ALU.mult, op1=ALU.add,
            accum_out=STAT[:, i:i + 1])),
            reads=[sl_Xv[i]], writes=[sl_STAT, sl_JNKV], partial=True)
        p.emit("scalar", (lambda e, i=i: e.activation(
            JNKS[:], Xv[i], AF.Square, bias=f32(0.0), scale=f32(1.0),
            accum_out=STAT[:, 16 + i:17 + i])),
            reads=[sl_Xv[i]], writes=[sl_STAT, sl_JNKS], partial=True)

    def emit_ln(Xv, sl_Xv, tag, stats_emitted=False):
        HT, _slHT = HTR.view(0, 0, 128, NT * D, BF16, "HT_" + tag)
        sl_HTt = []
        for i in range(NT):
            _, s_ = HTR.view(i * 2048, 0, 128, D, BF16, f"HTt_{tag}_{i}")
            sl_HTt.append(s_)
        if not stats_emitted:
            for i in range(NT):
                emit_ln_stats(Xv, sl_Xv, i)
        for g4 in range(4):
            t0 = g4 * 4
            p.emit("vector", (lambda e, t0=t0: e.tensor_scalar(
                STAT[:, 32 + t0:36 + t0], STAT[:, t0:t0 + 4], 1.0 / D, None,
                op0=ALU.mult)),
                reads=[sl_STAT], writes=[sl_STAT], partial=True)
            p.emit("vector", (lambda e, t0=t0: e.tensor_tensor(
                STAT[:, 80 + t0:84 + t0], STAT[:, 32 + t0:36 + t0],
                STAT[:, 32 + t0:36 + t0], op=ALU.mult)),
                reads=[sl_STAT], writes=[sl_STAT], partial=True)
            p.emit("vector", (lambda e, t0=t0: e.scalar_tensor_tensor(
                STAT[:, 48 + t0:52 + t0], STAT[:, 16 + t0:20 + t0], 1.0 / D,
                STAT[:, 80 + t0:84 + t0], op0=ALU.mult, op1=ALU.subtract)),
                reads=[sl_STAT], writes=[sl_STAT], partial=True)
            p.emit("scalar", (lambda e, t0=t0: e.activation(
                STAT[:, 64 + t0:68 + t0], STAT[:, 48 + t0:52 + t0], AF.Ln,
                bias=f32(EPS), scale=f32(1.0))),
                reads=[sl_STAT], writes=[sl_STAT], partial=True)
            p.emit("scalar", (lambda e, t0=t0: e.activation(
                STAT[:, 64 + t0:68 + t0], STAT[:, 64 + t0:68 + t0], AF.Exp,
                bias=f32(0.0), scale=f32(-0.5))),
                reads=[sl_STAT], writes=[sl_STAT], partial=True)
        drain = [("tensor", p.cnt["tensor"])]
        for i in range(NT):
            p.emit("vector", (lambda e, i=i, HT=HT: e.tensor_scalar(
                HT[:, i * D:(i + 1) * D], Xv[i],
                STAT[:, 32 + i:33 + i], STAT[:, 64 + i:65 + i],
                op0=ALU.subtract, op1=ALU.mult)),
                reads=[sl_Xv[i], sl_STAT], writes=[sl_HTt[i]], partial=False)
            for j in range(NDC):
                k = (i * NDC + j) % 4
                pslot = bank_bf(4 + k, 0, 128)
                bsl = sl_PB[4 + k]
                tr(pslot, HT[:, i * D + j * 128:i * D + (j + 1) * 128], IDENT,
                   reads=[sl_HTt[i]], wbank=bsl,
                   extra=(drain if (i == 0 and j == 0) else ()))
                dst = HF[:, j * T + i * 128:j * T + (i + 1) * 128]
                if j < 3:
                    vcopy(dst, pslot, reads=[bsl], writes=[sl_HFg[j][i]],
                          partial=False)
                else:
                    scopy(dst, pslot, reads=[bsl], writes=[sl_HFg[j][i]],
                          partial=False)
        return HT, sl_HTt

    for i in range(NT):
        b_, half = i // 2, i % 2
        dma("sync", XAv[i], X_in[b_, half * 128:(half + 1) * 128, :],
            writes=[sl_XA[i]])
        emit_ln_stats(XAv, sl_XA, i)
    dma("sync", CST[:, 0:128], ident_bf.ap(), writes=[sl_CST], partial=True)
    dma("sync", CSTh[0:16, 128:1152], SELPD.ap(), writes=[sl_CST], partial=True)
    dma("sync", CST[:, 1152:1664], BDMASKD.ap(), writes=[sl_CST], partial=True)
    dma("sync", CST[:, 1664:1665], ones_col.ap(), writes=[sl_CST], partial=True)
    dma("sync", CST[:, 1696:1824], TRILD.ap(), writes=[sl_CST], partial=True)
    dma("sync", CSTF[:, 0:128], ident_f.ap(), writes=[sl_CSTF])
    dma("sync", BIAh[0:1, 1152:1664], ones_row.ap(), writes=[sl_BIA], partial=True)

    HT1, sl_HT1 = emit_ln(XAv, sl_XA, "l0a", stats_emitted=True)
    if n_stages == 0:
        for i in range(NT):
            p.emit("vector", (lambda e, i=i: e.tensor_copy(
                XAv[i], HT1[:, i * D:(i + 1) * D])),
                reads=[sl_HT1[i]], writes=[sl_XA[i]])
        p.emit("vector", (lambda e: e.tensor_copy(
            XAv[15][:, 900:996], STAT[:])),
            reads=[sl_STAT], writes=[sl_XA[15]], partial=True)
    if n_stages == -1:
        pass
    if n_stages <= 0:
        for i in range(NT):
            b_, half = i // 2, i % 2
            dma("sync", X_out[b_, half * 128:(half + 1) * 128, :], XAv[i],
                reads=[sl_XA[i]], writes=())
        p.barrier_wait("sync", {"sync_q": p.cnt["sync_q"]})
        with nc.Block() as block:
            @block.tensor
            def _(t):
                for f in p.ops["tensor"]:
                    f(t)

            @block.vector
            def _(v):
                for f in p.ops["vector"]:
                    f(v)

            @block.scalar
            def _(s):
                for f in p.ops["scalar"]:
                    f(s)

            @block.gpsimd
            def _(g):
                for f in p.ops["gpsimd"]:
                    f(g)

            @block.sync
            def _(sy):
                for f in p.ops["sync"]:
                    f(sy)
        return nc

    WQTc, sl_WQTc, WPTc, sl_WPTc = [], [], [], []
    for j in range(NDC):
        a_, s_ = HTR.view(j * 2048, 0, 128, 1024, BF16, f"WQT{j}")
        WQTc.append(a_)
        sl_WQTc.append(s_)
    for j in range(NDC):
        a_, s_ = HTR.view(16384 + j * 2048, 0, 128, 1024, BF16, f"WPT{j}")
        WPTc.append(a_)
        sl_WPTc.append(s_)
    QWv, sl_QW = [], []
    for s in range(2):
        a, sl = R2.view(s * 4096, 0, 128, T, BF16, f"QW{s}")
        QWv.append(a)
        sl_QW.append(sl)
    KKF, sl_KKF = R2.view(8192, 0, 128, T, BF16, "KKF")
    VF, sl_VF = R2.view(12288, 0, 64, T, BF16, "VF")
    VT0, sl_VT0 = R2.view(16384, 0, 128, NT * 65, BF16, "VT0")
    ETv, sl_ET = [], []
    for s in range(2):
        a, sl = R2.view(18464 + s * 2048, 0, 128, 1024, BF16, f"ET{s}")
        ETv.append(a)
        sl_ET.append(sl)
    CTMv, sl_CTM = [], []
    for s in range(2):
        a_, s_ = R2.view(22560 if s == 0 else 58432, 0, 128, 256, BF16,
                         f"CTM{s}")
        CTMv.append(a_)
        sl_CTM.append(s_)
    KKW, sl_KKW = R2.view(23072, 0, 128, 1024, BF16, "KKW")
    KVW2, sl_KVW2 = R2.view(25120, 0, 128, 512, BF16, "KVW2")
    CTXF, sl_CTXF = R2.view(25632, 0, 128, NDC * T, BF16, "CTXF")
    p.emit("vector", (lambda e: e.memset(
        VT0[:].rearrange("p (i n) -> p i n", n=65)[:, :, 64], 1.0)),
        reads=(), writes=[sl_VT0], partial=False)
    dma("gpsimd", KKW[:].rearrange("p (j n) -> p j n", j=8),
        W0kkT.rearrange("(j p) n -> p j n", j=8), writes=[sl_KKW])
    dma("gpsimd", KVW2[:].rearrange("p (j n) -> p j n", j=8),
        W0vT.rearrange("(j p) n -> p j n", j=8), writes=[sl_KVW2])
    dma("gpsimd", BIAh[0:1, 0:1024], B0p, writes=[sl_BIA], partial=True)
    B0P_S = BIAh[0:1, 0:1024]
    for j in range(NDC):
        dma("gpsimd", WQTc[j][:], W0qT[j * 128:(j + 1) * 128, :],
            writes=[sl_WQTc[j]])
    for j in range(NDC):
        dma("gpsimd", WPTc[j][:], W0pT[j * 128:(j + 1) * 128, :],
            writes=[sl_WPTc[j]])
    w3st = w3st_t.ap()
    w1stA = w1st_t.ap()
    if n_stages >= 2:
        for j in range(NDC):
            dma_cast(w3st[j, :], WQKV0[j * 128:(j + 1) * 128, :], "w3c",
                     writes=[sl_w3st], partial=(j > 0))

    for tt in range(4):
        bk = 4
        for j in range(NDC):
            mm(bank(bk), KKW[:, j * 128:(j + 1) * 128],
               HF[:, j * T + tt * 512:j * T + (tt + 1) * 512],
               start=(j == 0), stop=(j == NDC - 1),
               reads=[sl_KKW] + hf_r(j, 4 * tt, 4 * tt + 4), wbank=sl_PB[bk])
        vcopy(KKF[:, tt * 512:(tt + 1) * 512], bank(bk),
              reads=[sl_PB[bk]], writes=[sl_KKF])
    for tt in range(4):
        bk = 5
        for j in range(NDC):
            mm(bank(bk)[0:64, :], KVW2[:, j * 64:(j + 1) * 64],
               HF[:, j * T + tt * 512:j * T + (tt + 1) * 512],
               start=(j == 0), stop=(j == NDC - 1),
               reads=[sl_KVW2] + hf_r(j, 4 * tt, 4 * tt + 4), wbank=sl_PB[bk])
        vcopy(VF[:, tt * 512:(tt + 1) * 512], bank(bk)[0:64, :],
              reads=[sl_PB[bk]], writes=[sl_VF])
    for i in range(NT):
        k = i % 4
        pslot = bank_bf(6 + k // 2, (k % 2) * 128, 64)
        bsl = sl_PB[6 + k // 2]
        tr(pslot, VF[0:64, i * 128:(i + 1) * 128], IDENT[0:64, 0:64],
           reads=[sl_VF], wbank=bsl)
        vcopy(VT0[:, i * 65:i * 65 + 64], pslot,
              reads=[bsl], writes=[sl_VT0])

    for hp in range(NDC):
        qs = hp % 2
        qsl = QWv[qs]
        for tt in range(4):
            bk = tt % 2
            for j in range(NDC):
                mm(bank(bk), WQTc[j][:, hp * 128:(hp + 1) * 128],
                   HF[:, j * T + tt * 512:j * T + (tt + 1) * 512],
                   start=(j == 0), stop=(j == NDC - 1),
                   reads=[sl_WQTc[j]] + hf_r(j, 4 * tt, 4 * tt + 4),
                   wbank=sl_PB[bk])
            p.emit("scalar", (lambda e, qsl=qsl, tt=tt, bk=bk: e.mul(
                qsl[:, tt * 512:(tt + 1) * 512], bank(bk), SCALE)),
                reads=[sl_PB[bk]], writes=[sl_QW[qs]], partial=(tt > 0))
        def l0_scores(b_):
            es = b_ % 2
            sb0 = 2 + 3 * es
            for h2 in range(2):
                for tkc in range(2):
                    out = PS[:, (sb0 + h2) * 512 + tkc * 256:
                             (sb0 + h2) * 512 + tkc * 256 + 256]
                    mm(out,
                       KKF[h2 * 64:h2 * 64 + 64,
                           b_ * 256 + tkc * 128:b_ * 256 + (tkc + 1) * 128],
                       qsl[h2 * 64:h2 * 64 + 64, b_ * 256:(b_ + 1) * 256],
                       start=True, stop=True,
                       reads=[sl_KKF, sl_QW[qs]], wbank=sl_PB[sb0 + h2])
            p.emit("scalar", (lambda e, es=es, sb0=sb0: e.activation(
                ETv[es], PS[:, sb0 * 512:sb0 * 512 + 1024],
                AF.Exp, bias=EXPB, scale=f32(1.0))),
                reads=[sl_PB[sb0], sl_PB[sb0 + 1]], writes=[sl_ET[es]])

        def l0_av(b_):
            es = b_ % 2
            ab = 4 if es == 0 else 7
            for h2 in range(2):
                for tqc in range(2):
                    for tkc in range(2):
                        lhs = ETv[es][:, h2 * 512 + tkc * 256 + tqc * 128:
                                      h2 * 512 + tkc * 256 + (tqc + 1) * 128]
                        mm(bank(ab, (h2 * 2 + tqc) * 65, 65), lhs,
                           VT0[:, (b_ * 2 + tkc) * 65:(b_ * 2 + tkc) * 65 + 65],
                           start=(tkc == 0), stop=(tkc == 1),
                           reads=[sl_ET[es], sl_VT0], wbank=sl_PB[ab])
            p.emit("vector", (lambda e, ab=ab, es=es: e.reciprocal(
                RS[:, es * 4:es * 4 + 4],
                bank(ab, 0, 260).rearrange("p (i n) -> p i n", n=65)[:, :, 64])),
                reads=[sl_PB[ab]], writes=[sl_RS], partial=True)
            for h2 in range(2):
                for tqc in range(2):
                    p.emit("vector", (lambda e, h2=h2, tqc=tqc, es=es, ab=ab:
                                      e.tensor_scalar(
                        CTMv[es][:, tqc * 128 + h2 * 64:tqc * 128 + h2 * 64 + 64],
                        bank(ab, (h2 * 2 + tqc) * 65, 64),
                        RS[:, es * 4 + h2 * 2 + tqc:es * 4 + h2 * 2 + tqc + 1],
                        None, op0=ALU.mult)),
                        reads=[sl_PB[ab], sl_RS], writes=[sl_CTM[es]],
                        partial=True)

        def l0_tr(b_):
            es = b_ % 2
            for tqc in range(2):
                pslot = bank_bf(tqc, 768, 128)
                bsl = sl_PB[tqc]
                tr(pslot, CTMv[es][:, tqc * 128:(tqc + 1) * 128], IDENT,
                   reads=[sl_CTM[es]], wbank=bsl)
                dst = CTXF[:, hp * T + b_ * 256 + tqc * 128:
                           hp * T + b_ * 256 + (tqc + 1) * 128]
                scopy(dst, pslot, reads=[bsl], writes=[sl_CTXF])

        l0_scores(0)
        for b_ in range(B):
            if b_ + 1 < B:
                l0_scores(b_ + 1)
            l0_av(b_)
            if b_ >= 1:
                l0_tr(b_ - 1)
        l0_tr(B - 1)
    for i in range(NT):
        for dh in range(2):
            bk = (i * 2 + dh) % 2
            mm(bank(bk), ONE_ROW128, B0P_S[0:1, dh * 512:(dh + 1) * 512],
               start=True, stop=False, reads=[sl_BIA], wbank=sl_PB[bk])
            for j in range(NDC):
                mm(bank(bk), CTXF[:, j * T + i * 128:j * T + (i + 1) * 128],
                   WPTc[j][:, dh * 512:(dh + 1) * 512],
                   start=False, stop=(j == NDC - 1),
                   reads=[sl_CTXF, sl_WPTc[j]], wbank=sl_PB[bk])
            p.emit("vector", (lambda e, i=i, dh=dh, bk=bk: e.scalar_tensor_tensor(
                XAv[i][:, dh * 512:(dh + 1) * 512], bank(bk), 1.0,
                XAv[i][:, dh * 512:(dh + 1) * 512],
                op0=ALU.mult, op1=ALU.add)),
                reads=[sl_PB[bk], sl_XA[i]], writes=[sl_XA[i]], partial=True)
        if n_stages >= 2:
            emit_ln_stats(XAv, sl_XA, i)

    if n_stages >= 2:
        HT2, sl_HT2 = emit_ln(XAv, sl_XA, "l0f", stats_emitted=True)
        W3c, sl_W3c = [], []
        for j in range(NDC):
            a, s = R2.view(j * 6144, 0, 128, 3072, BF16, f"W3_{j}")
            W3c.append(a)
            sl_W3c.append(s)
        QT0t, sl_QT0t, KT0t, sl_KT0t, VTft, sl_VTft = [], [], [], [], [], []
        for i in range(NT):
            a, s = R2.view(49152 + i * 128, 0, 128, HD, BF16, f"QT0_{i}")
            QT0t.append(a)
            sl_QT0t.append(s)
            a, s = R2.view(51200 + i * 128, 0, 128, HD, BF16, f"KT0_{i}")
            KT0t.append(a)
            sl_KT0t.append(s)
            a, s = R2.view(53248 + i * 128, 0, 128, HD, BF16, f"VTf_{i}")
            VTft.append(a)
            sl_VTft.append(s)
        QKVsrc = R2h[:, 24576:27648]
        SQS, sl_SQS = R2.view(55296, 0, 128, 128, BF16, "SQS")
        for j in range(NDC):
            dma("sync" if j % 2 == 0 else "gpsimd", W3c[j],
                w3st[j, :].rearrange("(p n) -> p n", p=128),
                reads=[sl_w3st], writes=[sl_W3c[j]])
        dma("gpsimd", SIMB[:].rearrange("p (j n) -> p j n", j=8),
            SIM.rearrange("(j p) n -> p j n", j=8), writes=[sl_SIMB])
        if n_stages >= 3:
            for j in range(NDC):
                dma_cast(w1stA[2 * NDC + j, :],
                         W1kT[j * 128:(j + 1) * 128, :], "w1c",
                         writes=[sl_w1st], partial=True)
        dma("sync", STT2[0:1, 0:16], GATES, writes=[sl_STT2], partial=True)
        p.emit("vector", (lambda e: e.tensor_tensor(
            SQS[:], SIMB[:], SIMB[:], op=ALU.mult)),
            reads=[sl_SIMB], writes=[sl_SQS])
        for j in range(NDC):
            mm(bank(7, 0, 16)[0:1, :], ONE_COL, SQS[:, j * 16:(j + 1) * 16],
               start=(j == 0), stop=(j == NDC - 1),
               reads=[sl_SQS, sl_CST], wbank=sl_PB[7])
        p.emit("scalar", (lambda e: e.activation(
            STT2[0:1, 16:32], bank(7, 0, 16)[0:1, :], AF.Ln,
            bias=f32(1e-12), scale=f32(1.0))),
            reads=[sl_PB[7]], writes=[sl_STT2], partial=True)
        p.emit("scalar", (lambda e: e.activation(
            STT2[0:1, 16:32], STT2[0:1, 16:32], AF.Exp, bias=f32(0.0),
            scale=f32(-0.5))),
            reads=[sl_STT2], writes=[sl_STT2], partial=True)
        p.emit("scalar", (lambda e: e.mul(
            STT2[0:1, 16:32], STT2[0:1, 16:32], f32(1.0 / 32.0))),
            reads=[sl_STT2], writes=[sl_STT2], partial=True)
        p.emit("scalar", (lambda e: e.activation(
            STT2[0:1, 32:48], STT2[0:1, 0:16], AF.Exp, bias=f32(0.0),
            scale=f32(-1.0))),
            reads=[sl_STT2], writes=[sl_STT2], partial=True)
        p.emit("vector", (lambda e: e.tensor_scalar(
            STT2[0:1, 32:48], STT2[0:1, 32:48], 1.0, None, op0=ALU.add)),
            reads=[sl_STT2], writes=[sl_STT2], partial=True)
        p.emit("vector", (lambda e: e.reciprocal(
            STT2[0:1, 32:48], STT2[0:1, 32:48])),
            reads=[sl_STT2], writes=[sl_STT2], partial=True)
        p.emit("vector", (lambda e: e.tensor_copy(
            BIAh[0:1, 1024:1040], STT2[0:1, 16:32])),
            reads=[sl_STT2], writes=[sl_BIA], partial=True)
        p.emit("vector", (lambda e: e.tensor_copy(
            BIAh[0:1, 1040:1056], STT2[0:1, 32:48])),
            reads=[sl_STT2], writes=[sl_BIA], partial=True)
        mm(bank(7, 16, 16), ONE_ROW128, BIAh[0:1, 1024:1040],
           start=True, stop=True, reads=[sl_BIA], wbank=sl_PB[7])
        mm(bank(7, 32, 16), ONE_ROW128, BIAh[0:1, 1040:1056],
           start=True, stop=True, reads=[sl_BIA], wbank=sl_PB[7])
        p.emit("vector", (lambda e: e.tensor_copy(
            GSCR[:, 128:144], bank(7, 16, 16))),
            reads=[sl_PB[7]], writes=[sl_GSCR], partial=True)
        p.emit("vector", (lambda e: e.tensor_copy(
            GSCR[:, 144:160], bank(7, 32, 16))),
            reads=[sl_PB[7]], writes=[sl_GSCR], partial=True)
        RSNB = GSCR[:, 128:144]
        SGB = GSCR[:, 144:160]
        p.emit("vector", (lambda e: e.tensor_tensor(
            RS[:, 16:32], STAT[:, 64:80], STAT[:, 64:80], op=ALU.mult)),
            reads=[sl_STAT], writes=[sl_RS], partial=True)
        p.emit("vector", (lambda e: e.tensor_tensor(
            RS[:, 32:48], STAT[:, 48:64], RS[:, 16:32], op=ALU.mult)),
            reads=[sl_STAT, sl_RS], writes=[sl_RS], partial=True)
        p.emit("scalar", (lambda e: e.activation(
            RS[:, 32:48], RS[:, 32:48], AF.Ln, bias=f32(0.0), scale=f32(1.0))),
            reads=[sl_RS], writes=[sl_RS], partial=True)
        p.emit("scalar", (lambda e: e.activation(
            RS[:, 32:48], RS[:, 32:48], AF.Exp, bias=f32(0.0), scale=f32(-0.5))),
            reads=[sl_RS], writes=[sl_RS], partial=True)
        RHINV = RS[:, 32:48]

        LG = GSCR[:, 0:16]
        MK = GSCR[:, 16:32]
        TMP = GSCR[:, 32:48]
        GTD = GSCR[:, 48:64]
        M1 = GSCR[:, 64:65]
        M2 = GSCR[:, 65:66]
        SEL = GSCR[:, 66:67]
        NMX = GSCR[:, 67:68]
        SUM = GSCR[:, 68:69]
        RSM = GSCR[:, 69:70]
        FB = GSCR[:, 80:96]
        UU = GSCR[:, 96:112]
        AM = GSCR[:, 112:128]
        a2aI = [t.ap() for t in a2aI_t]
        a2aO = [t.ap() for t in a2aO_t]
        for ti in range(NT):
            tc = (ti * 2) % NT + (ti * 2) // NT
            gb = 6 + (ti % 2)
            for j in range(NDC):
                mm(bank(gb, 0, 16), HF[:, j * T + tc * 128:j * T + (tc + 1) * 128],
                   SIMB[:, j * 16:(j + 1) * 16],
                   start=(j == 0), stop=(j == NDC - 1),
                   reads=[sl_HFg[j][tc], sl_SIMB], wbank=sl_PB[gb])
            p.emit("vector", (lambda e, tc=tc, gb=gb: e.scalar_tensor_tensor(
                LG, bank(gb, 0, 16), RHINV[:, tc:tc + 1], RSNB,
                op0=ALU.mult, op1=ALU.mult)),
                reads=[sl_PB[gb], sl_RS, sl_GSCR], writes=[sl_GSCR], partial=True)
            p.emit("vector", (lambda e: e.tensor_tensor(
                LG, LG, SGB, op=ALU.subtract)),
                reads=[sl_GSCR], writes=[sl_GSCR], partial=True)
            p.emit("vector", (lambda e: e.tensor_scalar(
                TMP, LG, 1.0, -3e38, op0=ALU.mult, op1=ALU.max, accum_out=M1)),
                reads=[sl_GSCR], writes=[sl_GSCR], partial=True)
            p.emit("vector", (lambda e: e.tensor_scalar(
                MK, LG, 0.0, None, op0=ALU.is_gt)),
                reads=[sl_GSCR], writes=[sl_GSCR], partial=True)
            p.emit("vector", (lambda e: e.tensor_scalar(
                GTD, LG, 0.0, None, op0=ALU.max)),
                reads=[sl_GSCR], writes=[sl_GSCR], partial=True)
            p.emit("vector", (lambda e: e.tensor_scalar(
                AM, LG, M1, None, op0=ALU.is_ge)),
                reads=[sl_GSCR], writes=[sl_GSCR], partial=True)
            p.emit("vector", (lambda e: e.scalar_tensor_tensor(
                TMP, AM, -2e9, LG, op0=ALU.mult, op1=ALU.add)),
                reads=[sl_GSCR], writes=[sl_GSCR], partial=True)
            p.emit("vector", (lambda e: e.tensor_scalar(
                TMP, TMP, 1.0, -3e38, op0=ALU.mult, op1=ALU.max,
                accum_out=M2)),
                reads=[sl_GSCR], writes=[sl_GSCR], partial=True)
            p.emit("vector", (lambda e: e.tensor_scalar(
                FB, LG, M2, None, op0=ALU.is_ge)),
                reads=[sl_GSCR], writes=[sl_GSCR], partial=True)
            p.emit("vector", (lambda e: e.tensor_scalar(
                SEL, M1, 0.0, None, op0=ALU.is_le)),
                reads=[sl_GSCR], writes=[sl_GSCR], partial=True)
            p.emit("vector", (lambda e: e.tensor_scalar(
                FB, FB, SEL, None, op0=ALU.mult)),
                reads=[sl_GSCR], writes=[sl_GSCR], partial=True)
            p.emit("vector", (lambda e: e.tensor_tensor(
                MK, MK, FB, op=ALU.add)),
                reads=[sl_GSCR], writes=[sl_GSCR], partial=True)
            p.emit("vector", (lambda e: e.tensor_scalar(
                UU, MK, 1e9, -1e9, op0=ALU.mult, op1=ALU.add)),
                reads=[sl_GSCR], writes=[sl_GSCR], partial=True)
            p.emit("vector", (lambda e: e.tensor_tensor(
                UU, UU, GTD, op=ALU.add)),
                reads=[sl_GSCR], writes=[sl_GSCR], partial=True)
            p.emit("vector", (lambda e: e.tensor_scalar(
                TMP, UU, 1.0, -3e38, op0=ALU.mult, op1=ALU.max,
                accum_out=NMX)),
                reads=[sl_GSCR], writes=[sl_GSCR], partial=True)
            p.emit("vector", (lambda e: e.tensor_scalar(
                NMX, NMX, -1.0, None, op0=ALU.mult)),
                reads=[sl_GSCR], writes=[sl_GSCR], partial=True)
            p.emit("scalar", (lambda e: e.activation(
                UU, UU, AF.Exp, bias=NMX, scale=f32(1.0), accum_out=SUM)),
                reads=[sl_GSCR], writes=[sl_GSCR], partial=True)
            p.emit("vector", (lambda e: e.reciprocal(RSM, SUM)),
                   reads=[sl_GSCR], writes=[sl_GSCR], partial=True)
            p.emit("vector", (lambda e, tc=tc: e.tensor_scalar(
                RW[:, tc * 16:(tc + 1) * 16], UU, RSM, None, op0=ALU.mult)),
                reads=[sl_GSCR], writes=[sl_RW], partial=True)
            for hf2 in range(2):
                for j in range(NDC):
                    lhs = HF[:, j * T + tc * 128:j * T + (tc + 1) * 128]
                    for s3 in range(3):
                        s = hf2 * 3 + s3
                        mm(bank(s), lhs,
                           W3c[j][:, s * 512:(s + 1) * 512],
                           start=(j == 0), stop=(j == NDC - 1),
                           reads=[sl_HFg[j][tc], sl_W3c[j]], wbank=sl_PB[s])
                for e8 in range(8):
                    e_ = hf2 * 8 + e8
                    src = PS[:, e_ * 192:(e_ + 1) * 192]
                    rbs = [sl_PB[(e_ * 192) // 512],
                           sl_PB[((e_ + 1) * 192 - 1) // 512]]
                    if e_ == 0:
                        p.emit("vector", (lambda e, src=src, tc=tc: e.tensor_scalar(
                            QKVA[:], src, RW[:, tc * 16:tc * 16 + 1], None,
                            op0=ALU.mult)),
                            reads=rbs + [sl_RW], writes=[sl_QKVA])
                    else:
                        p.emit("vector", (lambda e, src=src, tc=tc, e_=e_:
                                          e.scalar_tensor_tensor(
                            QKVA[:], src, RW[:, tc * 16 + e_:tc * 16 + e_ + 1],
                            QKVA[:], op0=ALU.mult, op1=ALU.add)),
                            reads=rbs + [sl_RW, sl_QKVA], writes=[sl_QKVA],
                            partial=True)
            p.emit("scalar", (lambda e, tc=tc: e.mul(
                QT0t[tc][:], QKVA[:, 0:64], SCALE)),
                reads=[sl_QKVA], writes=[sl_QT0t[tc]], partial=False)
            scopy(KT0t[tc][:], QKVA[:, 64:128],
                  reads=[sl_QKVA], writes=[sl_KT0t[tc]], partial=False)
            scopy(VTft[tc][:], QKVA[:, 128:192],
                  reads=[sl_QKVA], writes=[sl_VTft[tc]], partial=False)
            h_, j_ = tc % 2, tc // 2
            dma("sync",
                a2aI[h_][j_, :].rearrange("(blk p n) -> blk p n",
                                          blk=3, p=128).transpose([1, 0, 2]),
                QKVsrc.rearrange("p (blk n) -> p blk n",
                                 blk=3)[:, :, tc * HD:(tc + 1) * HD],
                reads=[sl_QT0t[tc], sl_KT0t[tc], sl_VTft[tc]],
                writes=[sl_a2aih[h_]], partial=(j_ > 0))
            if ti == NT // 2 - 1 or ti == NT - 1:
                hh = ti // (NT // 2)

                def cc_a2a(e, hh=hh):
                    with allow_ici_all_to_all():
                        return e.collective_compute(
                            "AllToAll", ALU.bypass,
                            replica_groups=[list(range(NCORES))],
                            ins=[a2aI_t[hh].ap().opt()],
                            outs=[a2aO_t[hh].ap().opt()])
                p.emit("gpsimd", cc_a2a, reads=[sl_a2aih[hh]],
                       writes=[sl_a2aoh[hh]], sem="cc", inc=1)

        GKVv, sl_GKVh = [], []
        for h in range(2):
            a_, s_ = R2.view(0, 0, 128, 3088, BF16, f"GKV{h}")
            GKVv.append(a_)
            sl_GKVh.append(s_)
        GKV = GKVv[0]
        p.emit("vector", (lambda e: e.memset(GKV[:, 2048:3088].rearrange(
            "p (i n) -> p i n", n=65)[:, :, 64], 1.0)),
            reads=(), writes=[sl_GKVh[0], sl_GKVh[1]], partial=False)
        QFF, sl_QFF0 = R2.view(6176, 0, 64, T, BF16, "QFF")
        KFF, sl_KFF0 = R2.view(10272, 0, 64, T, BF16, "KFF")
        sl_QFFt = [Slot(f"QFF{i}") for i in range(NT)]
        sl_KFFt = [Slot(f"KFF{i}") for i in range(NT)]
        for i in range(NT):
            _merge(sl_QFFt[i].w, sl_QFF0.w)
            _merge(sl_QFFt[i].we, sl_QFF0.we)
            _merge(sl_QFFt[i].r, sl_QFF0.r)
            _merge(sl_KFFt[i].w, sl_KFF0.w)
            _merge(sl_KFFt[i].we, sl_KFF0.we)
            _merge(sl_KFFt[i].r, sl_KFF0.r)
        ET2v, sl_ET2v = [], []
        for s in range(3):
            a_, s_ = R2.view((16416 + s * 2048) if s < 2 else 28704,
                             0, 128, 1024, BF16, f"ET2_{s}")
            ET2v.append(a_)
            sl_ET2v.append(s_)
        for h in range(2):
            for blk in range(3):
                if blk < 2:
                    dst = GKVv[h][:, blk * 1024:(blk + 1) * 1024].rearrange(
                        "p (r two n) -> p r two n", two=2, n=64)[:, :, h, :]
                else:
                    dst = GKVv[h][:, 2048:3088].rearrange(
                        "p (r two n) -> p r two n", two=2, n=65)[:, :, h, 0:64]
                srcb = a2aO[h][:, blk * 8192:(blk + 1) * 8192].rearrange(
                    "r (pp n) -> r pp n", pp=128)
                dma("gpsimd", dst, srcb.transpose([1, 0, 2]),
                    reads=[sl_a2aoh[h]], writes=[sl_GKVh[h]],
                    partial=True)
        if n_stages >= 3:
            for j in range(NDC):
                dma_cast(w1stA[3 * NDC + j, :],
                         W1vT[j * 128:(j + 1) * 128, :], "w1c",
                         writes=[sl_w1st], partial=True)
            for j in range(NDC):
                dma_cast(w1stA[j, :], W1qT[j * 128:(j + 1) * 128, :], "w1c",
                         writes=[sl_w1st], partial=True)
            for j in range(NDC):
                dma_cast(w1stA[NDC + j, :], W1oT[j * 128:(j + 1) * 128, :],
                         "w1c", writes=[sl_w1st], partial=True)
        if n_stages >= 4:
            for e_ in range(NE1):
                for j in range(NDC):
                    dma_cast(ewst_t.ap()[e_ * NDC + j, :],
                             EW[e_, j * 128:(j + 1) * 128, :], "ewc",
                             writes=[sl_ewst], partial=True)
        RWF, sl_RWF = R2.view(61440, 0, 16, T, BF16, "RWF")
        for tc in range(NT):
            k = tc % 2
            pslot = bank(0, k * 128, 128)
            bsl = sl_PB[0]
            tr(pslot[0:16, :], RW[:, tc * 16:(tc + 1) * 16], CSTF[:, 0:128],
               reads=[sl_RW], wbank=bsl, ident_slot=sl_CSTF)
            vcopy(RWF[0:16, tc * 128:(tc + 1) * 128], pslot[0:16, :],
                  reads=[bsl], writes=[sl_RWF])
        WOS, sl_WOS = R2.view(45056, 0, 128, 8192, BF16, "WOS")
        for j in range(NDC):
            dma("gpsimd", WOS[:, j * 1024:(j + 1) * 1024],
                WO0[j * 128:(j + 1) * 128, :], writes=[sl_WOS], partial=(j > 0))
        for i in [x for x in range(NT) if x % 2 == 0] + \
                 [x for x in range(NT) if x % 2 == 1]:
            k = i % 2
            pslot = bank_bf(6, k * 128, 128)[0:64, :]
            bsl = sl_PB[6]
            tr(pslot, GKV[:, i * HD:(i + 1) * HD], IDENT,
               reads=[sl_GKVh[i % 2]], wbank=bsl)
            vcopy(QFF[0:64, i * 128:(i + 1) * 128], pslot,
                  reads=[bsl], writes=[sl_QFFt[i]], partial=False)
            pslot2 = bank_bf(7, k * 128, 128)[0:64, :]
            bsl2 = sl_PB[7]
            tr(pslot2, GKV[:, 1024 + i * HD:1024 + (i + 1) * HD], IDENT,
               reads=[sl_GKVh[i % 2]], wbank=bsl2)
            scopy(KFF[0:64, i * 128:(i + 1) * 128], pslot2,
                  reads=[bsl2], writes=[sl_KFFt[i]], partial=False)

        c2aI = [t.ap() for t in c2aI_t]
        c2aO = [t.ap() for t in c2aO_t]
        CTX65, sl_CTX65 = R2.view(20512, 0, 65, 4096, BF16, "CTX65")
        qgroups = [[0, 2, 4, 6], [8, 10, 12, 14], [1, 3, 5, 7], [9, 11, 13, 15]]
        for gi, grp in enumerate(qgroups):
            t0 = grp[0]
            cb = 4 + gi % 2
            rhsq = QFF[0:64, :].rearrange(
                "p (a two n) -> p a two n", two=2,
                n=128)[:, t0 // 2:t0 // 2 + 4, t0 % 2, :]
            ntk = grp[-1] + 1
            first_av = True
            wl = [list(range(w0, min(w0 + 2, ntk))) for w0 in range(0, ntk, 2)]
            for wvi, wtk in enumerate(wl):
                pp = wvi % 3
                et = ET2v[pp]
                set_ = sl_ET2v[pp]
                for wi, tkc in enumerate(wtk):
                    bkx = (0, 2, 6)[pp] + wi
                    mm(bank(bkx), KFF[0:64, tkc * 128:(tkc + 1) * 128], rhsq,
                       start=True, stop=True,
                       reads=[sl_KFFt[tkc]] + [sl_QFFt[t] for t in grp],
                       wbank=sl_PB[bkx])
                b0 = (0, 2, 6)[pp]
                p.emit("scalar", (lambda e, b0=b0, et=et, wn=len(wtk):
                                  e.activation(
                    et[:, 0:wn * 512], PS[:, b0 * 512:b0 * 512 + wn * 512],
                    AF.Exp, bias=EXPB, scale=f32(1.0))),
                    reads=[sl_PB[b0], sl_PB[b0 + 1]],
                    writes=[set_], partial=False)
                for wi, tkc in enumerate(wtk):
                    for s, tq in enumerate(grp):
                        if tq < tkc:
                            p.emit("vector", (lambda e, wi=wi, s=s, et=et: e.memset(
                                et[:, wi * 512 + s * 128:wi * 512 + (s + 1) * 128],
                                0.0)),
                                reads=[set_], writes=[set_], partial=True)
                        elif tq == tkc:
                            p.emit("vector", (lambda e, wi=wi, s=s, et=et:
                                              e.tensor_tensor(
                                et[:, wi * 512 + s * 128:wi * 512 + (s + 1) * 128],
                                et[:, wi * 512 + s * 128:wi * 512 + (s + 1) * 128],
                                TRIL, op=ALU.mult)),
                                reads=[set_, sl_CST], writes=[set_],
                                partial=True)
                for wi, tkc in enumerate(wtk):
                    mm(PS[0:65, cb * 512:(cb + 1) * 512],
                       GKV[:, 2048 + tkc * 65:2048 + (tkc + 1) * 65],
                       et[:, wi * 512:(wi + 1) * 512],
                       start=first_av, stop=(tkc == ntk - 1),
                       reads=[sl_GKVh[tkc % 2], set_], wbank=sl_PB[cb])
                    first_av = False
            p.emit("vector", (lambda e, t0=t0, cb=cb: e.tensor_copy(
                CTX65[0:65, :].rearrange(
                    "p (a two n) -> p a two n", two=2,
                    n=128)[:, t0 // 2:t0 // 2 + 4, t0 % 2, :],
                PS[0:65, cb * 512:(cb + 1) * 512].rearrange(
                    "p (s n) -> p s n", n=128))),
                reads=[sl_PB[cb]], writes=[sl_CTX65], partial=(gi > 0))
            for s, tq in enumerate(grp):
                h_, j_ = tq % 2, tq // 2
                dma("sync", c2aI[h_][j_, :].rearrange("(p n) -> p n", p=65),
                    CTX65[0:65, tq * 128:(tq + 1) * 128],
                    reads=[sl_CTX65], writes=[sl_c2aih[h_]], partial=(j_ > 0))
            if gi in (1, 3):
                hh = gi // 2 if gi == 1 else 1
                hh = 0 if gi == 1 else 1

                def cc_c2a(e, hh=hh):
                    with allow_ici_all_to_all():
                        return e.collective_compute(
                            "AllToAll", ALU.bypass,
                            replica_groups=[list(range(NCORES))],
                            ins=[c2aI_t[hh].ap().opt()],
                            outs=[c2aO_t[hh].ap().opt()])
                p.emit("gpsimd", cc_c2a, reads=[sl_c2aih[hh]],
                       writes=[sl_c2aoh[hh]], sem="cc", inc=1)

        C2F, sl_C2F = R2.view(4096, 0, 128, T, BF16, "C2F")
        SUMR, sl_SUMR = R2.view(24608, 0, 1, T, BF16, "SUMR")
        for h in range(2):
            srcb = c2aO[h][:, :].rearrange("r (p n) -> r p n", p=65)
            dma("gpsimd",
                C2F[0:64, :].rearrange("p (r two n) -> p r two n",
                                       two=2, n=128)[:, :, h, :],
                srcb[:, 0:64, :].transpose([1, 0, 2]),
                reads=[sl_c2aoh[h]], writes=[sl_C2F], partial=(h > 0))
            dma("gpsimd",
                SUMR[0:1, :].rearrange("p (r two n) -> p r two n",
                                       two=2, n=128)[:, :, h, :],
                srcb[:, 64:65, :].transpose([1, 0, 2]),
                reads=[sl_c2aoh[h]], writes=[sl_SUMR], partial=(h > 0))
        dma("sync", C2F[64:128, :], C2F[0:64, :],
            reads=[sl_C2F], writes=[sl_C2F], partial=True)
        for sfc in range(4):
            mm(PS[0:16, sfc * 512:(sfc + 1) * 512],
               BIAh[0:1, 1152:1168],
               SUMR[0:1, sfc * 512:(sfc + 1) * 512],
               start=True, stop=True,
               reads=[sl_BIA, sl_SUMR], wbank=sl_PB[sfc])
            p.emit("vector", (lambda e, sfc=sfc: e.reciprocal(
                PS[0:16, sfc * 512:(sfc + 1) * 512],
                PS[0:16, sfc * 512:(sfc + 1) * 512])),
                reads=[sl_PB[sfc]], writes=[sl_PB[sfc]], partial=True)
            p.emit("vector", (lambda e, sfc=sfc: e.tensor_tensor(
                RWF[0:16, sfc * 512:(sfc + 1) * 512],
                RWF[0:16, sfc * 512:(sfc + 1) * 512],
                PS[0:16, sfc * 512:(sfc + 1) * 512], op=ALU.mult)),
                reads=[sl_RWF, sl_PB[sfc]], writes=[sl_RWF], partial=True)

        for i in range(NT):
            _merge(sl_QFF0.r, sl_QFFt[i].r)
            _merge(sl_QFF0.w, sl_QFFt[i].w)
            _merge(sl_KFF0.r, sl_KFFt[i].r)
            _merge(sl_KFF0.w, sl_KFFt[i].w)
        MIXv, sl_MIX = [], []
        for pr in range(8):
            a, sl = R2.view(12288 + pr * 4096, 0, 128, T, BF16, f"MIX{pr}")
            MIXv.append(a)
            sl_MIX.append(sl)
        for pr in range(8):
            for s in range(4):
                bk = 4 + (pr * 4 + s) % 2
                mm(bank(bk), SELP[:, pr * 128:(pr + 1) * 128],
                   RWF[0:16, s * 512:(s + 1) * 512],
                   start=True, stop=True,
                   reads=[sl_RWF, sl_CST], wbank=sl_PB[bk])
                p.emit("vector", (lambda e, pr=pr, s=s, bk=bk: e.tensor_tensor(
                    MIXv[pr][:, s * 512:(s + 1) * 512],
                    C2F[:, s * 512:(s + 1) * 512], bank(bk), op=ALU.mult)),
                    reads=[sl_C2F, sl_PB[bk]], writes=[sl_MIX[pr]],
                    partial=(s > 0))
        XBv = [None] * NT
        sl_XB = [None] * NT
        for i in range(3):
            a, s = R2.view(i * 4096, 0, 128, D, F32, f"XB{i}")
            XBv[i] = a
            sl_XB[i] = s
        XAfull = R1h[:].bitcast(F32)

        def emit_permute(i2, q):
            p0 = 16 * (i2 % 8)
            hi = i2 // 8
            srcp = XAfull[p0:p0 + 16, :].rearrange(
                "p (b two c) -> p b two c", two=2, c=1024)[:, :, hi, :]
            dma(q, XBv[i2], srcp,
                reads=[sl_XA[2 * b_ + hi] for b_ in range(8)],
                writes=[sl_XB[i2]])
            if n_stages >= 3:
                emit_ln_stats(XBv, sl_XB, i2)

        for i in [x for x in range(NT) if x % 2 == 0] + \
                 [x for x in range(NT) if x % 2 == 1]:
            for dh in range(2):
                bk = (i * 2 + dh) % 2
                for pr in range(8):
                    mm(bank(bk), MIXv[pr][:, i * 128:(i + 1) * 128],
                       WOS[:, pr * 1024 + dh * 512:pr * 1024 + (dh + 1) * 512],
                       start=(pr == 0), stop=(pr == 7),
                       reads=[sl_MIX[pr], sl_WOS], wbank=sl_PB[bk])
                p.emit("vector", (lambda e, i=i, dh=dh, bk=bk:
                                  e.scalar_tensor_tensor(
                    XAv[i][:, dh * 512:(dh + 1) * 512], bank(bk), 1.0,
                    XAv[i][:, dh * 512:(dh + 1) * 512],
                    op0=ALU.mult, op1=ALU.add)),
                    reads=[sl_PB[bk], sl_XA[i]], writes=[sl_XA[i]], partial=True)
            if i == NT - 2:
                for i2 in range(3):
                    emit_permute(i2, "sync" if i2 % 2 == 0 else "gpsimd")
            if i == NT - 1:
                for i2 in range(3, NT):
                    a_, s_ = R2.view(i2 * 4096, 0, 128, D, F32, f"XB{i2}")
                    XBv[i2] = a_
                    sl_XB[i2] = s_
                    emit_permute(i2, "sync" if i2 % 2 == 0 else "gpsimd")

    if n_stages >= 3:
        HT3, sl_HT3 = emit_ln(XBv, sl_XB, "l1a", stats_emitted=True)
        VT1, sl_VT1 = R1.view(0, 0, 128, NT * 16 * 65, BF16, "VT1")
        WKc, sl_WKc = [], []
        for j in range(NDC):
            a_, s_ = R1.view(33280 + j * 2048, 0, 128, 1024, BF16, f"WK_{j}")
            WKc.append(a_)
            sl_WKc.append(s_)
        WSv, sl_WS = [], []
        for s in range(3):
            a_, sl = R1.view(51712 + s * 2048, 0, 128, 1024, BF16, f"WS1_{s}")
            WSv.append(a_)
            sl_WS.append(sl)
        EB1, sl_EB1 = R1.view(57856, 0, 128, 2048, BF16, "EB1")
        sl_EB1g = []
        for hg in range(4):
            _, s_ = R1.view(57856 + hg * 1024, 0, 128, 512, BF16, f"EB1g{hg}")
            sl_EB1g.append(s_)
        KA, sl_KA = HTR.view(0, 0, 128, 16384, BF16, "KA")
        sl_KAc = []
        for dc in range(NDC):
            _, s_ = HTR.view(dc * 4096, 0, 128, 2048, BF16, f"KAc{dc}")
            sl_KAc.append(s_)
        p.emit("vector", (lambda e: e.memset(VT1[:].rearrange(
            "p (t h n) -> p t h n", h=16, n=65)[:, :, :, 64], 1.0)),
            reads=(), writes=[sl_VT1], partial=False)
        w1st = w1st_t.ap()

        for j in range(NDC):
            dma("sync" if j % 2 == 0 else "gpsimd", WKc[j][:],
                w1st[2 * NDC + j, :].rearrange("(p n) -> p n", p=128),
                reads=[sl_w1st], writes=[sl_WKc[j]])
        dma("gpsimd", BIAh[0:1, 0:1024], B1k, writes=[sl_BIA], partial=True)
        for tt in range(4):
            tcol = tt * 512
            for dc in range(NDC):
                for j in range(NDC):
                    mm(bank(dc), WKc[j][:, dc * 128:(dc + 1) * 128],
                       HF[:, j * T + tcol:j * T + tcol + 512],
                       start=(j == 0), stop=False,
                       reads=[sl_WKc[j]] + hf_r(j, 4 * tt, 4 * tt + 4),
                       wbank=sl_PB[dc])
                mm(bank(dc), BIAh[0:1, dc * 128:(dc + 1) * 128],
                   ONE_ROW512,
                   start=False, stop=True, reads=[sl_BIA], wbank=sl_PB[dc])
                p.emit("scalar", (lambda e, dc=dc, tcol=tcol: e.copy(
                    KA[:, dc * 2048 + tcol:dc * 2048 + tcol + 512], bank(dc))),
                    reads=[sl_PB[dc]], writes=[sl_KAc[dc]], partial=(tt > 0))
        WVc, sl_WVc = [], []
        for j in range(NDC):
            a_, s_ = R1.view(33280 + j * 2048, 0, 128, 1024, BF16, f"WV_{j}")
            WVc.append(a_)
            sl_WVc.append(s_)
            dma("sync" if j % 2 == 0 else "gpsimd", WVc[j][:],
                w1st[3 * NDC + j, :].rearrange("(p n) -> p n", p=128),
                reads=[sl_w1st], writes=[sl_WVc[j]])
        dma("gpsimd", BIAh[0:1, 0:1024], B1v, writes=[sl_BIA], partial=True)

        for tt in range(4):
            tcol = tt * 512
            for i4 in range(4):
                for half in range(2):
                    bk = i4 * 2 + half
                    for j in range(NDC):
                        mm(bank(bk), HF[:, j * T + tcol + i4 * 128:
                                        j * T + tcol + (i4 + 1) * 128],
                           WVc[j][:, half * 512:(half + 1) * 512],
                           start=(j == 0), stop=False,
                           reads=[sl_WVc[j], sl_HFg[j][4 * tt + i4]],
                           wbank=sl_PB[bk])
                    mm(bank(bk), ONE_ROW128,
                       BIAh[0:1, half * 512:(half + 1) * 512],
                       start=False, stop=True, reads=[sl_BIA], wbank=sl_PB[bk])
                    p.emit("scalar", (lambda e, tt=tt, i4=i4, half=half, bk=bk:
                                      e.copy(
                        VT1[:, (tt * 4 + i4) * 1040 + half * 520:
                            (tt * 4 + i4) * 1040 + half * 520 + 520].rearrange(
                            "p (h n) -> p h n", n=65)[:, :, 0:64],
                        bank(bk).rearrange("p (h n) -> p h n", n=64))),
                        reads=[sl_PB[bk]], writes=[sl_VT1], partial=True)

        QF1, sl_QF1 = R1.view(33280, 0, 128, 4096, BF16, "QF1")
        sl_QF1c = []
        for dc in range(NDC):
            _, s_ = R1.view(33280 + dc * 1024, 0, 128, 512, BF16, f"QF1c{dc}")
            sl_QF1c.append(s_)
        CT1, sl_CT1 = R1.view(41472, 0, 128, 4096, BF16, "CT1")
        CTTv2, sl_CTT2 = [], []
        for s in range(2):
            a_, s_ = R1.view(49664 + s * 12288, 0, 128, 1024, BF16, f"CTT{s}")
            CTTv2.append(a_)
            sl_CTT2.append(s_)
        ws_ctr = [0]

        def stream_w(src_f32, stash_row, g):
            ws = ws_ctr[0] % 3
            q = "sync" if ws_ctr[0] % 2 == 0 else "gpsimd"
            ws_ctr[0] += 1
            dma(q, WSv[ws],
                w1st[stash_row, :].rearrange("(p n) -> p n", p=128),
                reads=[sl_w1st], writes=[sl_WS[ws]])
            return ws

        def ebcol(h):
            hg = h // 4
            return hg * 512 + (h % 2) * 256 + ((h % 4) // 2) * 128

        for g in range(4):
            gcol = g * 512
            for j in range(NDC):
                ws = stream_w(W1qT[j * 128:(j + 1) * 128, :], j, g)
                for dc in range(NDC):
                    mm(bank(dc), WSv[ws][:, dc * 128:(dc + 1) * 128],
                       HF[:, j * T + gcol:j * T + gcol + 512],
                       start=(j == 0), stop=False,
                       reads=[sl_WS[ws]] + hf_r(j, 4 * g, 4 * g + 4),
                       wbank=sl_PB[dc])
            dma("gpsimd", BIAh[0:1, 0:1024], B1q, writes=[sl_BIA], partial=True)
            for dc in range(NDC):
                mm(bank(dc), BIAh[0:1, dc * 128:(dc + 1) * 128], ONE_ROW512,
                   start=False, stop=True, reads=[sl_BIA], wbank=sl_PB[dc])
                p.emit("scalar", (lambda e, dc=dc: e.mul(
                    QF1[:, dc * 512:(dc + 1) * 512], bank(dc), SCALE)),
                    reads=[sl_PB[dc]], writes=[sl_QF1c[dc]], partial=False)
            def s3_scores(i4):
                icol = i4 * 128
                for hg in range(4):
                    be = (hg % 2) * 2
                    for hh in range(4):
                        h = hg * 4 + hh
                        hp, hlo = h // 2, (h % 2) * 64
                        bk = be + (h % 2)
                        mm(bank(bk, ((h % 4) // 2) * 128, 128),
                           KA[hlo:hlo + 64,
                              hp * 2048 + gcol + icol:hp * 2048 + gcol + icol + 128],
                           QF1[hlo:hlo + 64, hp * 512 + icol:hp * 512 + icol + 128],
                           start=True, stop=True,
                           reads=[sl_KAc[hp], sl_QF1c[hp]], wbank=sl_PB[bk])
                    for bh in range(2):
                        p.emit("scalar", (lambda e, hg=hg, be=be, bh=bh:
                                          e.activation(
                            EB1[:, hg * 512 + bh * 256:hg * 512 + bh * 256 + 256],
                            PS[:, (be + bh) * 512:(be + bh) * 512 + 256],
                            AF.Exp, bias=EXPB, scale=f32(1.0))),
                            reads=[sl_PB[be + bh]], writes=[sl_EB1g[hg]],
                            partial=(bh > 0))
                    p.emit("vector", (lambda e, hg=hg: e.tensor_tensor(
                        EB1[:, hg * 512:(hg + 1) * 512],
                        EB1[:, hg * 512:(hg + 1) * 512],
                        BDMASK, op=ALU.mult)),
                        reads=[sl_EB1g[hg], sl_CST], writes=[sl_EB1g[hg]],
                        partial=True)

            def avb(h):
                if h < 7:
                    return 4, h * 65
                if h < 14:
                    return 5, (h - 7) * 65
                return 6, (h - 14) * 65

            def s3_av(i4):
                for h in range(NH):
                    lhs = EB1[:, ebcol(h):ebcol(h) + 128]
                    bkx, off = avb(h)
                    mm(bank(bkx, off, 65), lhs,
                       VT1[:, (g * 4 + i4) * 1040 + h * 65:
                           (g * 4 + i4) * 1040 + (h + 1) * 65],
                       start=True, stop=True,
                       reads=[sl_EB1g[h // 4], sl_VT1], wbank=sl_PB[bkx])

            def s3_evac(i4):
                ctt = CTTv2[i4 % 2]
                sct = sl_CTT2[i4 % 2]
                p.emit("vector", (lambda e: e.reciprocal(
                    RS[:, 0:7], bank(4, 0, 455).rearrange(
                        "p (i n) -> p i n", n=65)[:, :, 64])),
                    reads=[sl_PB[4]], writes=[sl_RS], partial=True)
                p.emit("vector", (lambda e: e.reciprocal(
                    RS[:, 7:14], bank(5, 0, 455).rearrange(
                        "p (i n) -> p i n", n=65)[:, :, 64])),
                    reads=[sl_PB[5]], writes=[sl_RS], partial=True)
                p.emit("vector", (lambda e: e.reciprocal(
                    RS[:, 14:16], bank(6, 0, 130).rearrange(
                        "p (i n) -> p i n", n=65)[:, :, 64])),
                    reads=[sl_PB[6]], writes=[sl_RS], partial=True)
                for h in range(NH):
                    bkx, off = avb(h)
                    p.emit("vector", (lambda e, h=h, ctt=ctt, bkx=bkx, off=off:
                                      e.tensor_scalar(
                        ctt[:, h * 64:(h + 1) * 64],
                        bank(bkx, off, 64),
                        RS[:, h:h + 1], None, op0=ALU.mult)),
                        reads=[sl_PB[bkx], sl_RS], writes=[sct],
                        partial=(h > 0))

            def s3_tr(i4):
                icol = i4 * 128
                ctt = CTTv2[i4 % 2]
                sct = sl_CTT2[i4 % 2]
                for hp in range(NDC):
                    k = hp % 4
                    pslot = bank_bf(7, k * 128, 128)
                    bsl = sl_PB[7]
                    tr(pslot, ctt[:, hp * 128:(hp + 1) * 128], IDENT,
                       reads=[sct], wbank=bsl)
                    if hp % 2 == 0:
                        scopy(CT1[:, hp * 512 + icol:hp * 512 + icol + 128],
                              pslot, reads=[bsl], writes=[sl_CT1])
                    else:
                        vcopy(CT1[:, hp * 512 + icol:hp * 512 + icol + 128],
                              pslot, reads=[bsl], writes=[sl_CT1])

            for i4 in range(4):
                s3_scores(i4)
                s3_av(i4)
                if i4 > 0:
                    s3_tr(i4 - 1)
                s3_evac(i4)
            s3_tr(3)
            for j in range(NDC):
                ws = stream_w(W1oT[j * 128:(j + 1) * 128, :], NDC + j, g)
                for i4 in range(4):
                    for half in range(2):
                        bk = i4 * 2 + half
                        mm(bank(bk), CT1[:, j * 512 + i4 * 128:
                                         j * 512 + (i4 + 1) * 128],
                           WSv[ws][:, half * 512:(half + 1) * 512],
                           start=(j == 0), stop=False,
                           reads=[sl_WS[ws], sl_CT1], wbank=sl_PB[bk])
            dma("gpsimd", BIAh[0:1, 0:1024], B1o, writes=[sl_BIA], partial=True)
            for i4 in range(4):
                ti = g * 4 + i4
                for half in range(2):
                    bk = i4 * 2 + half
                    mm(bank(bk), ONE_ROW128,
                       BIAh[0:1, half * 512:(half + 1) * 512],
                       start=False, stop=True, reads=[sl_BIA], wbank=sl_PB[bk])
                    p.emit("vector", (lambda e, ti=ti, half=half, bk=bk:
                                      e.scalar_tensor_tensor(
                        XBv[ti][:, half * 512:(half + 1) * 512], bank(bk),
                        1.0, XBv[ti][:, half * 512:(half + 1) * 512],
                        op0=ALU.mult, op1=ALU.add)),
                        reads=[sl_PB[bk], sl_XB[ti]], writes=[sl_XB[ti]],
                        partial=True)
                if n_stages >= 4:
                    emit_ln_stats(XBv, sl_XB, ti)

    if n_stages >= 4:
        HT4, sl_HT4 = emit_ln(XBv, sl_XB, "l1f", stats_emitted=True)
        WFM, sl_WFM = R1.view(0, 0, 8, T, BF16, "WFM")
        WBBv, sl_WBB = [], []
        for e_ in range(NE1):
            a, sl = R1.view(4096 + e_ * 4096, 0, 128, T, BF16, f"WBB{e_}")
            WBBv.append(a)
            sl_WBB.append(sl)
        XEv, sl_XE = [], []
        for s in range(2):
            a, sl = R1.view(36864 + s * 1024, 0, 128, 512, BF16, f"XE{s}")
            XEv.append(a)
            sl_XE.append(sl)
        OSTv, sl_OST = [], []
        for s in range(2):
            a, sl = R1.view(38912 + s * 2048, 0, 128, 512, F32, f"OST{s}")
            OSTv.append(a)
            sl_OST.append(sl)
        EXBs, sl_EXB = R1.view(55296, 0, 8, 1024, BF16, "EXB")
        WS4v, sl_WS4 = [], []
        for s in range(4):
            a, sl = R1.view(57344 + s * 2048, 0, 128, 1024, BF16, f"WS4_{s}")
            WS4v.append(a)
            sl_WS4.append(sl)
        dma("gpsimd", GWS[:].rearrange("p (j n) -> p j n", j=8),
            GW.rearrange("(j p) n -> p j n", j=8), writes=[sl_GWS])
        dma("gpsimd", BIAh[0:1, 1024:1032], GB, writes=[sl_BIA], partial=True)
        dma("gpsimd", EXBs[:], EB, writes=[sl_EXB])
        dma("sync", CSTh[0:8, 128:1152], SEL8D.ap(), writes=[sl_CST], partial=True)
        GBS = BIAh[0:1, 1024:1032]

        for tc in range(NT):
            gb4 = 6 + tc % 2
            for j in range(NDC):
                mm(bank(gb4, 0, 8), HF[:, j * T + tc * 128:j * T + (tc + 1) * 128],
                   GWS[:, j * 8:(j + 1) * 8],
                   start=(j == 0), stop=False,
                   reads=[sl_HFg[j][tc], sl_GWS], wbank=sl_PB[gb4])
            mm(bank(gb4, 0, 8), ONE_ROW128, GBS,
               start=False, stop=True, reads=[sl_BIA], wbank=sl_PB[gb4])
            G8 = GSCR[:, 0:8]
            NM = GSCR[:, 8:9]
            SM = GSCR[:, 9:10]
            RSM2 = GSCR[:, 10:11]
            MK8 = GSCR[:, 16:24]
            WSUM = GSCR[:, 24:25]
            p.emit("vector", (lambda e, gb4=gb4: e.tensor_scalar(
                GSCR[:, 32:40], bank(gb4, 0, 8), 1.0, -3e38, op0=ALU.mult,
                op1=ALU.max, accum_out=NM)),
                reads=[sl_PB[gb4]], writes=[sl_GSCR], partial=True)
            p.emit("vector", (lambda e: e.tensor_scalar(
                NM, NM, -1.0, None, op0=ALU.mult)),
                reads=[sl_GSCR], writes=[sl_GSCR], partial=True)
            p.emit("scalar", (lambda e, gb4=gb4: e.activation(
                G8, bank(gb4, 0, 8), AF.Exp, bias=NM, scale=f32(1.0),
                accum_out=SM)),
                reads=[sl_PB[gb4], sl_GSCR], writes=[sl_GSCR], partial=True)
            p.emit("vector", (lambda e: e.reciprocal(RSM2, SM)),
                   reads=[sl_GSCR], writes=[sl_GSCR], partial=True)
            p.emit("vector", (lambda e: e.tensor_scalar(
                G8, G8, RSM2, None, op0=ALU.mult)),
                reads=[sl_GSCR], writes=[sl_GSCR], partial=True)
            p.emit("vector", (lambda e: e.tensor_scalar(
                MK8, G8, 0.1, None, op0=ALU.is_ge)),
                reads=[sl_GSCR], writes=[sl_GSCR], partial=True)
            p.emit("vector", (lambda e: e.tensor_tensor(
                G8, G8, MK8, op=ALU.mult)),
                reads=[sl_GSCR], writes=[sl_GSCR], partial=True)
            p.emit("vector", (lambda e: e.tensor_scalar(
                GSCR[:, 32:40], G8, 1.0, 0.0, op0=ALU.mult,
                op1=ALU.add, accum_out=WSUM)),
                reads=[sl_GSCR], writes=[sl_GSCR], partial=True)
            p.emit("vector", (lambda e: e.reciprocal(RSM2, WSUM)),
                   reads=[sl_GSCR], writes=[sl_GSCR], partial=True)
            p.emit("vector", (lambda e, tc=tc: e.tensor_scalar(
                WTM[:, tc * 8:(tc + 1) * 8], G8, RSM2, None, op0=ALU.mult)),
                reads=[sl_GSCR], writes=[sl_WTM], partial=True)
        for tc in range(NT):
            k = tc % 2
            pslot = bank(6, k * 128, 128)
            bsl = sl_PB[6]
            tr(pslot[0:8, :], WTM[:, tc * 8:(tc + 1) * 8], CSTF[:, 0:128],
               reads=[sl_WTM], wbank=bsl, ident_slot=sl_CSTF)
            vcopy(WFM[0:8, tc * 128:(tc + 1) * 128], pslot[0:8, :],
                  reads=[bsl], writes=[sl_WFM])
        for e_ in range(NE1):
            for s in range(4):
                bk = 4 + (e_ * 4 + s) % 2
                mm(bank(bk), SEL8[:, e_ * 128:(e_ + 1) * 128],
                   WFM[0:8, s * 512:(s + 1) * 512],
                   start=True, stop=True,
                   reads=[sl_WFM, sl_CST], wbank=sl_PB[bk])
                p.emit("vector", (lambda e, e_=e_, s=s, bk=bk: e.tensor_copy(
                    WBBv[e_][:, s * 512:(s + 1) * 512], bank(bk))),
                    reads=[sl_PB[bk]], writes=[sl_WBB[e_]], partial=(s > 0))

        XOUTF = X_out
        for tt in range(4):
            tcol = tt * 512
            for dc in range(NDC):
                mm(bank(dc), EXBs[0:8, dc * 128:(dc + 1) * 128],
                   WFM[0:8, tcol:tcol + 512],
                   start=True, stop=False,
                   reads=[sl_EXB, sl_WFM], wbank=sl_PB[dc])
                for j2 in range(4):
                    ti = tt * 4 + j2
                    mm(bank(dc, j2 * 128, 128),
                       XBv[ti][:, dc * 128:(dc + 1) * 128], CSTF[:, 0:128],
                       start=False, stop=False,
                       reads=[sl_XB[ti], sl_CSTF], wbank=sl_PB[dc])
            for e_ in range(NE1):
                for j in range(NDC):
                    cix = e_ * NDC + j
                    ws = cix % 4
                    dma("sync" if cix % 2 == 0 else "gpsimd", WS4v[ws],
                        ewst_t.ap()[cix, :].rearrange("(p n) -> p n", p=128),
                        reads=[sl_ewst], writes=[sl_WS4[ws]])
                    xe = XEv[(e_ * NDC + j) % 2]
                    p.emit("vector", (lambda e, j=j, e_=e_, tcol=tcol, xe=xe:
                                      e.tensor_tensor(
                        xe, HF[:, j * T + tcol:j * T + tcol + 512],
                        WBBv[e_][:, tcol:tcol + 512], op=ALU.mult)),
                        reads=hf_r(j, 4 * tt, 4 * tt + 4) + [sl_WBB[e_]],
                        writes=[sl_XE[(e_ * NDC + j) % 2]])
                    last = (e_ == NE1 - 1 and j == NDC - 1)
                    for dc in range(NDC):
                        mm(bank(dc), WS4v[ws][:, dc * 128:(dc + 1) * 128], xe,
                           start=False, stop=last,
                           reads=[sl_WS4[ws], sl_XE[(e_ * NDC + j) % 2]],
                           wbank=sl_PB[dc])
            for dc in range(NDC):
                sslot = (tt * NDC + dc) % 2
                if dc % 2 == 0:
                    vcopy(OSTv[sslot][:], bank(dc),
                          reads=[sl_PB[dc]], writes=[sl_OST[sslot]],
                          partial=False)
                else:
                    scopy(OSTv[sslot][:], bank(dc),
                          reads=[sl_PB[dc]], writes=[sl_OST[sslot]],
                          partial=False)
                dma("sync",
                    XOUTF[dc * 128:(dc + 1) * 128, tt * 512:(tt + 1) * 512],
                    OSTv[sslot][:],
                    reads=[sl_OST[sslot]], writes=())

    if n_stages <= 2:
        for i in range(NT):
            b_, half = i // 2, i % 2
            dma("sync", X_out[b_, half * 128:(half + 1) * 128, :], XAv[i],
                reads=[sl_XA[i]], writes=())
    elif n_stages == 3:
        XOT = X_out.rearrange("b t c -> t b c")
        for i2 in range(NT):
            dma("sync", XOT[i2 * 16:(i2 + 1) * 16, :, :], XBv[i2],
                reads=[sl_XB[i2]], writes=())
    p.barrier_wait("sync", {"sync_q": p.cnt["sync_q"]})

    with nc.Block() as block:
        @block.tensor
        def _(t):
            for f in p.ops["tensor"]:
                f(t)

        @block.vector
        def _(v):
            for f in p.ops["vector"]:
                f(v)

        @block.scalar
        def _(s):
            for f in p.ops["scalar"]:
                f(s)

        @block.gpsimd
        def _(g):
            for f in p.ops["gpsimd"]:
                f(g)

        @block.sync
        def _(sy):
            for f in p.ops["sync"]:
                f(sy)

    return nc


def _stage_inputs(inputs, core):
    A = np.ascontiguousarray

    def idxs():
        return [h * 192 + k for h in range(NH) for k in range(HD)]

    x = inputs["L0_attn_x"]
    qi = [h * 192 + d_ for h in range(NH) for d_ in range(HD)]
    ki = [h * 192 + 64 + d_ for h in range(NH) for d_ in range(HD)]
    vi = [h * 192 + 128 + d_ for h in range(NH) for d_ in range(HD)]
    ipw = inputs["L1_attn_in_proj_weight"]
    ipb = inputs["L1_attn_in_proj_bias"]
    kvT = A(inputs["L0_attn_kv_w"].T)
    d = {
        "x": A(x[:, core * TLOC:(core + 1) * TLOC, :]),
        "w0qT": A(inputs["L0_attn_q_w"].T),
        "w0kkT": A(np.concatenate([kvT[:, 0:64], kvT[:, 0:64]], axis=1)),
        "w0vT": A(kvT[:, 64:128]),
        "w0pT": A(inputs["L0_attn_proj_w"].T),
        "b0p": A(inputs["L0_attn_proj_b"][None, :]),
        "sim": A(inputs["L0_ffn_sim_matrix"]),
        "gates": A(inputs["L0_ffn_gates"][None, :]),
        "wqkv0": A(np.concatenate(
            [np.concatenate([inputs["L0_ffn_q_proj"][e],
                             inputs["L0_ffn_k_proj"][e],
                             inputs["L0_ffn_v_proj"][e]], axis=1)
             for e in range(NE0)], axis=1)),
        "wo0": A(inputs["L0_ffn_o_proj"].reshape(NE0 * HD, D)),
        "w1qT": A(ipw[qi, :].T),
        "w1kT": A(ipw[ki, :].T),
        "w1vT": A(ipw[vi, :].T),
        "b1q": A(ipb[qi][None, :]),
        "b1k": A(ipb[ki][None, :]),
        "b1v": A(ipb[vi][None, :]),
        "w1oT": A(inputs["L1_attn_out_proj_weight"].T),
        "b1o": A(inputs["L1_attn_out_proj_bias"][None, :]),
        "gw": A(inputs["L1_ffn_gate_w"]),
        "gb": A(inputs["L1_ffn_gate_b"][None, :]),
        "ew": A(inputs["L1_ffn_expert_w"]),
        "eb": A(inputs["L1_ffn_expert_b"]),
    }
    return {k: np.asarray(v, dtype=np.float32) for k, v in d.items()}


_NC_CACHE = {}


def kernel(**inputs):
    key = N_STAGES
    if key not in _NC_CACHE:
        _NC_CACHE[key] = build_program(N_STAGES)
    nc = _NC_CACHE[key]
    in_maps = [_stage_inputs(inputs, c) for c in range(NCORES)]
    res = run_bass_kernel_spmd(nc, in_maps, core_ids=list(range(NCORES)))
    if N_STAGES >= 4:
        shards = [np.asarray(res.results[c]["out"])
                  .reshape(D, TLOC, B).transpose(2, 1, 0)
                  for c in range(NCORES)]
    else:
        shards = [np.asarray(res.results[c]["out"]) for c in range(NCORES)]
    out = np.concatenate(shards, axis=1)
    return out.astype(np.float32)


if __name__ == "__main__":
    nc = build_program(1)
    print("build ok")
